# version 04
# baseline (speedup 1.0000x reference)
import contextlib
import math
import numpy as np
import ml_dtypes
import concourse.bass as bass
import concourse.mybir as mybir
from concourse.bass_utils import run_bass_kernel_spmd

F32 = mybir.dt.float32
BF16 = mybir.dt.bfloat16
I32 = mybir.dt.int32
AF = mybir.ActivationFunctionType
ALU = mybir.AluOpType
AX = mybir.AxisListType

S, D = 8192, 2048
NC, JT = 8, 8
TPC = JT * 128
L = [8 * j + 8 for j in range(JT)]
BOFF = [sum(L[:j]) for j in range(JT)]
NSLOT = sum(L)

TWO_PI = 2 * math.pi
C1 = float(np.float32(int(TWO_PI * 2 ** 10) / 2 ** 10))
C2 = float(np.float32(TWO_PI - C1))
C3 = float(np.float32(TWO_PI - C1 - float(np.float32(TWO_PI - C1))))
INV2PI = float(np.float32(1.0 / TWO_PI))
BOUND = 3.141590
PI2_HI = float(np.float32(math.pi / 2))
PI2_LO = float(np.float32(math.pi / 2 - float(np.float32(math.pi / 2))))
SC64 = 1.0 / math.sqrt(128.0)
SC_HI = float(np.float32(SC64))
SC_LO = float(np.float32(SC64 / SC_HI))
BKILL = -30000.0
INV127 = float(np.float32(1.0 / 127.0))

_ID_BF = np.eye(128, dtype=ml_dtypes.bfloat16)
_ID_F32 = np.eye(128, dtype=np.float32)
_CAUSAL = np.tile((np.arange(128)[:, None] <= np.arange(128)[None, :])
                  .astype(ml_dtypes.bfloat16), (1, 8))


import os
_SEMALIAS = {"K": "Q"}
for _kv in os.environ.get("SEMALIAS", "").split(","):
    if ":" in _kv:
        _a, _b = _kv.split(":")
        _SEMALIAS[_a] = _b


class _W:
    ENG = "qgvat"

    def __init__(self):
        self.cmds = {0: {e: [] for e in self.ENG}, 1: {e: [] for e in self.ENG}}
        self.blk = 0
        self.n = {k: 0 for k in ("Q", "G", "PE", "V", "A", "P", "B",
                                 "K", "C", "S", "O", "K0", "K1", "K2", "K3",
                                 "C0", "C1", "S0", "S1", "T0", "T1")}
        self.m = dict(_SEMALIAS)
        self.last_wait = {}
        self.sems = None

    def emit(self, eng, fn):
        self.cmds[self.blk][eng].append(fn)

    def wait(self, eng, sem, val):
        sem = self.m.get(sem, sem)
        if val is None or val <= 0:
            return
        key = (eng, sem)
        if self.last_wait.get(key, -1) >= val:
            return
        self.last_wait[key] = val
        self.emit(eng, lambda e, s=sem, v=val: e.wait_ge(self.sems[s], v))

    def op(self, eng, fn, sem=None, k=1):
        if sem is None:
            self.emit(eng, fn)
            return None
        sem = self.m.get(sem, sem)
        self.n[sem] += k
        v = self.n[sem]
        self.emit(eng, lambda e, s=sem, kk=k: fn(e).then_inc(self.sems[s], kk))
        return v

    def dump(self, src_ap, dst_ap):
        self.wait("g", "V", self.n["V"])
        self.wait("g", "A", self.n["A"])
        self.wait("g", "PE", self.n["PE"])
        self.wait("g", "Q", self.n["Q"])
        ng = self.op("g", lambda e: e.dma_start(out=dst_ap, in_=src_ap), "G", 16)
        for e in "qvat":
            self.wait(e, "G", ng)
        return ng

    def fence(self):
        self.n["B"] += 5
        tgt = self.n["B"]
        for e in self.ENG:
            self.emit(e, lambda en: en.nop().then_inc(self.sems["B"], 1))
        for e in self.ENG:
            self.emit(e, lambda en, t=tgt: en.wait_ge(self.sems["B"], t))
        return tgt


def build(debug=False):
    w = _W()
    nc = bass.Bass("TRN2", target_bir_lowering=False, debug=False)
    dbg_kind = "ExternalOutput" if debug else "Internal"

    xq = nc.dram_tensor("xq", [TPC, D], F32, kind="ExternalInput").ap()
    xs = nc.dram_tensor("xs", [TPC, D], F32, kind="ExternalInput").ap()
    wqt = nc.dram_tensor("wqt", [D, D], F32, kind="ExternalInput").ap()
    wkt = nc.dram_tensor("wkt", [D, 512], F32, kind="ExternalInput").ap()
    wvt = nc.dram_tensor("wvt", [D, 512], F32, kind="ExternalInput").ap()
    wot = nc.dram_tensor("wot", [D, D], F32, kind="ExternalInput").ap()
    wft = nc.dram_tensor("wft", [D, D], F32, kind="ExternalInput").ap()
    gam = nc.dram_tensor("gam", [D], F32, kind="ExternalInput").ap()
    tidx = nc.dram_tensor("tidx", [1, 2048], F32, kind="ExternalInput").ap()
    btab = nc.dram_tensor("btab", [1, NSLOT], F32, kind="ExternalInput").ap()
    out = nc.dram_tensor("out", [TPC, D], F32, kind="ExternalOutput").ap()

    kvk_in = nc.dram_tensor("kvk_in", [4, 128, 1024], BF16).ap()
    kvv_in = nc.dram_tensor("kvv_in", [8, 128, 512], BF16).ap()
    kvk_ag = nc.dram_tensor("kvk_ag", [NC, 4, 128, 1024], BF16).ap()
    kvv_ag = nc.dram_tensor("kvv_ag", [NC, 8, 128, 512], BF16).ap()
    mx_in = nc.dram_tensor("mx_in", [1, 16], F32).ap()
    mx_out = nc.dram_tensor("mx_out", [1, 16], F32).ap()
    sc_d = nc.dram_tensor("sc_d", [1, 16], F32).ap()
    h1_d = nc.dram_tensor("h1_d", [TPC, D], F32, kind=dbg_kind).ap()
    xn_d = nc.dram_tensor("xn_d", [TPC, D], F32, kind=dbg_kind).ap()
    ao_d = nc.dram_tensor("ao_d", [TPC, D], F32, kind=dbg_kind).ap()
    fo_d = nc.dram_tensor("fo_d", [TPC, D], F32, kind=dbg_kind).ap()
    wfq_d = nc.dram_tensor("wfq_d", [16, 128, D], BF16).ap()

    if debug:
        dbg_q = nc.dram_tensor("dbg_q", [128, 16 * 1024], BF16, kind="ExternalOutput").ap()
        dbg_dk = nc.dram_tensor("dbg_dk", [128, 4096], BF16, kind="ExternalOutput").ap()
        dbg_dv = nc.dram_tensor("dbg_dv", [128, 4096], BF16, kind="ExternalOutput").ap()
        dbg_attn = nc.dram_tensor("dbg_attn", [128, 16 * 1024], BF16, kind="ExternalOutput").ap()
        dbg_tabc = nc.dram_tensor("dbg_tabc", [128, 2048], BF16, kind="ExternalOutput").ap()
        dbg_tabs = nc.dram_tensor("dbg_tabs", [128, 2048], BF16, kind="ExternalOutput").ap()
        dbg_kvk = nc.dram_tensor("dbg_kvk", [4, 128, 1024], BF16, kind="ExternalOutput").ap()
        dbg_kvv = nc.dram_tensor("dbg_kvv", [8, 128, 512], BF16, kind="ExternalOutput").ap()
        dbg_ln1t = nc.dram_tensor("dbg_ln1t", [128, 8192], BF16, kind="ExternalOutput").ap()
        dbg_scal = nc.dram_tensor("dbg_scal", [128, 16], F32, kind="ExternalOutput").ap()
        dbg_one = nc.dram_tensor("dbg_one", [1, 16], F32, kind="ExternalOutput").ap()
        dbg_wfq = nc.dram_tensor("dbg_wfq", [16, 128, D], BF16, kind="ExternalOutput").ap()
        dbg_xqt = nc.dram_tensor("dbg_xqt", [128, JT * 2048], BF16, kind="ExternalOutput").ap()
        dbg_fi = nc.dram_tensor("dbg_fi", [128, 1], F32, kind="ExternalOutput").ap()
        dbg_ang = nc.dram_tensor("dbg_ang", [128, 1024], F32, kind="ExternalOutput").ap()
        dbg_sin = nc.dram_tensor("dbg_sin", [128, 1024], F32, kind="ExternalOutput").ap()
        dbg_lnr = nc.dram_tensor("dbg_lnr", [128, D], BF16, kind="ExternalOutput").ap()
        dbg_st8 = nc.dram_tensor("dbg_st8", [128, 16], F32, kind="ExternalOutput").ap()
        dbg_wrow = nc.dram_tensor("dbg_wrow", [1, 128], F32, kind="ExternalOutput").ap()
        dbg_tokb = nc.dram_tensor("dbg_tokb", [128, 1024], F32, kind="ExternalOutput").ap()
        dbg_red = nc.dram_tensor("dbg_red", [128, 1024], F32, kind="ExternalOutput").ap()
        dbg_ang2 = nc.dram_tensor("dbg_ang2", [128, 1024], F32, kind="ExternalOutput").ap()
        dbg_fe = nc.dram_tensor("dbg_fe", [128, 1], F32, kind="ExternalOutput").ap()
    id_bf_d = nc.inline_tensor(_ID_BF, name="id_bf").ap()
    id_f_d = nc.inline_tensor(_ID_F32, name="id_f").ap()
    caus_d = nc.inline_tensor(np.asarray(_CAUSAL), name="caus").ap()

    with contextlib.ExitStack() as top:
        sems = {}
        for s in ("Q", "G", "PE", "V", "A", "P", "B", "K", "C", "S", "O",
                  "K0", "K1", "K2", "K3", "C0", "C1", "S0", "S1", "T0", "T1"):
            sems[s] = top.enter_context(nc.semaphore("s" + s))
        w.sems = sems

        q_all = top.enter_context(nc.sbuf_tensor("q_all", [128, 16 * 1024], BF16))
        diag_k = top.enter_context(nc.sbuf_tensor("diag_k", [128, 4 * 1024], BF16))
        diag_v = top.enter_context(nc.sbuf_tensor("diag_v", [128, JT * 512], BF16))
        gam_b = top.enter_context(nc.sbuf_tensor("gam_b", [128, D], F32))
        btab_b = top.enter_context(nc.sbuf_tensor("btab_b", [128, NSLOT], F32))
        scal = top.enter_context(nc.sbuf_tensor("scal", [128, 16], F32))
        id_bs = top.enter_context(nc.sbuf_tensor("id_bs", [128, 128], BF16))
        id_fs = top.enter_context(nc.sbuf_tensor("id_fs", [128, 128], F32))
        caus_s = top.enter_context(nc.sbuf_tensor("caus_s", [128, 1024], BF16))
        ones_bf = top.enter_context(nc.sbuf_tensor("ones_bf", [128, 128], BF16))
        pool = [top.enter_context(nc.sbuf_tensor(f"pool{i}", [128, D], F32))
                for i in range(2)]
        sq_s = top.enter_context(nc.sbuf_tensor("sq_s", [128, D], F32))
        st8 = top.enter_context(nc.sbuf_tensor("st8", [128, 16], F32))
        one1 = top.enter_context(nc.sbuf_tensor("one1", [1, 16], F32))
        wsum = top.enter_context(nc.sbuf_tensor("wsum", [128, 1], F32))
        wrow = top.enter_context(nc.sbuf_tensor("wrow", [1, 128], F32))
        epsb = top.enter_context(nc.sbuf_tensor("epsb", [128, 4], F32))

        (I_SUM, I_NM, I_VAR, I_SD, I_RI, I_SQ, I_M2, I_RMX, I_T1, I_RI2,
         I_T2, I_T3) = range(12)

        def stx(i):
            return st8[:, i:i + 1]

        def vdrain():
            w.emit("v", lambda e: e.drain())

        def adrain():
            w.emit("a", lambda e: e.drain())

        def v_ts(dst, src, s1, s2=None, op0=ALU.mult, op1=None, sem=None):
            if op1 is None:
                return w.op("v", lambda e: e.tensor_scalar(
                    out=dst, in0=src, scalar1=s1, scalar2=None, op0=op0), sem)
            return w.op("v", lambda e: e.tensor_scalar(
                out=dst, in0=src, scalar1=s1, scalar2=s2, op0=op0, op1=op1), sem)

        def v_tt(dst, a, bb, op, sem=None):
            return w.op("v", lambda e: e.tensor_tensor(out=dst, in0=a, in1=bb, op=op), sem)

        def v_stt(dst, a, s, bb, op0, op1, sem=None):
            return w.op("v", lambda e: e.scalar_tensor_tensor(
                out=dst, in0=a, scalar=s, in1=bb, op0=op0, op1=op1), sem)

        def ln_stats(px, eps, nq_ready, rsem="Q"):
            if rsem != "V":
                w.wait("v", rsem, nq_ready)
            nv1 = w.op("v", lambda e: e.tensor_reduce(
                out=stx(I_SUM), in_=px, axis=AX.X, op=ALU.add), "V")
            w.wait("a", rsem, nq_ready)
            w.wait("a", "V", nv1)
            na1 = w.op("a", lambda e: e.activation(
                out=sq_s[:], in_=px, func=AF.Square, accum_out=stx(I_SQ)), "A")
            w.wait("v", "A", na1)
            vdrain()
            v_ts(stx(I_NM), stx(I_SUM), -1.0 / 2048.0)
            vdrain()
            v_ts(stx(I_M2), stx(I_NM), stx(I_NM))
            v_ts(stx(I_VAR), stx(I_SQ), 1.0 / 2048.0)
            vdrain()
            nv2 = v_tt(stx(I_VAR), stx(I_VAR), stx(I_M2), ALU.subtract, "V")
            w.wait("a", "V", nv2)
            na2 = w.op("a", lambda e, ee=eps: e.activation(
                out=stx(I_SD), in_=stx(I_VAR), func=AF.Sqrt, bias=ee), "A")
            w.wait("v", "A", na2)
            nvr = w.op("v", lambda e: e.reciprocal(out=stx(I_RI), in_=stx(I_SD)), "V")
            vdrain()
            return nvr

        with contextlib.ExitStack() as sa:
            tab_c = sa.enter_context(nc.sbuf_tensor("tab_c", [128, 2048], BF16))
            tab_s = sa.enter_context(nc.sbuf_tensor("tab_s", [128, 2048], BF16))
            tab_cq = sa.enter_context(nc.sbuf_tensor("tab_cq", [128, 2048], BF16))
            tab_sq = sa.enter_context(nc.sbuf_tensor("tab_sq", [128, 2048], BF16))
            tokb_t = sa.enter_context(nc.sbuf_tensor("tokb_t", [128, 1024], F32))
            ang_t = sa.enter_context(nc.sbuf_tensor("ang_t", [128, 1024], F32))
            tr1_t = sa.enter_context(nc.sbuf_tensor("tr1_t", [128, 1024], F32))
            tr2_t = sa.enter_context(nc.sbuf_tensor("tr2_t", [128, 1024], F32))
            tokb = tokb_t[:, :]
            ang = ang_t[:, :]
            tr1 = tr1_t[:, :]
            tr2 = tr2_t[:, :]
            tr3 = tokb_t[:, :]
            sneg = ang_t[:, :]
            trk = sa.enter_context(nc.sbuf_tensor("trk", [128, 1024], I32))
            sin_f = sa.enter_context(nc.sbuf_tensor("sin_f", [128, 1024], F32))
            cos_f = sa.enter_context(nc.sbuf_tensor("cos_f", [128, 1024], F32))
            fi = sa.enter_context(nc.sbuf_tensor("fi", [128, 1], F32))
            fe = sa.enter_context(nc.sbuf_tensor("fe", [128, 1], F32))
            lnv = sa.enter_context(nc.sbuf_tensor("lnv", [128, 1], F32))
            tenk = sa.enter_context(nc.sbuf_tensor("tenk", [128, 1], F32))
            ln1t = sa.enter_context(nc.sbuf_tensor("ln1t", [128, 16 * 512], BF16))
            lnraw = sa.enter_context(nc.sbuf_tensor("lnraw", [128, 2 * D], BF16))
            wkt_s = sa.enter_context(nc.sbuf_tensor("wkt_s", [128, 16 * 512], BF16))
            wvt_s = sa.enter_context(nc.sbuf_tensor("wvt_s", [128, 16 * 512], BF16))
            wq_col = sa.enter_context(nc.sbuf_tensor("wq_col", [128, 2 * 2048], BF16))
            praw = sa.enter_context(nc.sbuf_tensor("praw", [128, 2 * 512], BF16))
            pswp = sa.enter_context(nc.sbuf_tensor("pswp", [128, 2 * 512], BF16))
            pr1 = sa.enter_context(nc.sbuf_tensor("pr1", [128, 512], BF16))
            pr2 = sa.enter_context(nc.sbuf_tensor("pr2", [128, 512], BF16))
            kvstg = sa.enter_context(nc.sbuf_tensor("kvstg", [128, 2 * 512], BF16))
            ps_tr = sa.enter_context(nc.psum_tensor("ps_tr", [128, 2048], BF16))
            ps_p = sa.enter_context(nc.psum_tensor("ps_p", [128, 2 * 512], F32))

            w.blk = 0
            w.op("q", lambda e: e.dma_start(out=id_fs[:], in_=id_f_d[:, :]), "Q", 16)
            w.op("q", lambda e: e.dma_start(
                out=tokb[0:64, :], in_=tidx[0:1, 0:1024].partition_broadcast(64)), "Q", 16)
            nQ_tok = w.op("q", lambda e: e.dma_start(
                out=tokb[64:128, :],
                in_=tidx[0:1, 1024:2048].partition_broadcast(64)), "Q", 16)
            w.op("q", lambda e: e.dma_start(
                out=gam_b[:], in_=gam[None, :].partition_broadcast(128)[:, 0, :]), "Q", 16)
            w.op("q", lambda e: e.dma_start(
                out=btab_b[:], in_=btab[0:1, :].partition_broadcast(128)), "Q", 16)
            nV_z = w.op("v", lambda e: e.memset(one1[0:1, 8:16], 0.0), "V")
            w.wait("q", "V", nV_z)
            w.op("q", lambda e: e.dma_start(out=mx_in[0:1, 0:8],
                                            in_=one1[0:1, 8:16]), "Q", 16)
            w.op("q", lambda e: e.dma_start(out=mx_in[0:1, 8:16],
                                            in_=one1[0:1, 8:16]), "Q", 16)
            nG_id = w.op("g", lambda e: e.dma_start(out=id_bs[:], in_=id_bf_d[:, :]), "G", 16)
            nG_caus = w.op("g", lambda e: e.dma_start(out=caus_s[:], in_=caus_d[:, :]), "G", 16)
            w.op("g", lambda e: e.dma_start(
                out=wkt_s[:], in_=wkt.rearrange("(ic p) o -> p ic o", p=128)), "G", 16)
            nG_w = w.op("g", lambda e: e.dma_start(
                out=wvt_s[:], in_=wvt.rearrange("(ic p) o -> p ic o", p=128)), "G", 16)
            w.op("v", lambda e: e.memset(ones_bf[:], 1.0), "V")
            w.op("v", lambda e: e.memset(epsb[:, 0:1], 1e-5))
            w.op("v", lambda e: e.memset(epsb[:, 1:2], 1e-6))
            nV_eps = w.op("v", lambda e: e.memset(epsb[:, 2:3], SC_HI), "V")
            w.wait("a", "V", nV_eps)

            w.op("g", lambda e: e.iota(
                out=fi[0:64, :], pattern=[[0, 1]], base=0, channel_multiplier=1,
                allow_small_or_imprecise_dtypes=True), "P")
            nP_iota = w.op("g", lambda e: e.iota(
                out=fi[64:128, :], pattern=[[0, 1]], base=0, channel_multiplier=1,
                allow_small_or_imprecise_dtypes=True), "P")
            nV_tenk = w.op("v", lambda e: e.memset(tenk[:], 10000.0), "V")
            w.wait("a", "V", nV_tenk)
            w.op("a", lambda e: e.activation(out=lnv[:], in_=tenk[:], func=AF.Ln), "A")
            adrain()
            w.wait("v", "P", nP_iota)
            nV_fe = v_ts(fe[:], fi[:], -1.0 / 64.0, sem="V")
            w.wait("a", "V", nV_fe)
            nA_fr = w.op("a", lambda e: e.activation(
                out=fi[:], in_=fe[:], func=AF.Exp, scale=lnv[:]), "A")
            w.wait("v", "Q", w.n["Q"])
            w.wait("v", "A", nA_fr)
            w.op("v", lambda e: e.tensor_copy(out=fe[:], in_=fi[:]))
            vdrain()
            nV_ang = v_ts(ang[:], tokb[:], fe[:], sem="V")
            if debug:
                w.dump(fe[:], dbg_fe[:, :])
                w.dump(fi[:], dbg_fi[:, :])
                w.dump(tokb, dbg_tokb[:, :])
                w.dump(ang, dbg_ang[:, :])
                w.dump(ang, dbg_ang2[:, :])

            def reduce_sin(src, dst):
                v_ts(tr1[:], src, INV2PI)
                vdrain()
                w.op("v", lambda e: e.tensor_copy(out=trk[:], in_=tr1[:]))
                vdrain()
                w.op("v", lambda e: e.tensor_copy(out=tr2[:], in_=trk[:]))
                vdrain()
                v_stt(tr1[:], tr2[:], -C1, src, ALU.mult, ALU.add)
                vdrain()
                v_stt(tr1[:], tr2[:], -C2, tr1[:], ALU.mult, ALU.add)
                vdrain()
                v_stt(tr1[:], tr2[:], -C3, tr1[:], ALU.mult, ALU.add)
                vdrain()
                v_ts(tr2[:], tr1[:], BOUND, op0=ALU.is_gt)
                v_ts(tr3[:], tr1[:], -BOUND, op0=ALU.is_lt)
                vdrain()
                v_tt(tr2[:], tr2[:], tr3[:], ALU.subtract)
                vdrain()
                v_stt(tr1[:], tr2[:], -C1, tr1[:], ALU.mult, ALU.add)
                vdrain()
                v_stt(tr1[:], tr2[:], -C2, tr1[:], ALU.mult, ALU.add)
                vdrain()
                nv = v_stt(tr1[:], tr2[:], -C3, tr1[:], ALU.mult, ALU.add, "V")
                vdrain()
                if debug:
                    w.dump(tr1, dbg_red[:, :])
                w.wait("a", "V", nv)
                return w.op("a", lambda e, d=dst: e.activation(
                    out=d[:], in_=tr1[:], func=AF.Sin), "A")

            vdrain()
            nA_sin = reduce_sin(ang[:], sin_f)
            if debug:
                w.wait("g", "A", nA_sin)
                w.dump(sin_f[:], dbg_sin[:, :])
            w.wait("v", "A", nA_sin)
            v_ts(ang[:], ang[:], PI2_HI, PI2_LO, ALU.add, ALU.add)
            vdrain()
            nA_cos = reduce_sin(ang[:], cos_f)
            w.wait("v", "A", nA_cos)
            v_ts(sneg[:], sin_f[:], -1.0)
            v_ts(tr1[:], cos_f[:], SC_LO)
            v_ts(tr2[:], sin_f[:], SC_LO)
            vdrain()
            nV_tab = v_ts(tr3[:], sneg[:], SC_LO, sem="V")
            vdrain()
            w.wait("g", "V", nV_tab)
            nG_tab = None
            for dst, half, srcp in ((tab_c, 0, cos_f), (tab_c, 1, cos_f),
                                    (tab_s, 0, sneg), (tab_s, 1, sin_f),
                                    (tab_cq, 0, tr1), (tab_cq, 1, tr1),
                                    (tab_sq, 0, tr3), (tab_sq, 1, tr2)):
                for th in range(2):
                    nG_tab = w.op("g", lambda e, d=dst, h=half, s=srcp, t=th:
                                  e.dma_start(
                        out=d[64 * h:64 * h + 64, t * 1024:t * 1024 + 1024],
                        in_=s[64 * t:64 * t + 64, :]), "G", 16)

            ln1t_v = ln1t.rearrange("p (ic f) -> p ic f", ic=16)
            busy = {"pv": {0: None, 1: None}, "pa": {0: None, 1: None},
                    "tr": None}

            dbg_first = {"done": False}

            def ln_transpose(src_dram, row0, gslot, npe_free):
                par = gslot % 2
                px = pool[par][:]
                lw = lnraw[:, par * D:par * D + D]
                w.wait("q", "V", busy["pv"][par])
                w.wait("q", "A", busy["pa"][par])
                nq = w.op("q", lambda e, r=row0, sd=src_dram: e.dma_start(
                    out=px, in_=sd[r:r + 128, :]), "Q", 16)
                ln_stats(px, epsb[:, 0:1], nq)
                busy["pa"][par] = w.n["A"]
                nv = w.op("v", lambda e, p=px, ll=lw: e.tensor_scalar(
                    out=ll, in0=p, scalar1=stx(I_NM), scalar2=stx(I_RI),
                    op0=ALU.add, op1=ALU.mult), "V")
                busy["pv"][par] = nv
                w.wait("t", "V", nv)
                w.wait("t", "G", nG_id)
                w.wait("t", "V", busy["tr"])
                npe = None
                for ic in range(16):
                    npe = w.op("t", lambda e, i=ic, ll=lw: e.transpose(
                        out=ps_tr[:, i * 128:i * 128 + 128],
                        in_=ll[:, i * 128:i * 128 + 128], identity=id_bs[:]),
                        "PE" if ic == 15 else None)
                w.wait("v", "PE", npe)
                if npe_free is not None:
                    w.wait("v", "PE", npe_free)
                nev = w.op("v", lambda e, g=gslot: e.tensor_copy(
                    out=ln1t_v[:, :, g * 128:g * 128 + 128], in_=ps_tr[:]), "V")
                busy["tr"] = nev
                if debug and not dbg_first["done"]:
                    dbg_first["done"] = True
                    w.dump(lw, dbg_lnr[:, :])
                    w.dump(st8[:], dbg_st8[:, :])
                return nev

            rope_busy = {0: None, 1: None}

            def rope(par, tabc, tabs, tcol, dst_ap_fn, na_raw):
                raw = praw[:, par * 512: par * 512 + 512]
                swp = pswp[:, par * 512: par * 512 + 512]
                w.wait("g", "A", na_raw)
                if rope_busy[par] is not None:
                    w.wait("g", "V", rope_busy[par])
                w.op("g", lambda e: e.dma_start(out=swp[0:64, :], in_=raw[64:128, :]), f"S{par}", 16)
                ng = w.op("g", lambda e: e.dma_start(out=swp[64:128, :], in_=raw[0:64, :]), f"S{par}", 16)
                w.wait("v", f"S{par}", ng)
                w.wait("v", "A", na_raw)
                v_tt(pr1[:], swp, tabs[:, tcol:tcol + 512], ALU.mult)
                v_tt(pr2[:], raw, tabc[:, tcol:tcol + 512], ALU.mult)
                nv = w.op("v", lambda e: e.tensor_tensor(
                    out=dst_ap_fn(), in0=pr1[:], in1=pr2[:], op=ALU.add), "V")
                rope_busy[par] = nv
                return nv

            def proj16(psum_ap, lhs_fn, rhs_fn, sem="PE"):
                npe = None
                for ic in range(16):
                    npe = w.op("t", lambda e, i=ic: e.matmul(
                        out=psum_ap, lhsT=lhs_fn(i), rhs=rhs_fn(i),
                        start=(i == 0), stop=(i == 15)),
                        sem if ic == 15 else None)
                return npe

            w.wait("t", "G", nG_w)
            last_kv_dma = 0
            prev_grp_pe = None
            nA_evac = {0: None, 1: None}
            for grp in range(2):
                nev = None
                for tl in range(4):
                    nev = ln_transpose(xs, (grp * 4 + tl) * 128, tl, prev_grp_pe)
                w.wait("t", "V", nev)
                if debug and grp == 0:
                    w.dump(ln1t[:, :], dbg_ln1t[:, :])
                for g in range(4):
                    par = g % 2
                    if nA_evac[par] is not None:
                        w.wait("t", "A", nA_evac[par])
                    npe = proj16(
                        ps_p[:, par * 512:par * 512 + 512],
                        lambda i, gg=g: wkt_s[:, i * 512 + gg * 128:
                                              i * 512 + gg * 128 + 128],
                        lambda i: ln1t[:, i * 512:i * 512 + 512])
                    w.wait("a", "PE", npe)
                    na = w.op("a", lambda e, p=par: e.activation(
                        out=praw[:, p * 512:p * 512 + 512],
                        in_=ps_p[:, p * 512:p * 512 + 512], func=AF.Identity), "A")
                    nA_evac[par] = na
                    w.wait("v", "G", nG_tab)
                    w.wait("v", "O", last_kv_dma)
                    nv = rope(par, tab_c, tab_s, grp * 512,
                              lambda p=par: kvstg[:, p * 512:p * 512 + 512], na)
                    w.wait("g", "V", nv)
                    last_kv_dma = w.op("g", lambda e, gg=g, gr=grp, p=par: e.dma_start(
                        out=kvk_in[gg, :, gr * 512:gr * 512 + 512],
                        in_=kvstg[:, p * 512:p * 512 + 512]), "O", 16)
                for tl in range(4):
                    par = tl % 2
                    if nA_evac[par] is not None:
                        w.wait("t", "A", nA_evac[par])
                    npe = proj16(
                        ps_p[:, par * 512:par * 512 + 512],
                        lambda i, t_=tl: ln1t_v[:, i:i + 1, t_ * 128:t_ * 128 + 128],
                        lambda i: wvt_s[:, i * 512:i * 512 + 512])
                    w.wait("a", "PE", npe)
                    w.wait("a", "O", last_kv_dma)
                    na = w.op("a", lambda e, p=par: e.activation(
                        out=kvstg[:, p * 512:p * 512 + 512],
                        in_=ps_p[:, p * 512:p * 512 + 512], func=AF.Identity), "A")
                    nA_evac[par] = na
                    w.wait("g", "A", na)
                    last_kv_dma = w.op("g", lambda e, t_=tl, gr=grp, p=par: e.dma_start(
                        out=kvv_in[gr * 4 + t_, :, :],
                        in_=kvstg[:, p * 512:p * 512 + 512]), "O", 16)
                prev_grp_pe = npe
            w.wait("g", "O", last_kv_dma)
            w.op("g", lambda e: e.collective_compute(
                "AllGather", ALU.bypass, replica_groups=[list(range(NC))],
                ins=[kvk_in], outs=[kvk_ag]), "P")
            nP_ag = w.op("g", lambda e: e.collective_compute(
                "AllGather", ALU.bypass, replica_groups=[list(range(NC))],
                ins=[kvv_in], outs=[kvv_ag]), "P")

            for grp in range(2):
                nev = None
                for tl in range(4):
                    nev = ln_transpose(xq, (grp * 4 + tl) * 128, tl, prev_grp_pe)
                w.wait("t", "V", nev)
                ncol = {0: 0, 1: 0}
                npe_q = {0: None, 1: None}
                for h in range(16):
                    parw = h % 2
                    w.wait("g", "PE", npe_q[parw])
                    ncol[parw] = w.op("g", lambda e, hh=h, p=parw: e.dma_start(
                        out=wq_col[:, p * 2048:p * 2048 + 2048],
                        in_=wqt.rearrange("(ic p) o -> p ic o", p=128)
                        [:, :, hh * 128:hh * 128 + 128]), f"C{parw}", 16)
                    par = h % 2
                    if nA_evac[par] is not None:
                        w.wait("t", "A", nA_evac[par])
                    w.wait("t", f"C{parw}", ncol[parw])
                    npe = proj16(
                        ps_p[:, par * 512:par * 512 + 512],
                        lambda i, pw=parw: wq_col[:, pw * 2048 + i * 128:
                                                  pw * 2048 + i * 128 + 128],
                        lambda i: ln1t[:, i * 512:i * 512 + 512])
                    npe_q[parw] = npe
                    w.wait("a", "PE", npe)
                    na = w.op("a", lambda e, p=par: e.activation(
                        out=praw[:, p * 512:p * 512 + 512],
                        in_=ps_p[:, p * 512:p * 512 + 512], func=AF.Identity), "A")
                    nA_evac[par] = na
                    rope(par, tab_cq, tab_sq, 1024 + grp * 512,
                         lambda hh=h, g=grp: q_all[:, hh * 1024 + g * 512:
                                                   hh * 1024 + g * 512 + 512], na)
                for g in range(4):
                    par = g % 2
                    if nA_evac[par] is not None:
                        w.wait("t", "A", nA_evac[par])
                    npe = proj16(
                        ps_p[:, par * 512:par * 512 + 512],
                        lambda i, gg=g: wkt_s[:, i * 512 + gg * 128:
                                              i * 512 + gg * 128 + 128],
                        lambda i: ln1t[:, i * 512:i * 512 + 512])
                    w.wait("a", "PE", npe)
                    na = w.op("a", lambda e, p=par: e.activation(
                        out=praw[:, p * 512:p * 512 + 512],
                        in_=ps_p[:, p * 512:p * 512 + 512], func=AF.Identity), "A")
                    nA_evac[par] = na
                    rope(par, tab_c, tab_s, 1024 + grp * 512,
                         lambda gg=g, gr=grp: diag_k[:, gg * 1024 + gr * 512:
                                                     gg * 1024 + gr * 512 + 512], na)
                for tl in range(4):
                    par = tl % 2
                    if nA_evac[par] is not None:
                        w.wait("t", "A", nA_evac[par])
                    npe = proj16(
                        ps_p[:, par * 512:par * 512 + 512],
                        lambda i, t_=tl: ln1t_v[:, i:i + 1, t_ * 128:t_ * 128 + 128],
                        lambda i: wvt_s[:, i * 512:i * 512 + 512])
                    w.wait("a", "PE", npe)
                    na = w.op("a", lambda e, t_=tl, gr=grp, p=par: e.activation(
                        out=diag_v[:, (gr * 4 + t_) * 512:(gr * 4 + t_) * 512 + 512],
                        in_=ps_p[:, p * 512:p * 512 + 512], func=AF.Identity), "A")
                    nA_evac[par] = na
                prev_grp_pe = npe
            if debug:
                w.wait("g", "V", w.n["V"])
                w.wait("g", "A", w.n["A"])
                for dst_, src_ in ((dbg_tabc, tab_c), (dbg_tabs, tab_s)):
                    w.op("g", lambda e, d=dst_, s=src_: e.dma_start(
                        out=d[:, :], in_=s[:]), "G", 16)
                w.op("g", lambda e: e.dma_start(out=dbg_q[:, :], in_=q_all[:]), "G", 16)
                w.op("g", lambda e: e.dma_start(out=dbg_dk[:, :], in_=diag_k[:]), "G", 16)
                w.op("g", lambda e: e.dma_start(out=dbg_dv[:, :], in_=diag_v[:]), "G", 16)
                w.op("g", lambda e: e.dma_start(out=dbg_kvk[:, :, :], in_=kvk_in[:, :, :]), "G", 16)
                w.op("g", lambda e: e.dma_start(out=dbg_kvv[:, :, :], in_=kvv_in[:, :, :]), "G", 16)
            w.fence()

            with nc.Block() as BA:
                @BA.sync
                def _(e):
                    for f in w.cmds[0]["q"]:
                        f(e)

                @BA.gpsimd
                def _(e):
                    for f in w.cmds[0]["g"]:
                        f(e)

                @BA.vector
                def _(e):
                    for f in w.cmds[0]["v"]:
                        f(e)

                @BA.scalar
                def _(e):
                    for f in w.cmds[0]["a"]:
                        f(e)

                @BA.tensor
                def _(e):
                    for f in w.cmds[0]["t"]:
                        f(e)

        with contextlib.ExitStack() as sb:
            kb = sb.enter_context(nc.sbuf_tensor("kb", [128, 1024], BF16))
            vb = sb.enter_context(nc.sbuf_tensor("vb", [128, 1024], BF16))
            expst = sb.enter_context(nc.sbuf_tensor("expst", [128, 2048], BF16))
            attn = sb.enter_context(nc.sbuf_tensor("attn", [128, 16 * 1024], BF16))
            rcp = sb.enter_context(nc.sbuf_tensor("rcp", [128, 1024], F32))
            wot_col = sb.enter_context(nc.sbuf_tensor("wot_col", [128, 16 * 512], BF16))
            wfq_col = sb.enter_context(nc.sbuf_tensor("wfq_col", [128, 16 * 512], BF16))
            xqt = sb.enter_context(nc.sbuf_tensor("xqt", [128, JT * 2048], BF16))
            wq_stage = xqt[:, 0:D]
            aout_stg = None
            ki_b = sb.enter_context(nc.sbuf_tensor("ki_b", [128, D], I32))
            sc2 = [sb.enter_context(nc.psum_tensor(f"sc2_{i}", [128, 1024], F32))
                   for i in range(2)]
            acc2 = sb.enter_context(nc.psum_tensor("acc2", [128, 1024], F32))
            sums2 = sb.enter_context(nc.psum_tensor("sums2", [128, 1024], F32))
            aout_stg = rcp

            w.blk = 1
            kvk_v = kvk_ag.rearrange("r g p t -> r p g t")
            kb_v = kb.rearrange("p (c g t) -> p c g t", c=4, g=2)
            q_v = q_all.rearrange("p (h f) -> p h f", h=16)
            attn_v = attn.rearrange("p (h f) -> p h f", h=16)

            wst = {"ic1": 0, "ic2": 0, "v_done": None, "a_done": None}

            def wprep1():
                ic = wst["ic1"]; wst["ic1"] += 1
                if wst["v_done"] is not None:
                    w.wait("g", "V", wst["v_done"])
                nq = w.op("g", lambda e, i=ic: e.dma_start(
                    out=pool[0][:], in_=wft[i * 128:i * 128 + 128, :]), "G", 16)
                w.wait("v", "G", nq)
                nv = w.op("v", lambda e: e.tensor_reduce(
                    out=stx(I_T1), in_=pool[0][:], axis=AX.X, op=ALU.add,
                    apply_absolute_value=True), "V")
                vdrain()
                if ic == 0:
                    nv = w.op("v", lambda e: e.tensor_copy(out=wsum[:], in_=stx(I_T1)), "V")
                else:
                    nv = v_tt(wsum[:], wsum[:], stx(I_T1), ALU.add, "V")
                vdrain()
                wst["v_done"] = nv

            def wprep_scale():
                w.wait("g", "V", wst["v_done"])
                nq = w.op("g", lambda e: e.dma_start(out=wrow[:, :], in_=wsum[:, :]), "G", 16)
                w.wait("v", "G", nq)
                nvr = w.op("v", lambda e: e.tensor_reduce(
                    out=one1[0:1, 0:1], in_=wrow[:, :], axis=AX.X, op=ALU.add), "V")
                vdrain()
                w.wait("g", "V", nvr)
                nq2 = w.op("g", lambda e: e.dma_start(
                    out=sc_d[0:1, 4:5], in_=one1[0:1, 0:1]), "G", 16)
                w.wait("g", "G", nq2)
                nq3 = w.op("g", lambda e: e.dma_start(
                    out=scal[:, 4:5],
                    in_=sc_d[0:1, 4:5].partition_broadcast(128)), "G", 16)
                w.wait("v", "G", nq3)
                w.op("v", lambda e: e.tensor_scalar(
                    out=scal[:, 0:1], in0=scal[:, 4:5],
                    scalar1=1.0 / 4194304.0, scalar2=1e-5,
                    op0=ALU.mult, op1=ALU.max))
                vdrain()
                nv3 = w.op("v", lambda e: e.tensor_scalar(
                    out=scal[:, 1:2], in0=scal[:, 0:1], scalar1=0.5,
                    scalar2=None, op0=ALU.mult), "V")
                vdrain()
                wst["v_done"] = nv3
                return nv3

            def wprep2(nq_tau):
                ic = wst["ic2"]; wst["ic2"] += 1
                if wst["v_done"] is not None:
                    w.wait("q", "V", wst["v_done"])
                if wst["a_done"] is not None:
                    w.wait("q", "A", wst["a_done"])
                nq = w.op("q", lambda e, i=ic: e.dma_start(
                    out=pool[0][:], in_=wft[i * 128:i * 128 + 128, :]), "Q", 16)
                w.wait("a", "Q", nq)
                na0 = w.op("a", lambda e: e.activation(
                    out=sq_s[:], in_=pool[0][:], func=AF.Abs), "A")
                na = w.op("a", lambda e: e.activation(
                    out=pool[1][:], in_=pool[0][:], func=AF.Sign), "A")
                wst["a_done"] = na
                w.wait("v", "A", na)
                v_ts(sq_s[:], sq_s[:], scal[:, 1:2], op0=ALU.is_ge)
                nv2 = v_tt(wq_stage, sq_s[:], pool[1][:], ALU.mult, "V")
                wst["v_done"] = nv2
                w.wait("g", "V", nv2)
                return w.op("g", lambda e, i=ic: e.dma_start(
                    out=wfq_d[i:i + 1, :, :], in_=wq_stage), "G", 16)

            SKIPW = bool(os.environ.get("SKIPW"))
            JMAX = int(os.environ.get("JMAX", "8"))
            SKIP456 = bool(os.environ.get("SKIP456"))
            ss = 0
            li = 0
            nA_exp = {}
            nPE_sc = {}
            nPE_av = {}
            li_slot = {}
            load_q = {}
            nV_norm = None
            nq_tau = None
            ng_wfq = 0
            for j in range(min(JT, JMAX)):
                if SKIPW:
                    pass
                elif j <= 3:
                    for _ in range(4):
                        wprep1()
                elif j == 4:
                    nq_tau = wprep_scale()
                else:
                    for _ in range(5 if j < 7 else 6):
                        ng_wfq = wprep2(nq_tau)
                for p in range(2):
                    g0 = 2 * p
                    pend = None

                    def emit_av(info):
                        (pdiag, pj, pg0, ppar, plp, pdeps, pfirst, plast,
                         pss) = info
                        if pfirst and nV_norm is not None:
                            w.wait("t", "V", nV_norm)
                        for i in range(2):
                            w.wait("t", pdeps[i][0], pdeps[i][1])
                            g = pg0 + i
                            if pdiag:
                                vl = diag_v[:, pj * 512 + g * 128:
                                            pj * 512 + g * 128 + 128]
                            else:
                                vl = vb[:, plp * 256 + i * 128:
                                        plp * 256 + i * 128 + 128]
                            w.op("t", lambda e, vv=vl, pp=ppar, ii=i, f=pfirst,
                                 la=plast: e.matmul(
                                out=acc2[:, ii * 512:ii * 512 + 512], lhsT=vv,
                                rhs=expst[:, pp * 1024 + ii * 512:
                                          pp * 1024 + ii * 512 + 512],
                                start=f, stop=la), None)
                            nPE_av[pss] = w.op("t", lambda e, pp=ppar, ii=i,
                                               f=pfirst, la=plast: e.matmul(
                                out=sums2[:, ii * 512:ii * 512 + 512],
                                lhsT=ones_bf[:],
                                rhs=expst[:, pp * 1024 + ii * 512:
                                          pp * 1024 + ii * 512 + 512],
                                start=f, stop=la),
                                "PE" if i == 1 else None)

                    for s in range(L[j]):
                        diag = (s == 0)
                        par = ss % 2
                        if diag:
                            cur_li = None
                            lp = None
                        else:
                            kt = s - 1
                            r, loc = kt // 8, kt % 8
                            lpar = li % 4
                            if li - 4 >= 0:
                                w.wait("q", "PE", nPE_av.get(li_slot[li - 4], 0))
                            w.wait("q", "P", nP_ag)
                            w.op("q", lambda e, rr=r, lo=loc, gg=g0, lpx=lpar: e.dma_start(
                                out=kb_v[:, lpx:lpx + 1, :, :],
                                in_=kvk_v[rr:rr + 1, :, gg:gg + 2,
                                          lo * 128:lo * 128 + 128]), f"K{lpar}", 16)
                            nqv = w.op("q", lambda e, rr=r, lo=loc, gg=g0, lpx=lpar: e.dma_start(
                                out=vb[:, lpx * 256:lpx * 256 + 256],
                                in_=kvv_ag[rr:rr + 1, lo:lo + 1, :,
                                           gg * 128:gg * 128 + 256]), f"K{lpar}", 16)
                            load_q[li] = nqv
                            li_slot[li] = ss
                            cur_li = li
                            lp = lpar
                            li += 1
                        if not diag:
                            w.wait("t", f"K{lp}", load_q[cur_li])
                        if ss >= 2:
                            w.wait("t", "A", nA_exp[(ss - 2, 1)])
                        sc_ms = {}
                        for i in range(2):
                            g = g0 + i
                            if diag:
                                kl = diag_k[:, g * 1024 + j * 128:
                                            g * 1024 + j * 128 + 128]
                            else:
                                kl = kb[:, lp * 256 + i * 128: lp * 256 + i * 128 + 128]
                            sc_ms[i] = w.op("t", lambda e, kk=kl, gg=g, pp=par, ii=i, jj=j: e.matmul(
                                out=sc2[pp][:, ii * 512:ii * 512 + 512],
                                lhsT=kk,
                                rhs=q_v[:, 4 * gg:4 * gg + 4, jj * 128:jj * 128 + 128],
                                start=True, stop=True), "PE")
                        if pend is not None:
                            emit_av(pend)
                        bidx = BOFF[j] + s
                        deps = []
                        for i in range(2):
                            w.wait("a", "PE", sc_ms[i])
                            if ss >= 2:
                                w.wait("a", "PE", nPE_av.get(ss - 2, 0))
                            nA_exp[(ss, i)] = w.op("a", lambda e, pp=par, bi=bidx, ii=i: e.activation(
                                out=expst[:, pp * 1024 + ii * 512:
                                          pp * 1024 + ii * 512 + 512],
                                in_=sc2[pp][:, ii * 512:ii * 512 + 512],
                                func=AF.Exp, scale=epsb[:, 2:3],
                                bias=btab_b[:, bi:bi + 1]), "A")
                            if diag:
                                w.wait("v", "A", nA_exp[(ss, i)])
                                nvm = w.op("v", lambda e, pp=par, ii=i: e.tensor_tensor(
                                    out=expst[:, pp * 1024 + ii * 512:
                                              pp * 1024 + ii * 512 + 512],
                                    in0=expst[:, pp * 1024 + ii * 512:
                                              pp * 1024 + ii * 512 + 512],
                                    in1=caus_s[:, 0:512], op=ALU.mult), "V")
                                deps.append(("V", nvm))
                            else:
                                deps.append(("A", nA_exp[(ss, i)]))
                        pend = (diag, j, g0, par, lp, deps,
                                s == 0, s == L[j] - 1, ss)
                        ss += 1
                    emit_av(pend)
                    pend = None
                    w.wait("v", "PE", nPE_av[ss - 1])
                    w.op("v", lambda e: e.reciprocal(out=rcp[:], in_=sums2[:]))
                    vdrain()
                    nV_norm = w.op("v", lambda e, pp=p, jj=j: e.tensor_tensor(
                        out=attn_v[:, 8 * pp:8 * pp + 8, jj * 128:jj * 128 + 128],
                        in0=acc2[:], in1=rcp[:], op=ALU.mult), "V")

            w.fence()

            if SKIP456:
                for j in range(JT):
                    nv0 = w.op("v", lambda e: e.memset(pool[0][:], 0.0), "V")
                    w.wait("q", "V", nv0)
                    w.op("q", lambda e, jj=j: e.dma_start(
                        out=out[jj * 128:jj * 128 + 128, :], in_=pool[0][:]), "Q", 16)
                w.wait("q", "Q", w.n["Q"])
                w.wait("q", "K", w.n["K"])
                w.wait("g", "G", w.n["G"])
                w.wait("g", "S", w.n["S"])
                w.wait("g", "O", w.n["O"])
            _skip456 = SKIP456
            nA_stg = {0: None, 1: None}
            if _skip456:
                JT_RANGE = []
            else:
                JT_RANGE = list(range(JT))
            nQ_stg = {0: 0, 1: 0}
            npe_ob = None
            for ob in (range(4) if not _skip456 else []):
                if npe_ob is not None:
                    w.wait("g", "PE", npe_ob)
                ncol = w.op("g", lambda e, o=ob: e.dma_start(
                    out=wot_col[:],
                    in_=wot.rearrange("(ic p) o -> p ic o", p=128)
                    [:, :, o * 512:o * 512 + 512]), "C", 16)
                for j in range(JT):
                    par = j % 2
                    w.wait("t", "C", ncol)
                    if nA_stg[par] is not None:
                        w.wait("t", "A", nA_stg[par])
                    npe_ob = proj16(
                        acc2[:, par * 512:par * 512 + 512],
                        lambda i, jj=j: attn_v[:, i:i + 1, jj * 128:jj * 128 + 128],
                        lambda i: wot_col[:, i * 512:i * 512 + 512])
                    w.wait("a", "PE", npe_ob)
                    w.wait("a", f"T{par}", nQ_stg[par])
                    na = w.op("a", lambda e, pp=par: e.activation(
                        out=aout_stg[:, pp * 512:pp * 512 + 512],
                        in_=acc2[:, pp * 512:pp * 512 + 512], func=AF.Identity), "A")
                    nA_stg[par] = na
                    w.wait("q", "A", na)
                    nQ_stg[par] = w.op("q", lambda e, jj=j, o=ob, pp=par: e.dma_start(
                        out=ao_d[jj * 128:jj * 128 + 128, o * 512:o * 512 + 512],
                        in_=aout_stg[:, pp * 512:pp * 512 + 512]), f"T{par}", 16)
            prev_v = None
            for j in JT_RANGE:
                if prev_v is not None:
                    w.wait("q", "V", prev_v)
                w.wait("q", "T0", nQ_stg[0])
                w.wait("q", "T1", nQ_stg[1])
                nq1 = w.op("q", lambda e, jj=j: e.dma_start(
                    out=pool[0][:], in_=ao_d[jj * 128:jj * 128 + 128, :]), "Q", 16)
                nq2 = w.op("q", lambda e, jj=j: e.dma_start(
                    out=pool[1][:], in_=xq[jj * 128:jj * 128 + 128, :]), "Q", 16)
                ln_stats(pool[0][:], epsb[:, 0:1], nq1)
                v_ts(sq_s[:], pool[0][:], stx(I_NM), stx(I_RI), ALU.add, ALU.mult)
                w.wait("v", "Q", nq2)
                nv = v_tt(pool[0][:], sq_s[:], pool[1][:], ALU.add, "V")
                w.wait("q", "V", nv)
                w.op("q", lambda e, jj=j: e.dma_start(
                    out=h1_d[jj * 128:jj * 128 + 128, :], in_=pool[0][:]), "Q", 16)
                ln_stats(pool[0][:], epsb[:, 0:1], nv, rsem="V")
                nv_f = v_ts(pool[1][:], pool[0][:], stx(I_NM), stx(I_RI),
                            ALU.add, ALU.mult, sem="V")
                w.wait("a", "V", nv_f)
                na = w.op("a", lambda e: e.activation(
                    out=sq_s[:], in_=pool[1][:], func=AF.Square,
                    accum_out=stx(I_T1)), "A")
                w.wait("v", "A", na)
                v_ts(stx(I_T2), stx(I_T1), 1.0 / 2048.0)
                vdrain()
                nv2 = w.op("v", lambda e: e.tensor_copy(out=stx(I_T3), in_=stx(I_T2)), "V")
                vdrain()
                w.wait("a", "V", nv2)
                na2 = w.op("a", lambda e: e.activation(
                    out=stx(I_SD), in_=stx(I_T3), func=AF.Sqrt,
                    bias=epsb[:, 1:2]), "A")
                w.wait("v", "A", na2)
                w.op("v", lambda e: e.reciprocal(out=stx(I_RI2), in_=stx(I_SD)))
                vdrain()
                v_stt(pool[1][:], pool[1][:], stx(I_RI2), gam_b[:], ALU.mult, ALU.mult)
                nv3 = w.op("v", lambda e: e.tensor_reduce(
                    out=stx(I_T1), in_=pool[1][:], axis=AX.X, op=ALU.max,
                    apply_absolute_value=True), "V")
                vdrain()
                if j == 0:
                    nv4 = w.op("v", lambda e: e.tensor_copy(out=stx(I_RMX), in_=stx(I_T1)), "V")
                else:
                    nv4 = v_tt(stx(I_RMX), stx(I_RMX), stx(I_T1), ALU.max, "V")
                vdrain()
                prev_v = nv4
                w.wait("q", "V", nv4)
                w.op("q", lambda e, jj=j: e.dma_start(
                    out=xn_d[jj * 128:jj * 128 + 128, :], in_=pool[1][:]), "Q", 16)

            if not _skip456:
                w.wait("q", "V", prev_v)
                nqm = w.op("q", lambda e: e.dma_start(
                    out=wrow[:, :], in_=st8[:, I_RMX:I_RMX + 1]), "Q", 16)
                w.wait("v", "Q", nqm)
                nvm = w.op("v", lambda e: e.tensor_reduce(
                    out=one1[0:1, 4:5], in_=wrow[:, :], axis=AX.X, op=ALU.max), "V")
                vdrain()
                w.wait("q", "V", nvm)
                nqm2 = w.op("q", lambda e: e.dma_start(
                    out=mx_in[0:1, 0:1], in_=one1[0:1, 4:5]), "Q", 16)
                w.wait("g", "Q", nqm2)
                nP_ar = w.op("g", lambda e: e.collective_compute(
                    "AllReduce", ALU.max, replica_groups=[list(range(NC))],
                    ins=[mx_in], outs=[mx_out]), "P")
                w.wait("q", "P", nP_ar)
                nqm3 = w.op("q", lambda e: e.dma_start(
                    out=scal[:, 5:6],
                    in_=mx_out[0:1, 0:1].partition_broadcast(128)), "Q", 16)
                w.wait("v", "Q", nqm3)
                w.op("v", lambda e: e.tensor_scalar(
                    out=scal[:, 6:7], in0=scal[:, 5:6], scalar1=1e-5,
                    scalar2=None, op0=ALU.max))
                vdrain()
                w.op("v", lambda e: e.reciprocal(out=scal[:, 7:8], in_=scal[:, 6:7]))
                vdrain()
                w.op("v", lambda e: e.tensor_scalar(
                    out=scal[:, 2:3], in0=scal[:, 7:8], scalar1=127.0,
                    scalar2=None, op0=ALU.mult))
                w.op("v", lambda e: e.tensor_scalar(
                    out=scal[:, 8:9], in0=scal[:, 6:7], scalar1=INV127,
                    scalar2=None, op0=ALU.mult))
                vdrain()
                nq_sc = w.op("v", lambda e: e.tensor_tensor(
                    out=scal[:, 3:4], in0=scal[:, 8:9], in1=scal[:, 0:1],
                    op=ALU.mult), "V")
                vdrain()

                prev_v = None
                prev_evac = None
                for j in range(JT):
                    if prev_v is not None:
                        w.wait("q", "V", prev_v)
                    nq1 = w.op("q", lambda e, jj=j: e.dma_start(
                        out=pool[0][:], in_=xn_d[jj * 128:jj * 128 + 128, :]), "Q", 16)
                    w.wait("v", "Q", nq1)
                    w.wait("v", "V", nq_sc) if False else None
                    v_ts(pool[1][:], pool[0][:], scal[:, 2:3])
                    v_ts(pool[1][:], pool[1][:], -128.0, 127.0, ALU.max, ALU.min)
                    w.op("v", lambda e: e.tensor_copy(out=ki_b[:], in_=pool[1][:]))
                    w.op("v", lambda e: e.tensor_copy(out=sq_s[:], in_=ki_b[:]))
                    v_tt(pool[1][:], pool[1][:], sq_s[:], ALU.subtract)
                    v_ts(pool[0][:], pool[1][:], 0.5, op0=ALU.is_ge)
                    v_ts(pool[1][:], pool[1][:], -0.5, op0=ALU.is_lt)
                    v_tt(pool[0][:], pool[0][:], pool[1][:], ALU.subtract)
                    nv = v_tt(pool[1][:], sq_s[:], pool[0][:], ALU.add, "V")
                    w.wait("t", "V", nv)
                    if prev_evac is not None:
                        w.wait("t", "V", prev_evac)
                    npe_h = {}
                    for half in range(2):
                        for ic in range(8):
                            icg = half * 8 + ic
                            npe_h[half] = w.op("t", lambda e, i=icg, hh=half, ii=ic: e.transpose(
                                out=sc2[hh][:, ii * 128:ii * 128 + 128],
                                in_=pool[1][:, i * 128:i * 128 + 128],
                                identity=id_fs[:]), "PE" if ic == 7 else None)
                    nevs = []
                    w.wait("v", "G", ng_wfq)
                    for half in range(2):
                        w.wait("v", "PE", npe_h[half])
                        nevs.append(w.op("v", lambda e, jj=j, hh=half: e.tensor_copy(
                            out=xqt[:, jj * 2048 + hh * 1024: jj * 2048 + hh * 1024 + 1024],
                            in_=sc2[hh][:]), "V"))
                    prev_evac = nevs[-1]
                    prev_v = nevs[-1]

                nA_stg = {0: None, 1: None}
                nQ_stg = {0: 0, 1: 0}
                npe_ob = None
                for ob in range(4):
                    if npe_ob is not None:
                        w.wait("q", "PE", npe_ob)
                    w.wait("q", "G", ng_wfq)
                    ncol = w.op("q", lambda e, o=ob: e.dma_start(
                        out=wfq_col[:],
                        in_=wfq_d.rearrange("ic p o -> p ic o")
                        [:, :, o * 512:o * 512 + 512]), "C", 16)
                    for j in range(JT):
                        par = j % 2
                        w.wait("t", "C", ncol)
                        w.wait("t", "V", prev_evac)
                        if nA_stg[par] is not None:
                            w.wait("t", "A", nA_stg[par])
                        npe_ob = proj16(
                            acc2[:, par * 512:par * 512 + 512],
                            lambda i, jj=j: xqt[:, jj * 2048 + i * 128:
                                               jj * 2048 + i * 128 + 128],
                            lambda i: wfq_col[:, i * 512:i * 512 + 512])
                        w.wait("a", "PE", npe_ob)
                        w.wait("a", "Q", nQ_stg[par])
                        w.wait("a", "V", nq_sc)
                        na = w.op("a", lambda e, pp=par: e.activation(
                            out=aout_stg[:, pp * 512:pp * 512 + 512],
                            in_=acc2[:, pp * 512:pp * 512 + 512], func=AF.Identity,
                            scale=scal[:, 3:4]), "A")
                        nA_stg[par] = na
                        w.wait("q", "A", na)
                        nQ_stg[par] = w.op("q", lambda e, jj=j, o=ob, pp=par: e.dma_start(
                            out=fo_d[jj * 128:jj * 128 + 128, o * 512:o * 512 + 512],
                            in_=aout_stg[:, pp * 512:pp * 512 + 512]), "Q", 16)
                prev_v = None
                for j in range(JT):
                    if prev_v is not None:
                        w.wait("q", "V", prev_v)
                    w.wait("q", "Q", nQ_stg[0])
                    w.wait("q", "Q", nQ_stg[1])
                    nq1 = w.op("q", lambda e, jj=j: e.dma_start(
                        out=pool[0][:], in_=fo_d[jj * 128:jj * 128 + 128, :]), "Q", 16)
                    nq2 = w.op("q", lambda e, jj=j: e.dma_start(
                        out=pool[1][:], in_=h1_d[jj * 128:jj * 128 + 128, :]), "Q", 16)
                    ln_stats(pool[0][:], epsb[:, 0:1], nq1)
                    v_ts(sq_s[:], pool[0][:], stx(I_NM), stx(I_RI), ALU.add, ALU.mult)
                    w.wait("v", "Q", nq2)
                    nv = v_tt(pool[0][:], sq_s[:], pool[1][:], ALU.add, "V")
                    prev_v = nv
                    w.wait("q", "V", nv)
                    w.op("q", lambda e, jj=j: e.dma_start(
                        out=out[jj * 128:jj * 128 + 128, :], in_=pool[0][:]), "Q", 16)

                if debug:
                    w.wait("g", "V", w.n["V"])
                    w.wait("g", "A", w.n["A"])
                    w.wait("g", "Q", w.n["Q"])
                    w.op("g", lambda e: e.dma_start(out=dbg_attn[:, :], in_=attn[:]), "G", 16)
                    w.op("g", lambda e: e.dma_start(out=dbg_scal[:, :], in_=scal[:]), "G", 16)
                    w.op("g", lambda e: e.dma_start(out=dbg_one[:, :], in_=one1[:]), "G", 16)
                    w.op("g", lambda e: e.dma_start(out=dbg_xqt[:, :], in_=xqt[:]), "G", 16)
                    w.op("g", lambda e: e.dma_start(out=dbg_wfq[:, :, :], in_=wfq_d[:, :, :]), "G", 16)
            w.wait("q", "Q", w.n["Q"])
            for s_ in ("K0", "K1", "K2", "K3", "T0", "T1"):
                w.wait("q", s_, w.n[s_])
            w.wait("q", "C", w.n["C"])
            w.wait("g", "G", w.n["G"])
            for s_ in ("C0", "C1", "S0", "S1"):
                w.wait("g", s_, w.n[s_])
            w.wait("g", "O", w.n["O"])

            with nc.Block() as BB:
                @BB.sync
                def _(e):
                    for f in w.cmds[1]["q"]:
                        f(e)

                @BB.gpsimd
                def _(e):
                    for f in w.cmds[1]["g"]:
                        f(e)

                @BB.vector
                def _(e):
                    for f in w.cmds[1]["v"]:
                        f(e)

                @BB.scalar
                def _(e):
                    for f in w.cmds[1]["a"]:
                        f(e)

                @BB.tensor
                def _(e):
                    for f in w.cmds[1]["t"]:
                        f(e)
    return nc


def kernel(L0_attn_x, L0_attn_Wq, L0_attn_Wk, L0_attn_Wv, L0_attn_Wo,
           L0_ffn_weight, L0_ffn_norm_weight):
    x = np.ascontiguousarray(np.asarray(L0_attn_x, dtype=np.float32).reshape(S, D))
    Wq = np.asarray(L0_attn_Wq, dtype=np.float32)
    Wk = np.asarray(L0_attn_Wk, dtype=np.float32)
    Wv = np.asarray(L0_attn_Wv, dtype=np.float32)
    Wo = np.asarray(L0_attn_Wo, dtype=np.float32)
    Wf = np.asarray(L0_ffn_weight, dtype=np.float32)
    gam = np.ascontiguousarray(np.asarray(L0_ffn_norm_weight, dtype=np.float32))

    perm = np.concatenate([np.arange(0, 128, 2), np.arange(1, 128, 2)])
    Wq_p = Wq.reshape(16, 128, D)[:, perm, :].reshape(D, D)
    Wk_p = Wk.reshape(4, 128, D)[:, perm, :].reshape(512, D)
    wqt = np.ascontiguousarray(Wq_p.T)
    wkt = np.ascontiguousarray(Wk_p.T)
    wvt = np.ascontiguousarray(Wv.T)
    wot = np.ascontiguousarray(Wo.T)
    wft = np.ascontiguousarray(Wf.T)

    nc = build(debug=False)
    in_maps = []
    for c in range(NC):
        tiles = [8 * j + c for j in range(JT)]
        xq_c = np.ascontiguousarray(x.reshape(64, 128, D)[tiles].reshape(TPC, D))
        xs_c = np.ascontiguousarray(x[c * TPC:(c + 1) * TPC])
        tvals = np.zeros((1, 2048), dtype=np.float32)
        tvals[0, :1024] = np.arange(c * TPC, (c + 1) * TPC, dtype=np.float32)
        own_tok = np.concatenate([np.arange(t * 128, t * 128 + 128) for t in tiles])
        tvals[0, 1024:] = own_tok.astype(np.float32)
        bt = np.zeros((1, NSLOT), dtype=np.float32)
        for j in range(JT):
            t_own = 8 * j + c
            for s_ in range(1, L[j]):
                if s_ - 1 >= t_own:
                    bt[0, BOFF[j] + s_] = BKILL
        in_maps.append({
            "xq": xq_c, "xs": xs_c, "wqt": wqt, "wkt": wkt, "wvt": wvt,
            "wot": wot, "wft": wft, "gam": gam, "tidx": tvals, "btab": bt,
        })
    res = run_bass_kernel_spmd(nc, in_maps, core_ids=list(range(NC)))
    full = np.empty((S, D), dtype=np.float32)
    for c in range(NC):
        o = res.results[c]["out"]
        for j in range(JT):
            t = 8 * j + c
            full[t * 128:(t + 1) * 128] = o[j * 128:(j + 1) * 128]
    return full.reshape(1, S, D)


# version 05
# speedup vs baseline: 1.0011x; 1.0011x over previous
import contextlib
import math
import numpy as np
import ml_dtypes
import concourse.bass as bass
import concourse.mybir as mybir
from concourse.bass_utils import run_bass_kernel_spmd

F32 = mybir.dt.float32
BF16 = mybir.dt.bfloat16
I32 = mybir.dt.int32
AF = mybir.ActivationFunctionType
ALU = mybir.AluOpType
AX = mybir.AxisListType

S, D = 8192, 2048
NC, JT = 8, 8
TPC = JT * 128
L = [8 * j + 8 for j in range(JT)]
BOFF = [sum(L[:j]) for j in range(JT)]
NSLOT = sum(L)

TWO_PI = 2 * math.pi
C1 = float(np.float32(int(TWO_PI * 2 ** 10) / 2 ** 10))
C2 = float(np.float32(TWO_PI - C1))
C3 = float(np.float32(TWO_PI - C1 - float(np.float32(TWO_PI - C1))))
INV2PI = float(np.float32(1.0 / TWO_PI))
BOUND = 3.141590
PI2_HI = float(np.float32(math.pi / 2))
PI2_LO = float(np.float32(math.pi / 2 - float(np.float32(math.pi / 2))))
SC64 = 1.0 / math.sqrt(128.0)
SC_HI = float(np.float32(SC64))
SC_LO = float(np.float32(SC64 / SC_HI))
BKILL = -30000.0
INV127 = float(np.float32(1.0 / 127.0))

_ID_BF = np.eye(128, dtype=ml_dtypes.bfloat16)
_ID_F32 = np.eye(128, dtype=np.float32)
_CAUSAL = np.tile((np.arange(128)[:, None] <= np.arange(128)[None, :])
                  .astype(ml_dtypes.bfloat16), (1, 8))


import os
_SEMALIAS = {"K": "Q"}
for _kv in os.environ.get("SEMALIAS", "").split(","):
    if ":" in _kv:
        _a, _b = _kv.split(":")
        _SEMALIAS[_a] = _b


class _W:
    ENG = "qgvat"

    def __init__(self):
        self.cmds = {0: {e: [] for e in self.ENG}, 1: {e: [] for e in self.ENG}}
        self.blk = 0
        self.n = {k: 0 for k in ("Q", "G", "PE", "V", "A", "P", "B",
                                 "K", "C", "S", "O", "K0", "K1", "K2", "K3",
                                 "C0", "C1", "S0", "S1", "T0", "T1")}
        self.m = dict(_SEMALIAS)
        self.last_wait = {}
        self.sems = None

    def emit(self, eng, fn):
        self.cmds[self.blk][eng].append(fn)

    def wait(self, eng, sem, val):
        sem = self.m.get(sem, sem)
        if val is None or val <= 0:
            return
        key = (eng, sem)
        if self.last_wait.get(key, -1) >= val:
            return
        self.last_wait[key] = val
        self.emit(eng, lambda e, s=sem, v=val: e.wait_ge(self.sems[s], v))

    def op(self, eng, fn, sem=None, k=1):
        if sem is None:
            self.emit(eng, fn)
            return None
        sem = self.m.get(sem, sem)
        self.n[sem] += k
        v = self.n[sem]
        self.emit(eng, lambda e, s=sem, kk=k: fn(e).then_inc(self.sems[s], kk))
        return v

    def dump(self, src_ap, dst_ap):
        self.wait("g", "V", self.n["V"])
        self.wait("g", "A", self.n["A"])
        self.wait("g", "PE", self.n["PE"])
        self.wait("g", "Q", self.n["Q"])
        ng = self.op("g", lambda e: e.dma_start(out=dst_ap, in_=src_ap), "G", 16)
        for e in "qvat":
            self.wait(e, "G", ng)
        return ng

    def fence(self):
        self.n["B"] += 5
        tgt = self.n["B"]
        for e in self.ENG:
            self.emit(e, lambda en: en.nop().then_inc(self.sems["B"], 1))
        for e in self.ENG:
            self.emit(e, lambda en, t=tgt: en.wait_ge(self.sems["B"], t))
        return tgt


def build(debug=False):
    w = _W()
    nc = bass.Bass("TRN2", target_bir_lowering=False, debug=False)
    dbg_kind = "ExternalOutput" if debug else "Internal"

    xq = nc.dram_tensor("xq", [TPC, D], F32, kind="ExternalInput").ap()
    xs = nc.dram_tensor("xs", [TPC, D], F32, kind="ExternalInput").ap()
    wqt = nc.dram_tensor("wqt", [D, D], F32, kind="ExternalInput").ap()
    wkt = nc.dram_tensor("wkt", [D, 512], F32, kind="ExternalInput").ap()
    wvt = nc.dram_tensor("wvt", [D, 512], F32, kind="ExternalInput").ap()
    wot = nc.dram_tensor("wot", [D, D], F32, kind="ExternalInput").ap()
    wft = nc.dram_tensor("wft", [D, D], F32, kind="ExternalInput").ap()
    gam = nc.dram_tensor("gam", [D], F32, kind="ExternalInput").ap()
    tidx = nc.dram_tensor("tidx", [1, 2048], F32, kind="ExternalInput").ap()
    btab = nc.dram_tensor("btab", [1, NSLOT], F32, kind="ExternalInput").ap()
    out = nc.dram_tensor("out", [TPC, D], F32, kind="ExternalOutput").ap()

    kvk_in = nc.dram_tensor("kvk_in", [4, 128, 1024], BF16).ap()
    kvv_in = nc.dram_tensor("kvv_in", [8, 128, 512], BF16).ap()
    kvk_ag = nc.dram_tensor("kvk_ag", [NC, 4, 128, 1024], BF16).ap()
    kvv_ag = nc.dram_tensor("kvv_ag", [NC, 8, 128, 512], BF16).ap()
    mx_in = nc.dram_tensor("mx_in", [1, 16], F32).ap()
    mx_out = nc.dram_tensor("mx_out", [1, 16], F32).ap()
    sc_d = nc.dram_tensor("sc_d", [1, 16], F32).ap()
    h1_d = nc.dram_tensor("h1_d", [TPC, D], F32, kind=dbg_kind).ap()
    xn_d = nc.dram_tensor("xn_d", [TPC, D], F32, kind=dbg_kind).ap()
    ao_d = nc.dram_tensor("ao_d", [TPC, D], F32, kind=dbg_kind).ap()
    fo_d = nc.dram_tensor("fo_d", [TPC, D], F32, kind=dbg_kind).ap()
    wfq_d = nc.dram_tensor("wfq_d", [16, 128, D], BF16).ap()

    if debug:
        dbg_q = nc.dram_tensor("dbg_q", [128, 16 * 1024], BF16, kind="ExternalOutput").ap()
        dbg_dk = nc.dram_tensor("dbg_dk", [128, 4096], BF16, kind="ExternalOutput").ap()
        dbg_dv = nc.dram_tensor("dbg_dv", [128, 4096], BF16, kind="ExternalOutput").ap()
        dbg_attn = nc.dram_tensor("dbg_attn", [128, 16 * 1024], BF16, kind="ExternalOutput").ap()
        dbg_tabc = nc.dram_tensor("dbg_tabc", [128, 2048], BF16, kind="ExternalOutput").ap()
        dbg_tabs = nc.dram_tensor("dbg_tabs", [128, 2048], BF16, kind="ExternalOutput").ap()
        dbg_kvk = nc.dram_tensor("dbg_kvk", [4, 128, 1024], BF16, kind="ExternalOutput").ap()
        dbg_kvv = nc.dram_tensor("dbg_kvv", [8, 128, 512], BF16, kind="ExternalOutput").ap()
        dbg_ln1t = nc.dram_tensor("dbg_ln1t", [128, 8192], BF16, kind="ExternalOutput").ap()
        dbg_scal = nc.dram_tensor("dbg_scal", [128, 16], F32, kind="ExternalOutput").ap()
        dbg_one = nc.dram_tensor("dbg_one", [1, 16], F32, kind="ExternalOutput").ap()
        dbg_wfq = nc.dram_tensor("dbg_wfq", [16, 128, D], BF16, kind="ExternalOutput").ap()
        dbg_xqt = nc.dram_tensor("dbg_xqt", [128, JT * 2048], BF16, kind="ExternalOutput").ap()
        dbg_fi = nc.dram_tensor("dbg_fi", [128, 1], F32, kind="ExternalOutput").ap()
        dbg_ang = nc.dram_tensor("dbg_ang", [128, 1024], F32, kind="ExternalOutput").ap()
        dbg_sin = nc.dram_tensor("dbg_sin", [128, 1024], F32, kind="ExternalOutput").ap()
        dbg_lnr = nc.dram_tensor("dbg_lnr", [128, D], BF16, kind="ExternalOutput").ap()
        dbg_st8 = nc.dram_tensor("dbg_st8", [128, 16], F32, kind="ExternalOutput").ap()
        dbg_wrow = nc.dram_tensor("dbg_wrow", [1, 128], F32, kind="ExternalOutput").ap()
        dbg_tokb = nc.dram_tensor("dbg_tokb", [128, 1024], F32, kind="ExternalOutput").ap()
        dbg_red = nc.dram_tensor("dbg_red", [128, 1024], F32, kind="ExternalOutput").ap()
        dbg_ang2 = nc.dram_tensor("dbg_ang2", [128, 1024], F32, kind="ExternalOutput").ap()
        dbg_fe = nc.dram_tensor("dbg_fe", [128, 1], F32, kind="ExternalOutput").ap()
    id_bf_d = nc.inline_tensor(_ID_BF, name="id_bf").ap()
    id_f_d = nc.inline_tensor(_ID_F32, name="id_f").ap()
    caus_d = nc.inline_tensor(np.asarray(_CAUSAL), name="caus").ap()

    with contextlib.ExitStack() as top:
        sems = {}
        for s in ("Q", "G", "PE", "V", "A", "P", "B", "K", "C", "S", "O",
                  "K0", "K1", "K2", "K3", "C0", "C1", "S0", "S1", "T0", "T1"):
            sems[s] = top.enter_context(nc.semaphore("s" + s))
        w.sems = sems

        q_all = top.enter_context(nc.sbuf_tensor("q_all", [128, 16 * 1024], BF16))
        diag_k = top.enter_context(nc.sbuf_tensor("diag_k", [128, 4 * 1024], BF16))
        diag_v = top.enter_context(nc.sbuf_tensor("diag_v", [128, JT * 512], BF16))
        gam_b = top.enter_context(nc.sbuf_tensor("gam_b", [128, D], F32))
        btab_b = top.enter_context(nc.sbuf_tensor("btab_b", [128, NSLOT], F32))
        scal = top.enter_context(nc.sbuf_tensor("scal", [128, 16], F32))
        id_bs = top.enter_context(nc.sbuf_tensor("id_bs", [128, 128], BF16))
        id_fs = top.enter_context(nc.sbuf_tensor("id_fs", [128, 128], F32))
        caus_s = top.enter_context(nc.sbuf_tensor("caus_s", [128, 1024], BF16))
        ones_bf = top.enter_context(nc.sbuf_tensor("ones_bf", [128, 128], BF16))
        pool = [top.enter_context(nc.sbuf_tensor(f"pool{i}", [128, D], F32))
                for i in range(2)]
        sq_s = top.enter_context(nc.sbuf_tensor("sq_s", [128, D], F32))
        st8 = top.enter_context(nc.sbuf_tensor("st8", [128, 16], F32))
        one1 = top.enter_context(nc.sbuf_tensor("one1", [1, 16], F32))
        wsum = top.enter_context(nc.sbuf_tensor("wsum", [128, 1], F32))
        wrow = top.enter_context(nc.sbuf_tensor("wrow", [1, 128], F32))
        epsb = top.enter_context(nc.sbuf_tensor("epsb", [128, 4], F32))

        (I_SUM, I_NM, I_VAR, I_SD, I_RI, I_SQ, I_M2, I_RMX, I_T1, I_RI2,
         I_T2, I_T3) = range(12)

        def stx(i):
            return st8[:, i:i + 1]

        def vdrain():
            w.emit("v", lambda e: e.drain())

        def adrain():
            w.emit("a", lambda e: e.drain())

        def v_ts(dst, src, s1, s2=None, op0=ALU.mult, op1=None, sem=None):
            if op1 is None:
                return w.op("v", lambda e: e.tensor_scalar(
                    out=dst, in0=src, scalar1=s1, scalar2=None, op0=op0), sem)
            return w.op("v", lambda e: e.tensor_scalar(
                out=dst, in0=src, scalar1=s1, scalar2=s2, op0=op0, op1=op1), sem)

        def v_tt(dst, a, bb, op, sem=None):
            return w.op("v", lambda e: e.tensor_tensor(out=dst, in0=a, in1=bb, op=op), sem)

        def v_stt(dst, a, s, bb, op0, op1, sem=None):
            return w.op("v", lambda e: e.scalar_tensor_tensor(
                out=dst, in0=a, scalar=s, in1=bb, op0=op0, op1=op1), sem)

        def ln_stats(px, eps, nq_ready, rsem="Q"):
            if rsem != "V":
                w.wait("v", rsem, nq_ready)
            nv1 = w.op("v", lambda e: e.tensor_reduce(
                out=stx(I_SUM), in_=px, axis=AX.X, op=ALU.add), "V")
            w.wait("a", rsem, nq_ready)
            w.wait("a", "V", nv1)
            na1 = w.op("a", lambda e: e.activation(
                out=sq_s[:], in_=px, func=AF.Square, accum_out=stx(I_SQ)), "A")
            w.wait("v", "A", na1)
            vdrain()
            v_ts(stx(I_NM), stx(I_SUM), -1.0 / 2048.0)
            vdrain()
            v_ts(stx(I_M2), stx(I_NM), stx(I_NM))
            v_ts(stx(I_VAR), stx(I_SQ), 1.0 / 2048.0)
            vdrain()
            nv2 = v_tt(stx(I_VAR), stx(I_VAR), stx(I_M2), ALU.subtract, "V")
            w.wait("a", "V", nv2)
            na2 = w.op("a", lambda e, ee=eps: e.activation(
                out=stx(I_SD), in_=stx(I_VAR), func=AF.Sqrt, bias=ee), "A")
            w.wait("v", "A", na2)
            nvr = w.op("v", lambda e: e.reciprocal(out=stx(I_RI), in_=stx(I_SD)), "V")
            vdrain()
            return nvr

        with contextlib.ExitStack() as sa:
            tab_c = sa.enter_context(nc.sbuf_tensor("tab_c", [128, 2048], BF16))
            tab_s = sa.enter_context(nc.sbuf_tensor("tab_s", [128, 2048], BF16))
            tab_cq = sa.enter_context(nc.sbuf_tensor("tab_cq", [128, 2048], BF16))
            tab_sq = sa.enter_context(nc.sbuf_tensor("tab_sq", [128, 2048], BF16))
            tokb_t = sa.enter_context(nc.sbuf_tensor("tokb_t", [128, 1024], F32))
            ang_t = sa.enter_context(nc.sbuf_tensor("ang_t", [128, 1024], F32))
            tr1_t = sa.enter_context(nc.sbuf_tensor("tr1_t", [128, 1024], F32))
            tr2_t = sa.enter_context(nc.sbuf_tensor("tr2_t", [128, 1024], F32))
            tokb = tokb_t[:, :]
            ang = ang_t[:, :]
            tr1 = tr1_t[:, :]
            tr2 = tr2_t[:, :]
            tr3 = tokb_t[:, :]
            sneg = ang_t[:, :]
            trk = sa.enter_context(nc.sbuf_tensor("trk", [128, 1024], I32))
            sin_f = sa.enter_context(nc.sbuf_tensor("sin_f", [128, 1024], F32))
            cos_f = sa.enter_context(nc.sbuf_tensor("cos_f", [128, 1024], F32))
            fi = sa.enter_context(nc.sbuf_tensor("fi", [128, 1], F32))
            fe = sa.enter_context(nc.sbuf_tensor("fe", [128, 1], F32))
            lnv = sa.enter_context(nc.sbuf_tensor("lnv", [128, 1], F32))
            tenk = sa.enter_context(nc.sbuf_tensor("tenk", [128, 1], F32))
            ln1t = sa.enter_context(nc.sbuf_tensor("ln1t", [128, 16 * 512], BF16))
            lnraw = sa.enter_context(nc.sbuf_tensor("lnraw", [128, 2 * D], BF16))
            wkt_s = sa.enter_context(nc.sbuf_tensor("wkt_s", [128, 16 * 512], BF16))
            wvt_s = sa.enter_context(nc.sbuf_tensor("wvt_s", [128, 16 * 512], BF16))
            wq_col = sa.enter_context(nc.sbuf_tensor("wq_col", [128, 2 * 2048], BF16))
            praw = sa.enter_context(nc.sbuf_tensor("praw", [128, 2 * 512], BF16))
            pswp = sa.enter_context(nc.sbuf_tensor("pswp", [128, 2 * 512], BF16))
            pr1 = sa.enter_context(nc.sbuf_tensor("pr1", [128, 512], BF16))
            pr2 = sa.enter_context(nc.sbuf_tensor("pr2", [128, 512], BF16))
            kvstg = sa.enter_context(nc.sbuf_tensor("kvstg", [128, 2 * 512], BF16))
            ps_tr = sa.enter_context(nc.psum_tensor("ps_tr", [128, 2048], BF16))
            ps_p = sa.enter_context(nc.psum_tensor("ps_p", [128, 2 * 512], F32))

            w.blk = 0
            w.op("q", lambda e: e.dma_start(out=id_fs[:], in_=id_f_d[:, :]), "Q", 16)
            w.op("q", lambda e: e.dma_start(
                out=tokb[0:64, :], in_=tidx[0:1, 0:1024].partition_broadcast(64)), "Q", 16)
            nQ_tok = w.op("q", lambda e: e.dma_start(
                out=tokb[64:128, :],
                in_=tidx[0:1, 1024:2048].partition_broadcast(64)), "Q", 16)
            w.op("q", lambda e: e.dma_start(
                out=gam_b[:], in_=gam[None, :].partition_broadcast(128)[:, 0, :]), "Q", 16)
            w.op("q", lambda e: e.dma_start(
                out=btab_b[:], in_=btab[0:1, :].partition_broadcast(128)), "Q", 16)
            nV_z = w.op("v", lambda e: e.memset(one1[0:1, 8:16], 0.0), "V")
            w.wait("q", "V", nV_z)
            w.op("q", lambda e: e.dma_start(out=mx_in[0:1, 0:8],
                                            in_=one1[0:1, 8:16]), "Q", 16)
            w.op("q", lambda e: e.dma_start(out=mx_in[0:1, 8:16],
                                            in_=one1[0:1, 8:16]), "Q", 16)
            nG_id = w.op("g", lambda e: e.dma_start(out=id_bs[:], in_=id_bf_d[:, :]), "G", 16)
            nG_caus = w.op("g", lambda e: e.dma_start(out=caus_s[:], in_=caus_d[:, :]), "G", 16)
            w.op("g", lambda e: e.dma_start(
                out=wkt_s[:], in_=wkt.rearrange("(ic p) o -> p ic o", p=128)), "G", 16)
            nG_w = w.op("g", lambda e: e.dma_start(
                out=wvt_s[:], in_=wvt.rearrange("(ic p) o -> p ic o", p=128)), "G", 16)
            w.op("v", lambda e: e.memset(ones_bf[:], 1.0), "V")
            w.op("v", lambda e: e.memset(epsb[:, 0:1], 1e-5))
            w.op("v", lambda e: e.memset(epsb[:, 1:2], 1e-6))
            nV_eps = w.op("v", lambda e: e.memset(epsb[:, 2:3], SC_HI), "V")
            w.wait("a", "V", nV_eps)

            w.op("g", lambda e: e.iota(
                out=fi[0:64, :], pattern=[[0, 1]], base=0, channel_multiplier=1,
                allow_small_or_imprecise_dtypes=True), "P")
            nP_iota = w.op("g", lambda e: e.iota(
                out=fi[64:128, :], pattern=[[0, 1]], base=0, channel_multiplier=1,
                allow_small_or_imprecise_dtypes=True), "P")
            nV_tenk = w.op("v", lambda e: e.memset(tenk[:], 10000.0), "V")
            w.wait("a", "V", nV_tenk)
            w.op("a", lambda e: e.activation(out=lnv[:], in_=tenk[:], func=AF.Ln), "A")
            adrain()
            w.wait("v", "P", nP_iota)
            nV_fe = v_ts(fe[:], fi[:], -1.0 / 64.0, sem="V")
            w.wait("a", "V", nV_fe)
            nA_fr = w.op("a", lambda e: e.activation(
                out=fi[:], in_=fe[:], func=AF.Exp, scale=lnv[:]), "A")
            w.wait("v", "Q", w.n["Q"])
            w.wait("v", "A", nA_fr)
            w.op("v", lambda e: e.tensor_copy(out=fe[:], in_=fi[:]))
            vdrain()
            nV_ang = v_ts(ang[:], tokb[:], fe[:], sem="V")
            if debug:
                w.dump(fe[:], dbg_fe[:, :])
                w.dump(fi[:], dbg_fi[:, :])
                w.dump(tokb, dbg_tokb[:, :])
                w.dump(ang, dbg_ang[:, :])
                w.dump(ang, dbg_ang2[:, :])

            def reduce_sin(src, dst):
                v_ts(tr1[:], src, INV2PI)
                vdrain()
                w.op("v", lambda e: e.tensor_copy(out=trk[:], in_=tr1[:]))
                vdrain()
                w.op("v", lambda e: e.tensor_copy(out=tr2[:], in_=trk[:]))
                vdrain()
                v_stt(tr1[:], tr2[:], -C1, src, ALU.mult, ALU.add)
                vdrain()
                v_stt(tr1[:], tr2[:], -C2, tr1[:], ALU.mult, ALU.add)
                vdrain()
                v_stt(tr1[:], tr2[:], -C3, tr1[:], ALU.mult, ALU.add)
                vdrain()
                v_ts(tr2[:], tr1[:], BOUND, op0=ALU.is_gt)
                v_ts(tr3[:], tr1[:], -BOUND, op0=ALU.is_lt)
                vdrain()
                v_tt(tr2[:], tr2[:], tr3[:], ALU.subtract)
                vdrain()
                v_stt(tr1[:], tr2[:], -C1, tr1[:], ALU.mult, ALU.add)
                vdrain()
                v_stt(tr1[:], tr2[:], -C2, tr1[:], ALU.mult, ALU.add)
                vdrain()
                nv = v_stt(tr1[:], tr2[:], -C3, tr1[:], ALU.mult, ALU.add, "V")
                vdrain()
                if debug:
                    w.dump(tr1, dbg_red[:, :])
                w.wait("a", "V", nv)
                return w.op("a", lambda e, d=dst: e.activation(
                    out=d[:], in_=tr1[:], func=AF.Sin), "A")

            vdrain()
            nA_sin = reduce_sin(ang[:], sin_f)
            if debug:
                w.wait("g", "A", nA_sin)
                w.dump(sin_f[:], dbg_sin[:, :])
            w.wait("v", "A", nA_sin)
            v_ts(ang[:], ang[:], PI2_HI, PI2_LO, ALU.add, ALU.add)
            vdrain()
            nA_cos = reduce_sin(ang[:], cos_f)
            w.wait("v", "A", nA_cos)
            v_ts(sneg[:], sin_f[:], -1.0)
            v_ts(tr1[:], cos_f[:], SC_LO)
            v_ts(tr2[:], sin_f[:], SC_LO)
            vdrain()
            nV_tab = v_ts(tr3[:], sneg[:], SC_LO, sem="V")
            vdrain()
            w.wait("g", "V", nV_tab)
            nG_tab = None
            for dst, half, srcp in ((tab_c, 0, cos_f), (tab_c, 1, cos_f),
                                    (tab_s, 0, sneg), (tab_s, 1, sin_f),
                                    (tab_cq, 0, tr1), (tab_cq, 1, tr1),
                                    (tab_sq, 0, tr3), (tab_sq, 1, tr2)):
                for th in range(2):
                    nG_tab = w.op("g", lambda e, d=dst, h=half, s=srcp, t=th:
                                  e.dma_start(
                        out=d[64 * h:64 * h + 64, t * 1024:t * 1024 + 1024],
                        in_=s[64 * t:64 * t + 64, :]), "G", 16)

            ln1t_v = ln1t.rearrange("p (ic f) -> p ic f", ic=16)
            busy = {"pv": {0: None, 1: None}, "pa": {0: None, 1: None},
                    "tr": None}

            dbg_first = {"done": False}

            def ln_transpose(src_dram, row0, gslot, npe_free):
                par = gslot % 2
                px = pool[par][:]
                lw = lnraw[:, par * D:par * D + D]
                w.wait("q", "V", busy["pv"][par])
                w.wait("q", "A", busy["pa"][par])
                nq = w.op("q", lambda e, r=row0, sd=src_dram: e.dma_start(
                    out=px, in_=sd[r:r + 128, :]), "Q", 16)
                ln_stats(px, epsb[:, 0:1], nq)
                busy["pa"][par] = w.n["A"]
                nv = w.op("v", lambda e, p=px, ll=lw: e.tensor_scalar(
                    out=ll, in0=p, scalar1=stx(I_NM), scalar2=stx(I_RI),
                    op0=ALU.add, op1=ALU.mult), "V")
                busy["pv"][par] = nv
                w.wait("t", "V", nv)
                w.wait("t", "G", nG_id)
                w.wait("t", "V", busy["tr"])
                npe = None
                for ic in range(16):
                    npe = w.op("t", lambda e, i=ic, ll=lw: e.transpose(
                        out=ps_tr[:, i * 128:i * 128 + 128],
                        in_=ll[:, i * 128:i * 128 + 128], identity=id_bs[:]),
                        "PE" if ic == 15 else None)
                w.wait("v", "PE", npe)
                if npe_free is not None:
                    w.wait("v", "PE", npe_free)
                nev = w.op("v", lambda e, g=gslot: e.tensor_copy(
                    out=ln1t_v[:, :, g * 128:g * 128 + 128], in_=ps_tr[:]), "V")
                busy["tr"] = nev
                if debug and not dbg_first["done"]:
                    dbg_first["done"] = True
                    w.dump(lw, dbg_lnr[:, :])
                    w.dump(st8[:], dbg_st8[:, :])
                return nev

            rope_busy = {0: None, 1: None}

            def rope(par, tabc, tabs, tcol, dst_ap_fn, na_raw):
                raw = praw[:, par * 512: par * 512 + 512]
                swp = pswp[:, par * 512: par * 512 + 512]
                w.wait("g", "A", na_raw)
                if rope_busy[par] is not None:
                    w.wait("g", "V", rope_busy[par])
                w.op("g", lambda e: e.dma_start(out=swp[0:64, :], in_=raw[64:128, :]), f"S{par}", 16)
                ng = w.op("g", lambda e: e.dma_start(out=swp[64:128, :], in_=raw[0:64, :]), f"S{par}", 16)
                w.wait("v", f"S{par}", ng)
                w.wait("v", "A", na_raw)
                v_tt(pr1[:], swp, tabs[:, tcol:tcol + 512], ALU.mult)
                v_tt(pr2[:], raw, tabc[:, tcol:tcol + 512], ALU.mult)
                nv = w.op("v", lambda e: e.tensor_tensor(
                    out=dst_ap_fn(), in0=pr1[:], in1=pr2[:], op=ALU.add), "V")
                rope_busy[par] = nv
                return nv

            def proj16(psum_ap, lhs_fn, rhs_fn, sem="PE"):
                npe = None
                for ic in range(16):
                    npe = w.op("t", lambda e, i=ic: e.matmul(
                        out=psum_ap, lhsT=lhs_fn(i), rhs=rhs_fn(i),
                        start=(i == 0), stop=(i == 15)),
                        sem if ic == 15 else None)
                return npe

            w.wait("t", "G", nG_w)
            last_kv_dma = 0
            prev_grp_pe = None
            nA_evac = {0: None, 1: None}
            for grp in range(2):
                nev = None
                for tl in range(4):
                    nev = ln_transpose(xs, (grp * 4 + tl) * 128, tl, prev_grp_pe)
                w.wait("t", "V", nev)
                if debug and grp == 0:
                    w.dump(ln1t[:, :], dbg_ln1t[:, :])
                for g in range(4):
                    par = g % 2
                    if nA_evac[par] is not None:
                        w.wait("t", "A", nA_evac[par])
                    npe = proj16(
                        ps_p[:, par * 512:par * 512 + 512],
                        lambda i, gg=g: wkt_s[:, i * 512 + gg * 128:
                                              i * 512 + gg * 128 + 128],
                        lambda i: ln1t[:, i * 512:i * 512 + 512])
                    w.wait("a", "PE", npe)
                    na = w.op("a", lambda e, p=par: e.activation(
                        out=praw[:, p * 512:p * 512 + 512],
                        in_=ps_p[:, p * 512:p * 512 + 512], func=AF.Identity), "A")
                    nA_evac[par] = na
                    w.wait("v", "G", nG_tab)
                    w.wait("v", "O", last_kv_dma)
                    nv = rope(par, tab_c, tab_s, grp * 512,
                              lambda p=par: kvstg[:, p * 512:p * 512 + 512], na)
                    w.wait("g", "V", nv)
                    last_kv_dma = w.op("g", lambda e, gg=g, gr=grp, p=par: e.dma_start(
                        out=kvk_in[gg, :, gr * 512:gr * 512 + 512],
                        in_=kvstg[:, p * 512:p * 512 + 512]), "O", 16)
                for tl in range(4):
                    par = tl % 2
                    if nA_evac[par] is not None:
                        w.wait("t", "A", nA_evac[par])
                    npe = proj16(
                        ps_p[:, par * 512:par * 512 + 512],
                        lambda i, t_=tl: ln1t_v[:, i:i + 1, t_ * 128:t_ * 128 + 128],
                        lambda i: wvt_s[:, i * 512:i * 512 + 512])
                    w.wait("a", "PE", npe)
                    w.wait("a", "O", last_kv_dma)
                    na = w.op("a", lambda e, p=par: e.activation(
                        out=kvstg[:, p * 512:p * 512 + 512],
                        in_=ps_p[:, p * 512:p * 512 + 512], func=AF.Identity), "A")
                    nA_evac[par] = na
                    w.wait("g", "A", na)
                    last_kv_dma = w.op("g", lambda e, t_=tl, gr=grp, p=par: e.dma_start(
                        out=kvv_in[gr * 4 + t_, :, :],
                        in_=kvstg[:, p * 512:p * 512 + 512]), "O", 16)
                prev_grp_pe = npe
            w.wait("g", "O", last_kv_dma)
            w.op("g", lambda e: e.collective_compute(
                "AllGather", ALU.bypass, replica_groups=[list(range(NC))],
                ins=[kvk_in], outs=[kvk_ag]), "P")
            nP_ag = w.op("g", lambda e: e.collective_compute(
                "AllGather", ALU.bypass, replica_groups=[list(range(NC))],
                ins=[kvv_in], outs=[kvv_ag]), "P")

            for grp in range(2):
                nev = None
                for tl in range(4):
                    nev = ln_transpose(xq, (grp * 4 + tl) * 128, tl, prev_grp_pe)
                w.wait("t", "V", nev)
                ncol = {0: 0, 1: 0}
                npe_q = {0: None, 1: None}
                for h in range(16):
                    parw = h % 2
                    w.wait("g", "PE", npe_q[parw])
                    ncol[parw] = w.op("g", lambda e, hh=h, p=parw: e.dma_start(
                        out=wq_col[:, p * 2048:p * 2048 + 2048],
                        in_=wqt.rearrange("(ic p) o -> p ic o", p=128)
                        [:, :, hh * 128:hh * 128 + 128]), f"C{parw}", 16)
                    par = h % 2
                    if nA_evac[par] is not None:
                        w.wait("t", "A", nA_evac[par])
                    w.wait("t", f"C{parw}", ncol[parw])
                    npe = proj16(
                        ps_p[:, par * 512:par * 512 + 512],
                        lambda i, pw=parw: wq_col[:, pw * 2048 + i * 128:
                                                  pw * 2048 + i * 128 + 128],
                        lambda i: ln1t[:, i * 512:i * 512 + 512])
                    npe_q[parw] = npe
                    w.wait("a", "PE", npe)
                    na = w.op("a", lambda e, p=par: e.activation(
                        out=praw[:, p * 512:p * 512 + 512],
                        in_=ps_p[:, p * 512:p * 512 + 512], func=AF.Identity), "A")
                    nA_evac[par] = na
                    rope(par, tab_cq, tab_sq, 1024 + grp * 512,
                         lambda hh=h, g=grp: q_all[:, hh * 1024 + g * 512:
                                                   hh * 1024 + g * 512 + 512], na)
                for g in range(4):
                    par = g % 2
                    if nA_evac[par] is not None:
                        w.wait("t", "A", nA_evac[par])
                    npe = proj16(
                        ps_p[:, par * 512:par * 512 + 512],
                        lambda i, gg=g: wkt_s[:, i * 512 + gg * 128:
                                              i * 512 + gg * 128 + 128],
                        lambda i: ln1t[:, i * 512:i * 512 + 512])
                    w.wait("a", "PE", npe)
                    na = w.op("a", lambda e, p=par: e.activation(
                        out=praw[:, p * 512:p * 512 + 512],
                        in_=ps_p[:, p * 512:p * 512 + 512], func=AF.Identity), "A")
                    nA_evac[par] = na
                    rope(par, tab_c, tab_s, 1024 + grp * 512,
                         lambda gg=g, gr=grp: diag_k[:, gg * 1024 + gr * 512:
                                                     gg * 1024 + gr * 512 + 512], na)
                for tl in range(4):
                    par = tl % 2
                    if nA_evac[par] is not None:
                        w.wait("t", "A", nA_evac[par])
                    npe = proj16(
                        ps_p[:, par * 512:par * 512 + 512],
                        lambda i, t_=tl: ln1t_v[:, i:i + 1, t_ * 128:t_ * 128 + 128],
                        lambda i: wvt_s[:, i * 512:i * 512 + 512])
                    w.wait("a", "PE", npe)
                    na = w.op("a", lambda e, t_=tl, gr=grp, p=par: e.activation(
                        out=diag_v[:, (gr * 4 + t_) * 512:(gr * 4 + t_) * 512 + 512],
                        in_=ps_p[:, p * 512:p * 512 + 512], func=AF.Identity), "A")
                    nA_evac[par] = na
                prev_grp_pe = npe
            if debug:
                w.wait("g", "V", w.n["V"])
                w.wait("g", "A", w.n["A"])
                for dst_, src_ in ((dbg_tabc, tab_c), (dbg_tabs, tab_s)):
                    w.op("g", lambda e, d=dst_, s=src_: e.dma_start(
                        out=d[:, :], in_=s[:]), "G", 16)
                w.op("g", lambda e: e.dma_start(out=dbg_q[:, :], in_=q_all[:]), "G", 16)
                w.op("g", lambda e: e.dma_start(out=dbg_dk[:, :], in_=diag_k[:]), "G", 16)
                w.op("g", lambda e: e.dma_start(out=dbg_dv[:, :], in_=diag_v[:]), "G", 16)
                w.op("g", lambda e: e.dma_start(out=dbg_kvk[:, :, :], in_=kvk_in[:, :, :]), "G", 16)
                w.op("g", lambda e: e.dma_start(out=dbg_kvv[:, :, :], in_=kvv_in[:, :, :]), "G", 16)
            w.fence()

            with nc.Block() as BA:
                @BA.sync
                def _(e):
                    for f in w.cmds[0]["q"]:
                        f(e)

                @BA.gpsimd
                def _(e):
                    for f in w.cmds[0]["g"]:
                        f(e)

                @BA.vector
                def _(e):
                    for f in w.cmds[0]["v"]:
                        f(e)

                @BA.scalar
                def _(e):
                    for f in w.cmds[0]["a"]:
                        f(e)

                @BA.tensor
                def _(e):
                    for f in w.cmds[0]["t"]:
                        f(e)

        with contextlib.ExitStack() as sb:
            kb = sb.enter_context(nc.sbuf_tensor("kb", [128, 1024], BF16))
            vb = sb.enter_context(nc.sbuf_tensor("vb", [128, 1024], BF16))
            expst = sb.enter_context(nc.sbuf_tensor("expst", [128, 2048], BF16))
            attn = sb.enter_context(nc.sbuf_tensor("attn", [128, 16 * 1024], BF16))
            rcp = sb.enter_context(nc.sbuf_tensor("rcp", [128, 1024], F32))
            wot_col = sb.enter_context(nc.sbuf_tensor("wot_col", [128, 16 * 512], BF16))
            wfq_col = sb.enter_context(nc.sbuf_tensor("wfq_col", [128, 16 * 512], BF16))
            xqt = sb.enter_context(nc.sbuf_tensor("xqt", [128, JT * 2048], BF16))
            wq_stage = xqt[:, 0:D]
            aout_stg = None
            ki_b = sb.enter_context(nc.sbuf_tensor("ki_b", [128, D], I32))
            sc2 = [sb.enter_context(nc.psum_tensor(f"sc2_{i}", [128, 1024], F32))
                   for i in range(2)]
            acc2 = sb.enter_context(nc.psum_tensor("acc2", [128, 1024], F32))
            sums2 = sb.enter_context(nc.psum_tensor("sums2", [128, 1024], F32))
            aout_stg = rcp

            w.blk = 1
            kvk_v = kvk_ag.rearrange("r g p t -> r p g t")
            kb_v = kb.rearrange("p (c g t) -> p c g t", c=4, g=2)
            q_v = q_all.rearrange("p (h f) -> p h f", h=16)
            attn_v = attn.rearrange("p (h f) -> p h f", h=16)

            wst = {"ic1": 0, "ic2": 0, "v_done": None, "a_done": None}

            def wprep1():
                ic = wst["ic1"]; wst["ic1"] += 1
                if wst["v_done"] is not None:
                    w.wait("g", "V", wst["v_done"])
                nq = w.op("g", lambda e, i=ic: e.dma_start(
                    out=pool[0][:], in_=wft[i * 128:i * 128 + 128, :]), "G", 16)
                w.wait("v", "G", nq)
                nv = w.op("v", lambda e: e.tensor_reduce(
                    out=stx(I_T1), in_=pool[0][:], axis=AX.X, op=ALU.add,
                    apply_absolute_value=True), "V")
                vdrain()
                if ic == 0:
                    nv = w.op("v", lambda e: e.tensor_copy(out=wsum[:], in_=stx(I_T1)), "V")
                else:
                    nv = v_tt(wsum[:], wsum[:], stx(I_T1), ALU.add, "V")
                vdrain()
                wst["v_done"] = nv

            def wprep_scale():
                w.wait("g", "V", wst["v_done"])
                nq = w.op("g", lambda e: e.dma_start(out=wrow[:, :], in_=wsum[:, :]), "G", 16)
                w.wait("v", "G", nq)
                nvr = w.op("v", lambda e: e.tensor_reduce(
                    out=one1[0:1, 0:1], in_=wrow[:, :], axis=AX.X, op=ALU.add), "V")
                vdrain()
                w.wait("g", "V", nvr)
                nq2 = w.op("g", lambda e: e.dma_start(
                    out=sc_d[0:1, 4:5], in_=one1[0:1, 0:1]), "G", 16)
                w.wait("g", "G", nq2)
                nq3 = w.op("g", lambda e: e.dma_start(
                    out=scal[:, 4:5],
                    in_=sc_d[0:1, 4:5].partition_broadcast(128)), "G", 16)
                w.wait("v", "G", nq3)
                w.op("v", lambda e: e.tensor_scalar(
                    out=scal[:, 0:1], in0=scal[:, 4:5],
                    scalar1=1.0 / 4194304.0, scalar2=1e-5,
                    op0=ALU.mult, op1=ALU.max))
                vdrain()
                nv3 = w.op("v", lambda e: e.tensor_scalar(
                    out=scal[:, 1:2], in0=scal[:, 0:1], scalar1=0.5,
                    scalar2=None, op0=ALU.mult), "V")
                vdrain()
                wst["v_done"] = nv3
                return nv3

            def wprep2(nq_tau):
                ic = wst["ic2"]; wst["ic2"] += 1
                if wst["v_done"] is not None:
                    w.wait("q", "V", wst["v_done"])
                if wst["a_done"] is not None:
                    w.wait("q", "A", wst["a_done"])
                nq = w.op("q", lambda e, i=ic: e.dma_start(
                    out=pool[0][:], in_=wft[i * 128:i * 128 + 128, :]), "Q", 16)
                w.wait("a", "Q", nq)
                na0 = w.op("a", lambda e: e.activation(
                    out=sq_s[:], in_=pool[0][:], func=AF.Abs), "A")
                na = w.op("a", lambda e: e.activation(
                    out=pool[1][:], in_=pool[0][:], func=AF.Sign), "A")
                wst["a_done"] = na
                w.wait("v", "A", na)
                v_ts(sq_s[:], sq_s[:], scal[:, 1:2], op0=ALU.is_ge)
                nv2 = v_tt(wq_stage, sq_s[:], pool[1][:], ALU.mult, "V")
                wst["v_done"] = nv2
                w.wait("g", "V", nv2)
                return w.op("g", lambda e, i=ic: e.dma_start(
                    out=wfq_d[i:i + 1, :, :], in_=wq_stage), "G", 16)

            SKIPW = bool(os.environ.get("SKIPW"))
            JMAX = int(os.environ.get("JMAX", "8"))
            SKIP456 = bool(os.environ.get("SKIP456"))
            ss = 0
            li = 0
            nA_exp = {}
            nPE_sc = {}
            nPE_av = {}
            li_slot = {}
            load_q = {}
            nV_norm = None
            nq_tau = None
            ng_wfq = 0
            for j in range(min(JT, JMAX)):
                if SKIPW:
                    pass
                elif j <= 3:
                    for _ in range(4):
                        wprep1()
                elif j == 4:
                    nq_tau = wprep_scale()
                else:
                    for _ in range(5 if j < 7 else 6):
                        ng_wfq = wprep2(nq_tau)
                for p in range(2):
                    g0 = 2 * p
                    pend = None

                    def emit_av(info):
                        (pdiag, pj, pg0, ppar, plp, pdeps, pfirst, plast,
                         pss) = info
                        if pfirst and nV_norm is not None:
                            w.wait("t", "V", nV_norm)
                        for i in range(2):
                            w.wait("t", pdeps[i][0], pdeps[i][1])
                            g = pg0 + i
                            if pdiag:
                                vl = diag_v[:, pj * 512 + g * 128:
                                            pj * 512 + g * 128 + 128]
                            else:
                                vl = vb[:, plp * 256 + i * 128:
                                        plp * 256 + i * 128 + 128]
                            w.op("t", lambda e, vv=vl, pp=ppar, ii=i, f=pfirst,
                                 la=plast: e.matmul(
                                out=acc2[:, ii * 512:ii * 512 + 512], lhsT=vv,
                                rhs=expst[:, pp * 1024 + ii * 512:
                                          pp * 1024 + ii * 512 + 512],
                                start=f, stop=la), None)
                            nPE_av[pss] = w.op("t", lambda e, pp=ppar, ii=i,
                                               f=pfirst, la=plast: e.matmul(
                                out=sums2[:, ii * 512:ii * 512 + 512],
                                lhsT=ones_bf[:],
                                rhs=expst[:, pp * 1024 + ii * 512:
                                          pp * 1024 + ii * 512 + 512],
                                start=f, stop=la),
                                "PE" if i == 1 else None)

                    for s in range(L[j]):
                        diag = (s == 0)
                        par = ss % 2
                        if diag:
                            cur_li = None
                            lp = None
                        else:
                            kt = s - 1
                            r, loc = kt // 8, kt % 8
                            lpar = li % 4
                            if li - 4 >= 0:
                                w.wait("q", "PE", nPE_av.get(li_slot[li - 4], 0))
                            w.wait("q", "P", nP_ag)
                            w.op("q", lambda e, rr=r, lo=loc, gg=g0, lpx=lpar: e.dma_start(
                                out=kb_v[:, lpx:lpx + 1, :, :],
                                in_=kvk_v[rr:rr + 1, :, gg:gg + 2,
                                          lo * 128:lo * 128 + 128]), f"K{lpar}", 16)
                            nqv = w.op("q", lambda e, rr=r, lo=loc, gg=g0, lpx=lpar: e.dma_start(
                                out=vb[:, lpx * 256:lpx * 256 + 256],
                                in_=kvv_ag[rr:rr + 1, lo:lo + 1, :,
                                           gg * 128:gg * 128 + 256]), f"K{lpar}", 16)
                            load_q[li] = nqv
                            li_slot[li] = ss
                            cur_li = li
                            lp = lpar
                            li += 1
                        if not diag:
                            w.wait("t", f"K{lp}", load_q[cur_li])
                        if ss >= 2:
                            w.wait("t", "A", nA_exp[(ss - 2, 1)])
                        sc_ms = {}
                        for i in range(2):
                            g = g0 + i
                            if diag:
                                kl = diag_k[:, g * 1024 + j * 128:
                                            g * 1024 + j * 128 + 128]
                            else:
                                kl = kb[:, lp * 256 + i * 128: lp * 256 + i * 128 + 128]
                            sc_ms[i] = w.op("t", lambda e, kk=kl, gg=g, pp=par, ii=i, jj=j: e.matmul(
                                out=sc2[pp][:, ii * 512:ii * 512 + 512],
                                lhsT=kk,
                                rhs=q_v[:, 4 * gg:4 * gg + 4, jj * 128:jj * 128 + 128],
                                start=True, stop=True), "PE")
                        if pend is not None:
                            emit_av(pend)
                        bidx = BOFF[j] + s
                        deps = []
                        for i in range(2):
                            w.wait("a", "PE", sc_ms[i])
                            if ss >= 2:
                                w.wait("a", "PE", nPE_av.get(ss - 2, 0))
                            nA_exp[(ss, i)] = w.op("a", lambda e, pp=par, bi=bidx, ii=i: e.activation(
                                out=expst[:, pp * 1024 + ii * 512:
                                          pp * 1024 + ii * 512 + 512],
                                in_=sc2[pp][:, ii * 512:ii * 512 + 512],
                                func=AF.Exp, scale=epsb[:, 2:3],
                                bias=btab_b[:, bi:bi + 1]), "A")
                            if diag:
                                w.wait("v", "A", nA_exp[(ss, i)])
                                nvm = w.op("v", lambda e, pp=par, ii=i: e.tensor_tensor(
                                    out=expst[:, pp * 1024 + ii * 512:
                                              pp * 1024 + ii * 512 + 512],
                                    in0=expst[:, pp * 1024 + ii * 512:
                                              pp * 1024 + ii * 512 + 512],
                                    in1=caus_s[:, 0:512], op=ALU.mult), "V")
                                deps.append(("V", nvm))
                            else:
                                deps.append(("A", nA_exp[(ss, i)]))
                        pend = (diag, j, g0, par, lp, deps,
                                s == 0, s == L[j] - 1, ss)
                        ss += 1
                    emit_av(pend)
                    pend = None
                    w.wait("v", "PE", nPE_av[ss - 1])
                    w.op("v", lambda e: e.reciprocal(out=rcp[:], in_=sums2[:]))
                    vdrain()
                    nV_norm = w.op("v", lambda e, pp=p, jj=j: e.tensor_tensor(
                        out=attn_v[:, 8 * pp:8 * pp + 8, jj * 128:jj * 128 + 128],
                        in0=acc2[:], in1=rcp[:], op=ALU.mult), "V")

            w.fence()

            if SKIP456:
                for j in range(JT):
                    nv0 = w.op("v", lambda e: e.memset(pool[0][:], 0.0), "V")
                    w.wait("q", "V", nv0)
                    w.op("q", lambda e, jj=j: e.dma_start(
                        out=out[jj * 128:jj * 128 + 128, :], in_=pool[0][:]), "Q", 16)
                w.wait("q", "Q", w.n["Q"])
                w.wait("q", "K", w.n["K"])
                w.wait("g", "G", w.n["G"])
                w.wait("g", "S", w.n["S"])
                w.wait("g", "O", w.n["O"])
            _skip456 = SKIP456
            nA_stg = {0: None, 1: None}
            if _skip456:
                JT_RANGE = []
            else:
                JT_RANGE = list(range(JT))
            nQ_stg = {0: 0, 1: 0}
            npe_ob = None
            for ob in (range(4) if not _skip456 else []):
                if npe_ob is not None:
                    w.wait("g", "PE", npe_ob)
                ncol = w.op("g", lambda e, o=ob: e.dma_start(
                    out=wot_col[:],
                    in_=wot.rearrange("(ic p) o -> p ic o", p=128)
                    [:, :, o * 512:o * 512 + 512]), "C", 16)
                for j in range(JT):
                    par = j % 2
                    w.wait("t", "C", ncol)
                    if nA_stg[par] is not None:
                        w.wait("t", "A", nA_stg[par])
                    npe_ob = proj16(
                        acc2[:, par * 512:par * 512 + 512],
                        lambda i, jj=j: attn_v[:, i:i + 1, jj * 128:jj * 128 + 128],
                        lambda i: wot_col[:, i * 512:i * 512 + 512])
                    w.wait("a", "PE", npe_ob)
                    w.wait("a", f"T{par}", nQ_stg[par])
                    na = w.op("a", lambda e, pp=par: e.activation(
                        out=aout_stg[:, pp * 512:pp * 512 + 512],
                        in_=acc2[:, pp * 512:pp * 512 + 512], func=AF.Identity), "A")
                    nA_stg[par] = na
                    w.wait("q", "A", na)
                    nQ_stg[par] = w.op("q", lambda e, jj=j, o=ob, pp=par: e.dma_start(
                        out=ao_d[jj * 128:jj * 128 + 128, o * 512:o * 512 + 512],
                        in_=aout_stg[:, pp * 512:pp * 512 + 512]), f"T{par}", 16)
            prev_v = None
            for j in JT_RANGE:
                if prev_v is not None:
                    w.wait("q", "V", prev_v)
                w.wait("q", "T0", nQ_stg[0])
                w.wait("q", "T1", nQ_stg[1])
                nq1 = w.op("q", lambda e, jj=j: e.dma_start(
                    out=pool[0][:], in_=ao_d[jj * 128:jj * 128 + 128, :]), "Q", 16)
                nq2 = w.op("q", lambda e, jj=j: e.dma_start(
                    out=pool[1][:], in_=xq[jj * 128:jj * 128 + 128, :]), "Q", 16)
                ln_stats(pool[0][:], epsb[:, 0:1], nq1)
                v_ts(sq_s[:], pool[0][:], stx(I_NM), stx(I_RI), ALU.add, ALU.mult)
                w.wait("v", "Q", nq2)
                nv = v_tt(pool[0][:], sq_s[:], pool[1][:], ALU.add, "V")
                w.wait("q", "V", nv)
                w.op("q", lambda e, jj=j: e.dma_start(
                    out=h1_d[jj * 128:jj * 128 + 128, :], in_=pool[0][:]), "Q", 16)
                ln_stats(pool[0][:], epsb[:, 0:1], nv, rsem="V")
                nv_f = v_ts(pool[1][:], pool[0][:], stx(I_NM), stx(I_RI),
                            ALU.add, ALU.mult, sem="V")
                w.wait("a", "V", nv_f)
                na = w.op("a", lambda e: e.activation(
                    out=sq_s[:], in_=pool[1][:], func=AF.Square,
                    accum_out=stx(I_T1)), "A")
                w.wait("v", "A", na)
                v_ts(stx(I_T2), stx(I_T1), 1.0 / 2048.0)
                vdrain()
                nv2 = w.op("v", lambda e: e.tensor_copy(out=stx(I_T3), in_=stx(I_T2)), "V")
                vdrain()
                w.wait("a", "V", nv2)
                na2 = w.op("a", lambda e: e.activation(
                    out=stx(I_SD), in_=stx(I_T3), func=AF.Sqrt,
                    bias=epsb[:, 1:2]), "A")
                w.wait("v", "A", na2)
                w.op("v", lambda e: e.reciprocal(out=stx(I_RI2), in_=stx(I_SD)))
                vdrain()
                v_stt(pool[1][:], pool[1][:], stx(I_RI2), gam_b[:], ALU.mult, ALU.mult)
                nv3 = w.op("v", lambda e: e.tensor_reduce(
                    out=stx(I_T1), in_=pool[1][:], axis=AX.X, op=ALU.max,
                    apply_absolute_value=True), "V")
                vdrain()
                if j == 0:
                    nv4 = w.op("v", lambda e: e.tensor_copy(out=stx(I_RMX), in_=stx(I_T1)), "V")
                else:
                    nv4 = v_tt(stx(I_RMX), stx(I_RMX), stx(I_T1), ALU.max, "V")
                vdrain()
                prev_v = nv4
                w.wait("q", "V", nv4)
                w.op("q", lambda e, jj=j: e.dma_start(
                    out=xn_d[jj * 128:jj * 128 + 128, :], in_=pool[1][:]), "Q", 16)

            if not _skip456:
                w.wait("q", "V", prev_v)
                nqm = w.op("q", lambda e: e.dma_start(
                    out=wrow[:, :], in_=st8[:, I_RMX:I_RMX + 1]), "Q", 16)
                w.wait("v", "Q", nqm)
                nvm = w.op("v", lambda e: e.tensor_reduce(
                    out=one1[0:1, 4:5], in_=wrow[:, :], axis=AX.X, op=ALU.max), "V")
                vdrain()
                w.wait("q", "V", nvm)
                nqm2 = w.op("q", lambda e: e.dma_start(
                    out=mx_in[0:1, 0:1], in_=one1[0:1, 4:5]), "Q", 16)
                w.wait("g", "Q", nqm2)
                nP_ar = w.op("g", lambda e: e.collective_compute(
                    "AllReduce", ALU.max, replica_groups=[list(range(NC))],
                    ins=[mx_in], outs=[mx_out]), "P")
                w.wait("q", "P", nP_ar)
                nqm3 = w.op("q", lambda e: e.dma_start(
                    out=scal[:, 5:6],
                    in_=mx_out[0:1, 0:1].partition_broadcast(128)), "Q", 16)
                w.wait("v", "Q", nqm3)
                w.op("v", lambda e: e.tensor_scalar(
                    out=scal[:, 6:7], in0=scal[:, 5:6], scalar1=1e-5,
                    scalar2=None, op0=ALU.max))
                vdrain()
                w.op("v", lambda e: e.reciprocal(out=scal[:, 7:8], in_=scal[:, 6:7]))
                vdrain()
                w.op("v", lambda e: e.tensor_scalar(
                    out=scal[:, 2:3], in0=scal[:, 7:8], scalar1=127.0,
                    scalar2=None, op0=ALU.mult))
                w.op("v", lambda e: e.tensor_scalar(
                    out=scal[:, 8:9], in0=scal[:, 6:7], scalar1=INV127,
                    scalar2=None, op0=ALU.mult))
                vdrain()
                nq_sc = w.op("v", lambda e: e.tensor_tensor(
                    out=scal[:, 3:4], in0=scal[:, 8:9], in1=scal[:, 0:1],
                    op=ALU.mult), "V")
                vdrain()

                prev_v = None
                prev_evac = None
                for j in range(JT):
                    if prev_v is not None:
                        w.wait("q", "V", prev_v)
                    nq1 = w.op("q", lambda e, jj=j: e.dma_start(
                        out=pool[0][:], in_=xn_d[jj * 128:jj * 128 + 128, :]), "Q", 16)
                    w.wait("v", "Q", nq1)
                    w.wait("v", "V", nq_sc) if False else None
                    v_ts(pool[1][:], pool[0][:], scal[:, 2:3])
                    v_ts(pool[1][:], pool[1][:], -128.0, 127.0, ALU.max, ALU.min)
                    w.op("v", lambda e: e.tensor_copy(out=ki_b[:], in_=pool[1][:]))
                    w.op("v", lambda e: e.tensor_copy(out=sq_s[:], in_=ki_b[:]))
                    v_tt(pool[1][:], pool[1][:], sq_s[:], ALU.subtract)
                    v_ts(pool[0][:], pool[1][:], 0.5, op0=ALU.is_ge)
                    v_ts(pool[1][:], pool[1][:], -0.5, op0=ALU.is_lt)
                    v_tt(pool[0][:], pool[0][:], pool[1][:], ALU.subtract)
                    nv = v_tt(pool[1][:], sq_s[:], pool[0][:], ALU.add, "V")
                    w.wait("t", "V", nv)
                    if prev_evac is not None:
                        w.wait("t", "V", prev_evac)
                    npe_h = {}
                    for half in range(2):
                        for ic in range(8):
                            icg = half * 8 + ic
                            npe_h[half] = w.op("t", lambda e, i=icg, hh=half, ii=ic: e.transpose(
                                out=sc2[hh][:, ii * 128:ii * 128 + 128],
                                in_=pool[1][:, i * 128:i * 128 + 128],
                                identity=id_fs[:]), "PE" if ic == 7 else None)
                    nevs = []
                    w.wait("v", "G", ng_wfq)
                    for half in range(2):
                        w.wait("v", "PE", npe_h[half])
                        nevs.append(w.op("v", lambda e, jj=j, hh=half: e.tensor_copy(
                            out=xqt[:, jj * 2048 + hh * 1024: jj * 2048 + hh * 1024 + 1024],
                            in_=sc2[hh][:]), "V"))
                    prev_evac = nevs[-1]
                    prev_v = nevs[-1]

                nA_stg = {0: None, 1: None}
                nQ_stg = {0: w.n["T0"], 1: w.n["T1"]}
                npe_ob = None
                for ob in range(4):
                    if npe_ob is not None:
                        w.wait("q", "PE", npe_ob)
                    w.wait("q", "G", ng_wfq)
                    ncol = w.op("q", lambda e, o=ob: e.dma_start(
                        out=wfq_col[:],
                        in_=wfq_d.rearrange("ic p o -> p ic o")
                        [:, :, o * 512:o * 512 + 512]), "C", 16)
                    for j in range(JT):
                        par = j % 2
                        w.wait("t", "C", ncol)
                        w.wait("t", "V", prev_evac)
                        if nA_stg[par] is not None:
                            w.wait("t", "A", nA_stg[par])
                        npe_ob = proj16(
                            acc2[:, par * 512:par * 512 + 512],
                            lambda i, jj=j: xqt[:, jj * 2048 + i * 128:
                                               jj * 2048 + i * 128 + 128],
                            lambda i: wfq_col[:, i * 512:i * 512 + 512])
                        w.wait("a", "PE", npe_ob)
                        w.wait("a", "Q", nQ_stg[par])
                        w.wait("a", "V", nq_sc)
                        na = w.op("a", lambda e, pp=par: e.activation(
                            out=aout_stg[:, pp * 512:pp * 512 + 512],
                            in_=acc2[:, pp * 512:pp * 512 + 512], func=AF.Identity,
                            scale=scal[:, 3:4]), "A")
                        nA_stg[par] = na
                        w.wait("q", "A", na)
                        nQ_stg[par] = w.op("q", lambda e, jj=j, o=ob, pp=par: e.dma_start(
                            out=fo_d[jj * 128:jj * 128 + 128, o * 512:o * 512 + 512],
                            in_=aout_stg[:, pp * 512:pp * 512 + 512]), "Q", 16)
                prev_v = None
                for j in range(JT):
                    if prev_v is not None:
                        w.wait("q", "V", prev_v)
                    w.wait("q", "Q", nQ_stg[0])
                    w.wait("q", "Q", nQ_stg[1])
                    nq1 = w.op("q", lambda e, jj=j: e.dma_start(
                        out=pool[0][:], in_=fo_d[jj * 128:jj * 128 + 128, :]), "Q", 16)
                    nq2 = w.op("q", lambda e, jj=j: e.dma_start(
                        out=pool[1][:], in_=h1_d[jj * 128:jj * 128 + 128, :]), "Q", 16)
                    ln_stats(pool[0][:], epsb[:, 0:1], nq1)
                    v_ts(sq_s[:], pool[0][:], stx(I_NM), stx(I_RI), ALU.add, ALU.mult)
                    w.wait("v", "Q", nq2)
                    nv = v_tt(pool[0][:], sq_s[:], pool[1][:], ALU.add, "V")
                    prev_v = nv
                    w.wait("q", "V", nv)
                    w.op("q", lambda e, jj=j: e.dma_start(
                        out=out[jj * 128:jj * 128 + 128, :], in_=pool[0][:]), "Q", 16)

                if debug:
                    w.wait("g", "V", w.n["V"])
                    w.wait("g", "A", w.n["A"])
                    w.wait("g", "Q", w.n["Q"])
                    w.op("g", lambda e: e.dma_start(out=dbg_attn[:, :], in_=attn[:]), "G", 16)
                    w.op("g", lambda e: e.dma_start(out=dbg_scal[:, :], in_=scal[:]), "G", 16)
                    w.op("g", lambda e: e.dma_start(out=dbg_one[:, :], in_=one1[:]), "G", 16)
                    w.op("g", lambda e: e.dma_start(out=dbg_xqt[:, :], in_=xqt[:]), "G", 16)
                    w.op("g", lambda e: e.dma_start(out=dbg_wfq[:, :, :], in_=wfq_d[:, :, :]), "G", 16)
            w.wait("q", "Q", w.n["Q"])
            for s_ in ("K0", "K1", "K2", "K3", "T0", "T1"):
                w.wait("q", s_, w.n[s_])
            w.wait("q", "C", w.n["C"])
            w.wait("g", "G", w.n["G"])
            for s_ in ("C0", "C1", "S0", "S1"):
                w.wait("g", s_, w.n[s_])
            w.wait("g", "O", w.n["O"])

            with nc.Block() as BB:
                @BB.sync
                def _(e):
                    for f in w.cmds[1]["q"]:
                        f(e)

                @BB.gpsimd
                def _(e):
                    for f in w.cmds[1]["g"]:
                        f(e)

                @BB.vector
                def _(e):
                    for f in w.cmds[1]["v"]:
                        f(e)

                @BB.scalar
                def _(e):
                    for f in w.cmds[1]["a"]:
                        f(e)

                @BB.tensor
                def _(e):
                    for f in w.cmds[1]["t"]:
                        f(e)
    return nc


def kernel(L0_attn_x, L0_attn_Wq, L0_attn_Wk, L0_attn_Wv, L0_attn_Wo,
           L0_ffn_weight, L0_ffn_norm_weight):
    x = np.ascontiguousarray(np.asarray(L0_attn_x, dtype=np.float32).reshape(S, D))
    Wq = np.asarray(L0_attn_Wq, dtype=np.float32)
    Wk = np.asarray(L0_attn_Wk, dtype=np.float32)
    Wv = np.asarray(L0_attn_Wv, dtype=np.float32)
    Wo = np.asarray(L0_attn_Wo, dtype=np.float32)
    Wf = np.asarray(L0_ffn_weight, dtype=np.float32)
    gam = np.ascontiguousarray(np.asarray(L0_ffn_norm_weight, dtype=np.float32))

    perm = np.concatenate([np.arange(0, 128, 2), np.arange(1, 128, 2)])
    Wq_p = Wq.reshape(16, 128, D)[:, perm, :].reshape(D, D)
    Wk_p = Wk.reshape(4, 128, D)[:, perm, :].reshape(512, D)
    wqt = np.ascontiguousarray(Wq_p.T)
    wkt = np.ascontiguousarray(Wk_p.T)
    wvt = np.ascontiguousarray(Wv.T)
    wot = np.ascontiguousarray(Wo.T)
    wft = np.ascontiguousarray(Wf.T)

    nc = build(debug=False)
    in_maps = []
    for c in range(NC):
        tiles = [8 * j + c for j in range(JT)]
        xq_c = np.ascontiguousarray(x.reshape(64, 128, D)[tiles].reshape(TPC, D))
        xs_c = np.ascontiguousarray(x[c * TPC:(c + 1) * TPC])
        tvals = np.zeros((1, 2048), dtype=np.float32)
        tvals[0, :1024] = np.arange(c * TPC, (c + 1) * TPC, dtype=np.float32)
        own_tok = np.concatenate([np.arange(t * 128, t * 128 + 128) for t in tiles])
        tvals[0, 1024:] = own_tok.astype(np.float32)
        bt = np.zeros((1, NSLOT), dtype=np.float32)
        for j in range(JT):
            t_own = 8 * j + c
            for s_ in range(1, L[j]):
                if s_ - 1 >= t_own:
                    bt[0, BOFF[j] + s_] = BKILL
        in_maps.append({
            "xq": xq_c, "xs": xs_c, "wqt": wqt, "wkt": wkt, "wvt": wvt,
            "wot": wot, "wft": wft, "gam": gam, "tidx": tvals, "btab": bt,
        })
    res = run_bass_kernel_spmd(nc, in_maps, core_ids=list(range(NC)))
    full = np.empty((S, D), dtype=np.float32)
    for c in range(NC):
        o = res.results[c]["out"]
        for j in range(JT):
            t = 8 * j + c
            full[t * 128:(t + 1) * 128] = o[j * 128:(j + 1) * 128]
    return full.reshape(1, S, D)


# version 06
# speedup vs baseline: 1.0884x; 1.0872x over previous
import contextlib
import math
import numpy as np
import ml_dtypes
import concourse.bass as bass
import concourse.mybir as mybir
from concourse.bass_utils import run_bass_kernel_spmd

F32 = mybir.dt.float32
BF16 = mybir.dt.bfloat16
I32 = mybir.dt.int32
AF = mybir.ActivationFunctionType
ALU = mybir.AluOpType
AX = mybir.AxisListType

S, D = 8192, 2048
NC, JT = 8, 8
TPC = JT * 128
L = [8 * j + 8 for j in range(JT)]
BOFF = [sum(L[:j]) for j in range(JT)]
NSLOT = sum(L)

TWO_PI = 2 * math.pi
C1 = float(np.float32(int(TWO_PI * 2 ** 10) / 2 ** 10))
C2 = float(np.float32(TWO_PI - C1))
C3 = float(np.float32(TWO_PI - C1 - float(np.float32(TWO_PI - C1))))
INV2PI = float(np.float32(1.0 / TWO_PI))
BOUND = 3.141590
PI2_HI = float(np.float32(math.pi / 2))
PI2_LO = float(np.float32(math.pi / 2 - float(np.float32(math.pi / 2))))
SC64 = 1.0 / math.sqrt(128.0)
SC_HI = float(np.float32(SC64))
SC_LO = float(np.float32(SC64 / SC_HI))
BKILL = -30000.0
INV127 = float(np.float32(1.0 / 127.0))

_ID_BF = np.eye(128, dtype=ml_dtypes.bfloat16)
_ID_F32 = np.eye(128, dtype=np.float32)
_CAUSAL = np.tile((np.arange(128)[:, None] <= np.arange(128)[None, :])
                  .astype(ml_dtypes.bfloat16), (1, 8))


import os
_SEMALIAS = {"K": "Q"}
for _kv in os.environ.get("SEMALIAS", "").split(","):
    if ":" in _kv:
        _a, _b = _kv.split(":")
        _SEMALIAS[_a] = _b


class _W:
    ENG = "qgvat"

    def __init__(self):
        self.cmds = {0: {e: [] for e in self.ENG}, 1: {e: [] for e in self.ENG}}
        self.blk = 0
        self.n = {k: 0 for k in ("Q", "G", "PE", "V", "A", "P", "B",
                                 "K", "C", "S", "O", "K0", "K1", "K2", "K3",
                                 "C0", "C1", "S0", "S1", "T0", "T1", "O0", "O1")}
        self.m = dict(_SEMALIAS)
        self.last_wait = {}
        self.sems = None

    def emit(self, eng, fn):
        self.cmds[self.blk][eng].append(fn)

    def wait(self, eng, sem, val):
        sem = self.m.get(sem, sem)
        if val is None or val <= 0:
            return
        key = (eng, sem)
        if self.last_wait.get(key, -1) >= val:
            return
        self.last_wait[key] = val
        self.emit(eng, lambda e, s=sem, v=val: e.wait_ge(self.sems[s], v))

    def op(self, eng, fn, sem=None, k=1):
        if sem is None:
            self.emit(eng, fn)
            return None
        sem = self.m.get(sem, sem)
        self.n[sem] += k
        v = self.n[sem]
        self.emit(eng, lambda e, s=sem, kk=k: fn(e).then_inc(self.sems[s], kk))
        return v

    def dump(self, src_ap, dst_ap):
        self.wait("g", "V", self.n["V"])
        self.wait("g", "A", self.n["A"])
        self.wait("g", "PE", self.n["PE"])
        self.wait("g", "Q", self.n["Q"])
        ng = self.op("g", lambda e: e.dma_start(out=dst_ap, in_=src_ap), "G", 16)
        for e in "qvat":
            self.wait(e, "G", ng)
        return ng

    def fence(self):
        self.n["B"] += 5
        tgt = self.n["B"]
        for e in self.ENG:
            self.emit(e, lambda en: en.nop().then_inc(self.sems["B"], 1))
        for e in self.ENG:
            self.emit(e, lambda en, t=tgt: en.wait_ge(self.sems["B"], t))
        return tgt


def build(debug=False):
    w = _W()
    nc = bass.Bass("TRN2", target_bir_lowering=False, debug=False)
    dbg_kind = "ExternalOutput" if debug else "Internal"

    xq = nc.dram_tensor("xq", [TPC, D], F32, kind="ExternalInput").ap()
    xs = nc.dram_tensor("xs", [TPC, D], F32, kind="ExternalInput").ap()
    wqt = nc.dram_tensor("wqt", [D, D], F32, kind="ExternalInput").ap()
    wkt = nc.dram_tensor("wkt", [D, 512], F32, kind="ExternalInput").ap()
    wvt = nc.dram_tensor("wvt", [D, 512], F32, kind="ExternalInput").ap()
    wot = nc.dram_tensor("wot", [D, D], F32, kind="ExternalInput").ap()
    wft = nc.dram_tensor("wft", [D, D], F32, kind="ExternalInput").ap()
    gam = nc.dram_tensor("gam", [D], F32, kind="ExternalInput").ap()
    tidx = nc.dram_tensor("tidx", [1, 2048], F32, kind="ExternalInput").ap()
    btab = nc.dram_tensor("btab", [1, NSLOT], F32, kind="ExternalInput").ap()
    out = nc.dram_tensor("out", [TPC, D], F32, kind="ExternalOutput").ap()

    kvk_in = nc.dram_tensor("kvk_in", [4, 128, 1024], BF16).ap()
    kvv_in = nc.dram_tensor("kvv_in", [8, 128, 512], BF16).ap()
    kvk_ag = nc.dram_tensor("kvk_ag", [NC, 4, 128, 1024], BF16).ap()
    kvv_ag = nc.dram_tensor("kvv_ag", [NC, 8, 128, 512], BF16).ap()
    mx_in = nc.dram_tensor("mx_in", [1, 16], F32).ap()
    mx_out = nc.dram_tensor("mx_out", [1, 16], F32).ap()
    sc_d = nc.dram_tensor("sc_d", [1, 16], F32).ap()
    h1_d = nc.dram_tensor("h1_d", [TPC, D], F32, kind=dbg_kind).ap()
    xn_d = nc.dram_tensor("xn_d", [TPC, D], F32, kind=dbg_kind).ap()
    ao_d = nc.dram_tensor("ao_d", [TPC, D], F32, kind=dbg_kind).ap()
    fo_d = nc.dram_tensor("fo_d", [TPC, D], F32, kind=dbg_kind).ap()
    wfq_d = nc.dram_tensor("wfq_d", [16, 128, D], BF16).ap()

    if debug:
        dbg_q = nc.dram_tensor("dbg_q", [128, 16 * 1024], BF16, kind="ExternalOutput").ap()
        dbg_dk = nc.dram_tensor("dbg_dk", [128, 4096], BF16, kind="ExternalOutput").ap()
        dbg_dv = nc.dram_tensor("dbg_dv", [128, 4096], BF16, kind="ExternalOutput").ap()
        dbg_attn = nc.dram_tensor("dbg_attn", [128, 16 * 1024], BF16, kind="ExternalOutput").ap()
        dbg_tabc = nc.dram_tensor("dbg_tabc", [128, 2048], BF16, kind="ExternalOutput").ap()
        dbg_tabs = nc.dram_tensor("dbg_tabs", [128, 2048], BF16, kind="ExternalOutput").ap()
        dbg_kvk = nc.dram_tensor("dbg_kvk", [4, 128, 1024], BF16, kind="ExternalOutput").ap()
        dbg_kvv = nc.dram_tensor("dbg_kvv", [8, 128, 512], BF16, kind="ExternalOutput").ap()
        dbg_ln1t = nc.dram_tensor("dbg_ln1t", [128, 8192], BF16, kind="ExternalOutput").ap()
        dbg_scal = nc.dram_tensor("dbg_scal", [128, 16], F32, kind="ExternalOutput").ap()
        dbg_one = nc.dram_tensor("dbg_one", [1, 16], F32, kind="ExternalOutput").ap()
        dbg_wfq = nc.dram_tensor("dbg_wfq", [16, 128, D], BF16, kind="ExternalOutput").ap()
        dbg_xqt = nc.dram_tensor("dbg_xqt", [128, JT * 2048], BF16, kind="ExternalOutput").ap()
        dbg_fi = nc.dram_tensor("dbg_fi", [128, 1], F32, kind="ExternalOutput").ap()
        dbg_ang = nc.dram_tensor("dbg_ang", [128, 1024], F32, kind="ExternalOutput").ap()
        dbg_sin = nc.dram_tensor("dbg_sin", [128, 1024], F32, kind="ExternalOutput").ap()
        dbg_lnr = nc.dram_tensor("dbg_lnr", [128, D], BF16, kind="ExternalOutput").ap()
        dbg_st8 = nc.dram_tensor("dbg_st8", [128, 16], F32, kind="ExternalOutput").ap()
        dbg_wrow = nc.dram_tensor("dbg_wrow", [1, 128], F32, kind="ExternalOutput").ap()
        dbg_tokb = nc.dram_tensor("dbg_tokb", [128, 1024], F32, kind="ExternalOutput").ap()
        dbg_red = nc.dram_tensor("dbg_red", [128, 1024], F32, kind="ExternalOutput").ap()
        dbg_ang2 = nc.dram_tensor("dbg_ang2", [128, 1024], F32, kind="ExternalOutput").ap()
        dbg_fe = nc.dram_tensor("dbg_fe", [128, 1], F32, kind="ExternalOutput").ap()
    id_bf_d = nc.inline_tensor(_ID_BF, name="id_bf").ap()
    id_f_d = nc.inline_tensor(_ID_F32, name="id_f").ap()
    caus_d = nc.inline_tensor(np.asarray(_CAUSAL), name="caus").ap()

    with contextlib.ExitStack() as top:
        sems = {}
        for s in ("Q", "G", "PE", "V", "A", "P", "B", "K", "C", "S", "O",
                  "K0", "K1", "K2", "K3", "C0", "C1", "S0", "S1", "T0", "T1", "O0", "O1"):
            sems[s] = top.enter_context(nc.semaphore("s" + s))
        w.sems = sems

        q_all = top.enter_context(nc.sbuf_tensor("q_all", [128, 16 * 1024], BF16))
        diag_k = top.enter_context(nc.sbuf_tensor("diag_k", [128, 4 * 1024], BF16))
        diag_v = top.enter_context(nc.sbuf_tensor("diag_v", [128, JT * 512], BF16))
        gam_b = top.enter_context(nc.sbuf_tensor("gam_b", [128, D], F32))
        btab_b = top.enter_context(nc.sbuf_tensor("btab_b", [128, NSLOT], F32))
        scal = top.enter_context(nc.sbuf_tensor("scal", [128, 16], F32))
        id_bs = top.enter_context(nc.sbuf_tensor("id_bs", [128, 128], BF16))
        id_fs = top.enter_context(nc.sbuf_tensor("id_fs", [128, 128], F32))
        caus_s = top.enter_context(nc.sbuf_tensor("caus_s", [128, 1024], BF16))
        ones_bf = top.enter_context(nc.sbuf_tensor("ones_bf", [128, 128], BF16))
        pool = [top.enter_context(nc.sbuf_tensor(f"pool{i}", [128, D], F32))
                for i in range(2)]
        sq_s = top.enter_context(nc.sbuf_tensor("sq_s", [128, D], F32))
        st8 = top.enter_context(nc.sbuf_tensor("st8", [128, 16], F32))
        one1 = top.enter_context(nc.sbuf_tensor("one1", [1, 16], F32))
        wsum = top.enter_context(nc.sbuf_tensor("wsum", [128, 1], F32))
        wrow = top.enter_context(nc.sbuf_tensor("wrow", [1, 128], F32))
        epsb = top.enter_context(nc.sbuf_tensor("epsb", [128, 4], F32))

        (I_SUM, I_NM, I_VAR, I_SD, I_RI, I_SQ, I_M2, I_RMX, I_T1, I_RI2,
         I_T2, I_T3) = range(12)

        def stx(i):
            return st8[:, i:i + 1]

        def vdrain():
            w.emit("v", lambda e: e.drain())

        def adrain():
            w.emit("a", lambda e: e.drain())

        def v_ts(dst, src, s1, s2=None, op0=ALU.mult, op1=None, sem=None):
            if op1 is None:
                return w.op("v", lambda e: e.tensor_scalar(
                    out=dst, in0=src, scalar1=s1, scalar2=None, op0=op0), sem)
            return w.op("v", lambda e: e.tensor_scalar(
                out=dst, in0=src, scalar1=s1, scalar2=s2, op0=op0, op1=op1), sem)

        def v_tt(dst, a, bb, op, sem=None):
            return w.op("v", lambda e: e.tensor_tensor(out=dst, in0=a, in1=bb, op=op), sem)

        def v_stt(dst, a, s, bb, op0, op1, sem=None):
            return w.op("v", lambda e: e.scalar_tensor_tensor(
                out=dst, in0=a, scalar=s, in1=bb, op0=op0, op1=op1), sem)

        def ln_stats(px, eps, nq_ready, rsem="Q"):
            if rsem != "V":
                w.wait("v", rsem, nq_ready)
            nv1 = w.op("v", lambda e: e.tensor_reduce(
                out=stx(I_SUM), in_=px, axis=AX.X, op=ALU.add), "V")
            w.wait("a", rsem, nq_ready)
            w.wait("a", "V", nv1)
            na1 = w.op("a", lambda e: e.activation(
                out=sq_s[:], in_=px, func=AF.Square, accum_out=stx(I_SQ)), "A")
            w.wait("v", "A", na1)
            vdrain()
            v_ts(stx(I_NM), stx(I_SUM), -1.0 / 2048.0)
            vdrain()
            v_ts(stx(I_M2), stx(I_NM), stx(I_NM))
            v_ts(stx(I_VAR), stx(I_SQ), 1.0 / 2048.0)
            vdrain()
            nv2 = v_tt(stx(I_VAR), stx(I_VAR), stx(I_M2), ALU.subtract, "V")
            w.wait("a", "V", nv2)
            na2 = w.op("a", lambda e, ee=eps: e.activation(
                out=stx(I_SD), in_=stx(I_VAR), func=AF.Sqrt, bias=ee), "A")
            w.wait("v", "A", na2)
            nvr = w.op("v", lambda e: e.reciprocal(out=stx(I_RI), in_=stx(I_SD)), "V")
            vdrain()
            return nvr

        with contextlib.ExitStack() as sa:
            tab_c = sa.enter_context(nc.sbuf_tensor("tab_c", [128, 2048], BF16))
            tab_s = sa.enter_context(nc.sbuf_tensor("tab_s", [128, 2048], BF16))
            tab_cq = sa.enter_context(nc.sbuf_tensor("tab_cq", [128, 2048], BF16))
            tab_sq = sa.enter_context(nc.sbuf_tensor("tab_sq", [128, 2048], BF16))
            tokb_t = sa.enter_context(nc.sbuf_tensor("tokb_t", [128, 1024], F32))
            ang_t = sa.enter_context(nc.sbuf_tensor("ang_t", [128, 1024], F32))
            tr1_t = sa.enter_context(nc.sbuf_tensor("tr1_t", [128, 1024], F32))
            tr2_t = sa.enter_context(nc.sbuf_tensor("tr2_t", [128, 1024], F32))
            tokb = tokb_t[:, :]
            ang = ang_t[:, :]
            tr1 = tr1_t[:, :]
            tr2 = tr2_t[:, :]
            tr3 = tokb_t[:, :]
            sneg = ang_t[:, :]
            trk = sa.enter_context(nc.sbuf_tensor("trk", [128, 1024], I32))
            sin_f = sa.enter_context(nc.sbuf_tensor("sin_f", [128, 1024], F32))
            cos_f = sa.enter_context(nc.sbuf_tensor("cos_f", [128, 1024], F32))
            fi = sa.enter_context(nc.sbuf_tensor("fi", [128, 1], F32))
            fe = sa.enter_context(nc.sbuf_tensor("fe", [128, 1], F32))
            lnv = sa.enter_context(nc.sbuf_tensor("lnv", [128, 1], F32))
            tenk = sa.enter_context(nc.sbuf_tensor("tenk", [128, 1], F32))
            ln1t = sa.enter_context(nc.sbuf_tensor("ln1t", [128, 16 * 512], BF16))
            lnraw = sa.enter_context(nc.sbuf_tensor("lnraw", [128, 2 * D], BF16))
            wkt_s = sa.enter_context(nc.sbuf_tensor("wkt_s", [128, 16 * 512], BF16))
            wvt_s = sa.enter_context(nc.sbuf_tensor("wvt_s", [128, 16 * 512], BF16))
            wq_col = sa.enter_context(nc.sbuf_tensor("wq_col", [128, 2 * 2048], BF16))
            praw = sa.enter_context(nc.sbuf_tensor("praw", [128, 2 * 512], BF16))
            pswp = sa.enter_context(nc.sbuf_tensor("pswp", [128, 2 * 512], BF16))
            pr1 = sa.enter_context(nc.sbuf_tensor("pr1", [128, 512], BF16))
            pr2 = sa.enter_context(nc.sbuf_tensor("pr2", [128, 512], BF16))
            kvstg = sa.enter_context(nc.sbuf_tensor("kvstg", [128, 2 * 512], BF16))
            ps_tr = sa.enter_context(nc.psum_tensor("ps_tr", [128, 2048], BF16))
            ps_p = sa.enter_context(nc.psum_tensor("ps_p", [128, 2 * 512], F32))

            w.blk = 0
            w.op("q", lambda e: e.dma_start(out=id_fs[:], in_=id_f_d[:, :]), "Q", 16)
            w.op("q", lambda e: e.dma_start(
                out=tokb[0:64, :], in_=tidx[0:1, 0:1024].partition_broadcast(64)), "Q", 16)
            nQ_tok = w.op("q", lambda e: e.dma_start(
                out=tokb[64:128, :],
                in_=tidx[0:1, 1024:2048].partition_broadcast(64)), "Q", 16)
            w.op("q", lambda e: e.dma_start(
                out=gam_b[:], in_=gam[None, :].partition_broadcast(128)[:, 0, :]), "Q", 16)
            w.op("q", lambda e: e.dma_start(
                out=btab_b[:], in_=btab[0:1, :].partition_broadcast(128)), "Q", 16)
            nV_z = w.op("v", lambda e: e.memset(one1[0:1, 8:16], 0.0), "V")
            w.wait("q", "V", nV_z)
            w.op("q", lambda e: e.dma_start(out=mx_in[0:1, 0:8],
                                            in_=one1[0:1, 8:16]), "Q", 16)
            w.op("q", lambda e: e.dma_start(out=mx_in[0:1, 8:16],
                                            in_=one1[0:1, 8:16]), "Q", 16)
            nG_id = w.op("g", lambda e: e.dma_start(out=id_bs[:], in_=id_bf_d[:, :]), "G", 16)
            nG_caus = w.op("g", lambda e: e.dma_start(out=caus_s[:], in_=caus_d[:, :]), "G", 16)
            w.op("g", lambda e: e.dma_start(
                out=wkt_s[:], in_=wkt.rearrange("(ic p) o -> p ic o", p=128)), "G", 16)
            nG_w = w.op("g", lambda e: e.dma_start(
                out=wvt_s[:], in_=wvt.rearrange("(ic p) o -> p ic o", p=128)), "G", 16)
            w.op("v", lambda e: e.memset(ones_bf[:], 1.0), "V")
            w.op("v", lambda e: e.memset(epsb[:, 0:1], 1e-5))
            w.op("v", lambda e: e.memset(epsb[:, 1:2], 1e-6))
            nV_eps = w.op("v", lambda e: e.memset(epsb[:, 2:3], SC_HI), "V")
            w.wait("a", "V", nV_eps)

            w.op("g", lambda e: e.iota(
                out=fi[0:64, :], pattern=[[0, 1]], base=0, channel_multiplier=1,
                allow_small_or_imprecise_dtypes=True), "P")
            nP_iota = w.op("g", lambda e: e.iota(
                out=fi[64:128, :], pattern=[[0, 1]], base=0, channel_multiplier=1,
                allow_small_or_imprecise_dtypes=True), "P")
            nV_tenk = w.op("v", lambda e: e.memset(tenk[:], 10000.0), "V")
            w.wait("a", "V", nV_tenk)
            w.op("a", lambda e: e.activation(out=lnv[:], in_=tenk[:], func=AF.Ln), "A")
            adrain()
            w.wait("v", "P", nP_iota)
            nV_fe = v_ts(fe[:], fi[:], -1.0 / 64.0, sem="V")
            w.wait("a", "V", nV_fe)
            nA_fr = w.op("a", lambda e: e.activation(
                out=fi[:], in_=fe[:], func=AF.Exp, scale=lnv[:]), "A")
            w.wait("v", "Q", w.n["Q"])
            w.wait("v", "A", nA_fr)
            w.op("v", lambda e: e.tensor_copy(out=fe[:], in_=fi[:]))
            vdrain()
            nV_ang = v_ts(ang[:], tokb[:], fe[:], sem="V")
            if debug:
                w.dump(fe[:], dbg_fe[:, :])
                w.dump(fi[:], dbg_fi[:, :])
                w.dump(tokb, dbg_tokb[:, :])
                w.dump(ang, dbg_ang[:, :])
                w.dump(ang, dbg_ang2[:, :])

            def reduce_sin(src, dst):
                v_ts(tr1[:], src, INV2PI)
                vdrain()
                w.op("v", lambda e: e.tensor_copy(out=trk[:], in_=tr1[:]))
                vdrain()
                w.op("v", lambda e: e.tensor_copy(out=tr2[:], in_=trk[:]))
                vdrain()
                v_stt(tr1[:], tr2[:], -C1, src, ALU.mult, ALU.add)
                vdrain()
                v_stt(tr1[:], tr2[:], -C2, tr1[:], ALU.mult, ALU.add)
                vdrain()
                v_stt(tr1[:], tr2[:], -C3, tr1[:], ALU.mult, ALU.add)
                vdrain()
                v_ts(tr2[:], tr1[:], BOUND, op0=ALU.is_gt)
                v_ts(tr3[:], tr1[:], -BOUND, op0=ALU.is_lt)
                vdrain()
                v_tt(tr2[:], tr2[:], tr3[:], ALU.subtract)
                vdrain()
                v_stt(tr1[:], tr2[:], -C1, tr1[:], ALU.mult, ALU.add)
                vdrain()
                v_stt(tr1[:], tr2[:], -C2, tr1[:], ALU.mult, ALU.add)
                vdrain()
                nv = v_stt(tr1[:], tr2[:], -C3, tr1[:], ALU.mult, ALU.add, "V")
                vdrain()
                if debug:
                    w.dump(tr1, dbg_red[:, :])
                w.wait("a", "V", nv)
                return w.op("a", lambda e, d=dst: e.activation(
                    out=d[:], in_=tr1[:], func=AF.Sin), "A")

            vdrain()
            nA_sin = reduce_sin(ang[:], sin_f)
            if debug:
                w.wait("g", "A", nA_sin)
                w.dump(sin_f[:], dbg_sin[:, :])
            w.wait("v", "A", nA_sin)
            v_ts(ang[:], ang[:], PI2_HI, PI2_LO, ALU.add, ALU.add)
            vdrain()
            nA_cos = reduce_sin(ang[:], cos_f)
            w.wait("v", "A", nA_cos)
            v_ts(sneg[:], sin_f[:], -1.0)
            v_ts(tr1[:], cos_f[:], SC_LO)
            v_ts(tr2[:], sin_f[:], SC_LO)
            vdrain()
            nV_tab = v_ts(tr3[:], sneg[:], SC_LO, sem="V")
            vdrain()
            w.wait("g", "V", nV_tab)
            nG_tab = None
            for dst, half, srcp in ((tab_c, 0, cos_f), (tab_c, 1, cos_f),
                                    (tab_s, 0, sneg), (tab_s, 1, sin_f),
                                    (tab_cq, 0, tr1), (tab_cq, 1, tr1),
                                    (tab_sq, 0, tr3), (tab_sq, 1, tr2)):
                for th in range(2):
                    nG_tab = w.op("g", lambda e, d=dst, h=half, s=srcp, t=th:
                                  e.dma_start(
                        out=d[64 * h:64 * h + 64, t * 1024:t * 1024 + 1024],
                        in_=s[64 * t:64 * t + 64, :]), "G", 16)

            ln1t_v = ln1t.rearrange("p (ic f) -> p ic f", ic=16)
            busy = {"pv": {0: None, 1: None}, "pa": {0: None, 1: None},
                    "tr": None}

            dbg_first = {"done": False}

            def ln_transpose(src_dram, row0, gslot, npe_free):
                par = gslot % 2
                px = pool[par][:]
                lw = lnraw[:, par * D:par * D + D]
                w.wait("q", "V", busy["pv"][par])
                w.wait("q", "A", busy["pa"][par])
                nq = w.op("q", lambda e, r=row0, sd=src_dram: e.dma_start(
                    out=px, in_=sd[r:r + 128, :]), f"T{par}", 16)
                ln_stats(px, epsb[:, 0:1], nq, rsem=f"T{par}")
                busy["pa"][par] = w.n["A"]
                nv = w.op("v", lambda e, p=px, ll=lw: e.tensor_scalar(
                    out=ll, in0=p, scalar1=stx(I_NM), scalar2=stx(I_RI),
                    op0=ALU.add, op1=ALU.mult), "V")
                busy["pv"][par] = nv
                w.wait("t", "V", nv)
                w.wait("t", "G", nG_id)
                w.wait("t", "V", busy["tr"])
                npe = None
                for ic in range(16):
                    npe = w.op("t", lambda e, i=ic, ll=lw: e.transpose(
                        out=ps_tr[:, i * 128:i * 128 + 128],
                        in_=ll[:, i * 128:i * 128 + 128], identity=id_bs[:]),
                        "PE" if ic == 15 else None)
                w.wait("v", "PE", npe)
                if npe_free is not None:
                    w.wait("v", "PE", npe_free)
                nev = w.op("v", lambda e, g=gslot: e.tensor_copy(
                    out=ln1t_v[:, :, g * 128:g * 128 + 128], in_=ps_tr[:]), "V")
                busy["tr"] = nev
                if debug and not dbg_first["done"]:
                    dbg_first["done"] = True
                    w.dump(lw, dbg_lnr[:, :])
                    w.dump(st8[:], dbg_st8[:, :])
                return nev

            rope_busy = {0: None, 1: None}

            def rope(par, tabc, tabs, tcol, dst_ap_fn, na_raw):
                raw = praw[:, par * 512: par * 512 + 512]
                swp = pswp[:, par * 512: par * 512 + 512]
                w.wait("g", "A", na_raw)
                if rope_busy[par] is not None:
                    w.wait("g", "V", rope_busy[par])
                w.op("g", lambda e: e.dma_start(out=swp[0:64, :], in_=raw[64:128, :]), f"S{par}", 16)
                ng = w.op("g", lambda e: e.dma_start(out=swp[64:128, :], in_=raw[0:64, :]), f"S{par}", 16)
                w.wait("v", f"S{par}", ng)
                w.wait("v", "A", na_raw)
                v_tt(pr1[:], swp, tabs[:, tcol:tcol + 512], ALU.mult)
                v_tt(pr2[:], raw, tabc[:, tcol:tcol + 512], ALU.mult)
                nv = w.op("v", lambda e: e.tensor_tensor(
                    out=dst_ap_fn(), in0=pr1[:], in1=pr2[:], op=ALU.add), "V")
                rope_busy[par] = nv
                return nv

            def proj16(psum_ap, lhs_fn, rhs_fn, sem="PE"):
                npe = None
                for ic in range(16):
                    npe = w.op("t", lambda e, i=ic: e.matmul(
                        out=psum_ap, lhsT=lhs_fn(i), rhs=rhs_fn(i),
                        start=(i == 0), stop=(i == 15)),
                        sem if ic == 15 else None)
                return npe

            w.wait("t", "G", nG_w)
            last_o = {}
            prev_grp_pe = None
            nA_evac = {0: None, 1: None}
            for grp in range(2):
                nev = None
                for tl in range(4):
                    nev = ln_transpose(xs, (grp * 4 + tl) * 128, tl, prev_grp_pe)
                w.wait("t", "V", nev)
                if debug and grp == 0:
                    w.dump(ln1t[:, :], dbg_ln1t[:, :])
                for g in range(4):
                    par = g % 2
                    if nA_evac[par] is not None:
                        w.wait("t", "A", nA_evac[par])
                    npe = proj16(
                        ps_p[:, par * 512:par * 512 + 512],
                        lambda i, gg=g: wkt_s[:, i * 512 + gg * 128:
                                              i * 512 + gg * 128 + 128],
                        lambda i: ln1t[:, i * 512:i * 512 + 512])
                    w.wait("a", "PE", npe)
                    na = w.op("a", lambda e, p=par: e.activation(
                        out=praw[:, p * 512:p * 512 + 512],
                        in_=ps_p[:, p * 512:p * 512 + 512], func=AF.Identity), "A")
                    nA_evac[par] = na
                    w.wait("v", "G", nG_tab)
                    w.wait("v", f"O{par}", last_o.get(par, 0))
                    nv = rope(par, tab_c, tab_s, grp * 512,
                              lambda p=par: kvstg[:, p * 512:p * 512 + 512], na)
                    w.wait("g", "V", nv)
                    last_o[par] = w.op("g", lambda e, gg=g, gr=grp, p=par: e.dma_start(
                        out=kvk_in[gg, :, gr * 512:gr * 512 + 512],
                        in_=kvstg[:, p * 512:p * 512 + 512]), f"O{par}", 16)
                for tl in range(4):
                    par = tl % 2
                    if nA_evac[par] is not None:
                        w.wait("t", "A", nA_evac[par])
                    npe = proj16(
                        ps_p[:, par * 512:par * 512 + 512],
                        lambda i, t_=tl: ln1t_v[:, i:i + 1, t_ * 128:t_ * 128 + 128],
                        lambda i: wvt_s[:, i * 512:i * 512 + 512])
                    w.wait("a", "PE", npe)
                    w.wait("a", f"O{par}", last_o.get(par, 0))
                    na = w.op("a", lambda e, p=par: e.activation(
                        out=kvstg[:, p * 512:p * 512 + 512],
                        in_=ps_p[:, p * 512:p * 512 + 512], func=AF.Identity), "A")
                    nA_evac[par] = na
                    w.wait("g", "A", na)
                    last_o[par] = w.op("g", lambda e, t_=tl, gr=grp, p=par: e.dma_start(
                        out=kvv_in[gr * 4 + t_, :, :],
                        in_=kvstg[:, p * 512:p * 512 + 512]), f"O{par}", 16)
                prev_grp_pe = npe
            w.wait("g", "O0", w.n["O0"])
            w.wait("g", "O1", w.n["O1"])
            w.op("g", lambda e: e.collective_compute(
                "AllGather", ALU.bypass, replica_groups=[list(range(NC))],
                ins=[kvk_in], outs=[kvk_ag]), "P")
            nP_ag = w.op("g", lambda e: e.collective_compute(
                "AllGather", ALU.bypass, replica_groups=[list(range(NC))],
                ins=[kvv_in], outs=[kvv_ag]), "P")

            for grp in range(2):
                nev = None
                for tl in range(4):
                    nev = ln_transpose(xq, (grp * 4 + tl) * 128, tl, prev_grp_pe)
                w.wait("t", "V", nev)
                ncol = {0: 0, 1: 0}
                npe_q = {0: None, 1: None}
                for h in range(16):
                    parw = h % 2
                    w.wait("g", "PE", npe_q[parw])
                    ncol[parw] = w.op("g", lambda e, hh=h, p=parw: e.dma_start(
                        out=wq_col[:, p * 2048:p * 2048 + 2048],
                        in_=wqt.rearrange("(ic p) o -> p ic o", p=128)
                        [:, :, hh * 128:hh * 128 + 128]), f"C{parw}", 16)
                    par = h % 2
                    if nA_evac[par] is not None:
                        w.wait("t", "A", nA_evac[par])
                    w.wait("t", f"C{parw}", ncol[parw])
                    npe = proj16(
                        ps_p[:, par * 512:par * 512 + 512],
                        lambda i, pw=parw: wq_col[:, pw * 2048 + i * 128:
                                                  pw * 2048 + i * 128 + 128],
                        lambda i: ln1t[:, i * 512:i * 512 + 512])
                    npe_q[parw] = npe
                    w.wait("a", "PE", npe)
                    na = w.op("a", lambda e, p=par: e.activation(
                        out=praw[:, p * 512:p * 512 + 512],
                        in_=ps_p[:, p * 512:p * 512 + 512], func=AF.Identity), "A")
                    nA_evac[par] = na
                    rope(par, tab_cq, tab_sq, 1024 + grp * 512,
                         lambda hh=h, g=grp: q_all[:, hh * 1024 + g * 512:
                                                   hh * 1024 + g * 512 + 512], na)
                for g in range(4):
                    par = g % 2
                    if nA_evac[par] is not None:
                        w.wait("t", "A", nA_evac[par])
                    npe = proj16(
                        ps_p[:, par * 512:par * 512 + 512],
                        lambda i, gg=g: wkt_s[:, i * 512 + gg * 128:
                                              i * 512 + gg * 128 + 128],
                        lambda i: ln1t[:, i * 512:i * 512 + 512])
                    w.wait("a", "PE", npe)
                    na = w.op("a", lambda e, p=par: e.activation(
                        out=praw[:, p * 512:p * 512 + 512],
                        in_=ps_p[:, p * 512:p * 512 + 512], func=AF.Identity), "A")
                    nA_evac[par] = na
                    rope(par, tab_c, tab_s, 1024 + grp * 512,
                         lambda gg=g, gr=grp: diag_k[:, gg * 1024 + gr * 512:
                                                     gg * 1024 + gr * 512 + 512], na)
                for tl in range(4):
                    par = tl % 2
                    if nA_evac[par] is not None:
                        w.wait("t", "A", nA_evac[par])
                    npe = proj16(
                        ps_p[:, par * 512:par * 512 + 512],
                        lambda i, t_=tl: ln1t_v[:, i:i + 1, t_ * 128:t_ * 128 + 128],
                        lambda i: wvt_s[:, i * 512:i * 512 + 512])
                    w.wait("a", "PE", npe)
                    na = w.op("a", lambda e, t_=tl, gr=grp, p=par: e.activation(
                        out=diag_v[:, (gr * 4 + t_) * 512:(gr * 4 + t_) * 512 + 512],
                        in_=ps_p[:, p * 512:p * 512 + 512], func=AF.Identity), "A")
                    nA_evac[par] = na
                prev_grp_pe = npe
            if debug:
                w.wait("g", "V", w.n["V"])
                w.wait("g", "A", w.n["A"])
                for dst_, src_ in ((dbg_tabc, tab_c), (dbg_tabs, tab_s)):
                    w.op("g", lambda e, d=dst_, s=src_: e.dma_start(
                        out=d[:, :], in_=s[:]), "G", 16)
                w.op("g", lambda e: e.dma_start(out=dbg_q[:, :], in_=q_all[:]), "G", 16)
                w.op("g", lambda e: e.dma_start(out=dbg_dk[:, :], in_=diag_k[:]), "G", 16)
                w.op("g", lambda e: e.dma_start(out=dbg_dv[:, :], in_=diag_v[:]), "G", 16)
                w.op("g", lambda e: e.dma_start(out=dbg_kvk[:, :, :], in_=kvk_in[:, :, :]), "G", 16)
                w.op("g", lambda e: e.dma_start(out=dbg_kvv[:, :, :], in_=kvv_in[:, :, :]), "G", 16)
            w.fence()

            with nc.Block() as BA:
                @BA.sync
                def _(e):
                    for f in w.cmds[0]["q"]:
                        f(e)

                @BA.gpsimd
                def _(e):
                    for f in w.cmds[0]["g"]:
                        f(e)

                @BA.vector
                def _(e):
                    for f in w.cmds[0]["v"]:
                        f(e)

                @BA.scalar
                def _(e):
                    for f in w.cmds[0]["a"]:
                        f(e)

                @BA.tensor
                def _(e):
                    for f in w.cmds[0]["t"]:
                        f(e)

        with contextlib.ExitStack() as sb:
            kb = sb.enter_context(nc.sbuf_tensor("kb", [128, 1024], BF16))
            vb = sb.enter_context(nc.sbuf_tensor("vb", [128, 1024], BF16))
            expst = sb.enter_context(nc.sbuf_tensor("expst", [128, 2048], BF16))
            attn = sb.enter_context(nc.sbuf_tensor("attn", [128, 16 * 1024], BF16))
            rcp = sb.enter_context(nc.sbuf_tensor("rcp", [128, 1024], F32))
            wot_col = sb.enter_context(nc.sbuf_tensor("wot_col", [128, 16 * 512], BF16))
            wfq_col = sb.enter_context(nc.sbuf_tensor("wfq_col", [128, 16 * 512], BF16))
            xqt = sb.enter_context(nc.sbuf_tensor("xqt", [128, JT * 2048], BF16))
            wq_stage = xqt[:, 0:D]
            aout_stg = None
            ki_b = sb.enter_context(nc.sbuf_tensor("ki_b", [128, D], I32))
            sc2 = [sb.enter_context(nc.psum_tensor(f"sc2_{i}", [128, 1024], F32))
                   for i in range(2)]
            acc2 = sb.enter_context(nc.psum_tensor("acc2", [128, 1024], F32))
            sums2 = sb.enter_context(nc.psum_tensor("sums2", [128, 1024], F32))
            aout_stg = rcp

            w.blk = 1
            kvk_v = kvk_ag.rearrange("r g p t -> r p g t")
            kb_v = kb.rearrange("p (c g t) -> p c g t", c=4, g=2)
            q_v = q_all.rearrange("p (h f) -> p h f", h=16)
            attn_v = attn.rearrange("p (h f) -> p h f", h=16)

            wst = {"ic1": 0, "ic2": 0, "v_done": None, "a_done": None}

            def wprep1():
                ic = wst["ic1"]; wst["ic1"] += 1
                if wst["v_done"] is not None:
                    w.wait("g", "V", wst["v_done"])
                nq = w.op("g", lambda e, i=ic: e.dma_start(
                    out=pool[0][:], in_=wft[i * 128:i * 128 + 128, :]), "G", 16)
                w.wait("v", "G", nq)
                nv = w.op("v", lambda e: e.tensor_reduce(
                    out=stx(I_T1), in_=pool[0][:], axis=AX.X, op=ALU.add,
                    apply_absolute_value=True), "V")
                vdrain()
                if ic == 0:
                    nv = w.op("v", lambda e: e.tensor_copy(out=wsum[:], in_=stx(I_T1)), "V")
                else:
                    nv = v_tt(wsum[:], wsum[:], stx(I_T1), ALU.add, "V")
                vdrain()
                wst["v_done"] = nv

            def wprep_scale():
                w.wait("g", "V", wst["v_done"])
                nq = w.op("g", lambda e: e.dma_start(out=wrow[:, :], in_=wsum[:, :]), "G", 16)
                w.wait("v", "G", nq)
                nvr = w.op("v", lambda e: e.tensor_reduce(
                    out=one1[0:1, 0:1], in_=wrow[:, :], axis=AX.X, op=ALU.add), "V")
                vdrain()
                w.wait("g", "V", nvr)
                nq2 = w.op("g", lambda e: e.dma_start(
                    out=sc_d[0:1, 4:5], in_=one1[0:1, 0:1]), "G", 16)
                w.wait("g", "G", nq2)
                nq3 = w.op("g", lambda e: e.dma_start(
                    out=scal[:, 4:5],
                    in_=sc_d[0:1, 4:5].partition_broadcast(128)), "G", 16)
                w.wait("v", "G", nq3)
                w.op("v", lambda e: e.tensor_scalar(
                    out=scal[:, 0:1], in0=scal[:, 4:5],
                    scalar1=1.0 / 4194304.0, scalar2=1e-5,
                    op0=ALU.mult, op1=ALU.max))
                vdrain()
                nv3 = w.op("v", lambda e: e.tensor_scalar(
                    out=scal[:, 1:2], in0=scal[:, 0:1], scalar1=0.5,
                    scalar2=None, op0=ALU.mult), "V")
                vdrain()
                wst["v_done"] = nv3
                return nv3

            def wprep2(nq_tau):
                ic = wst["ic2"]; wst["ic2"] += 1
                if wst["v_done"] is not None:
                    w.wait("q", "V", wst["v_done"])
                if wst["a_done"] is not None:
                    w.wait("q", "A", wst["a_done"])
                nq = w.op("q", lambda e, i=ic: e.dma_start(
                    out=pool[0][:], in_=wft[i * 128:i * 128 + 128, :]), "Q", 16)
                w.wait("a", "Q", nq)
                na0 = w.op("a", lambda e: e.activation(
                    out=sq_s[:], in_=pool[0][:], func=AF.Abs), "A")
                na = w.op("a", lambda e: e.activation(
                    out=pool[1][:], in_=pool[0][:], func=AF.Sign), "A")
                wst["a_done"] = na
                w.wait("v", "A", na)
                v_ts(sq_s[:], sq_s[:], scal[:, 1:2], op0=ALU.is_ge)
                nv2 = v_tt(wq_stage, sq_s[:], pool[1][:], ALU.mult, "V")
                wst["v_done"] = nv2
                w.wait("g", "V", nv2)
                return w.op("g", lambda e, i=ic: e.dma_start(
                    out=wfq_d[i:i + 1, :, :], in_=wq_stage), "G", 16)

            SKIPW = bool(os.environ.get("SKIPW"))
            JMAX = int(os.environ.get("JMAX", "8"))
            SKIP456 = bool(os.environ.get("SKIP456"))
            ss = 0
            li = 0
            nA_exp = {}
            nPE_sc = {}
            nPE_av = {}
            li_slot = {}
            load_q = {}
            nV_norm = None
            nq_tau = None
            ng_wfq = 0
            for j in range(min(JT, JMAX)):
                if SKIPW:
                    pass
                elif j <= 3:
                    for _ in range(4):
                        wprep1()
                elif j == 4:
                    nq_tau = wprep_scale()
                else:
                    for _ in range(5 if j < 7 else 6):
                        ng_wfq = wprep2(nq_tau)
                for p in range(2):
                    g0 = 2 * p
                    pend = None

                    def emit_av(info):
                        (pdiag, pj, pg0, ppar, plp, pdeps, pfirst, plast,
                         pss) = info
                        if pfirst and nV_norm is not None:
                            w.wait("t", "V", nV_norm)
                        for i in range(2):
                            w.wait("t", pdeps[i][0], pdeps[i][1])
                            g = pg0 + i
                            if pdiag:
                                vl = diag_v[:, pj * 512 + g * 128:
                                            pj * 512 + g * 128 + 128]
                            else:
                                vl = vb[:, plp * 256 + i * 128:
                                        plp * 256 + i * 128 + 128]
                            w.op("t", lambda e, vv=vl, pp=ppar, ii=i, f=pfirst,
                                 la=plast: e.matmul(
                                out=acc2[:, ii * 512:ii * 512 + 512], lhsT=vv,
                                rhs=expst[:, pp * 1024 + ii * 512:
                                          pp * 1024 + ii * 512 + 512],
                                start=f, stop=la), None)
                            nPE_av[pss] = w.op("t", lambda e, pp=ppar, ii=i,
                                               f=pfirst, la=plast: e.matmul(
                                out=sums2[:, ii * 512:ii * 512 + 512],
                                lhsT=ones_bf[:],
                                rhs=expst[:, pp * 1024 + ii * 512:
                                          pp * 1024 + ii * 512 + 512],
                                start=f, stop=la),
                                "PE" if i == 1 else None)

                    for s in range(L[j]):
                        diag = (s == 0)
                        par = ss % 2
                        if diag:
                            cur_li = None
                            lp = None
                        else:
                            kt = s - 1
                            r, loc = kt // 8, kt % 8
                            lpar = li % 4
                            if li - 4 >= 0:
                                w.wait("q", "PE", nPE_av.get(li_slot[li - 4], 0))
                            w.wait("q", "P", nP_ag)
                            w.op("q", lambda e, rr=r, lo=loc, gg=g0, lpx=lpar: e.dma_start(
                                out=kb_v[:, lpx:lpx + 1, :, :],
                                in_=kvk_v[rr:rr + 1, :, gg:gg + 2,
                                          lo * 128:lo * 128 + 128]), f"K{lpar}", 16)
                            nqv = w.op("q", lambda e, rr=r, lo=loc, gg=g0, lpx=lpar: e.dma_start(
                                out=vb[:, lpx * 256:lpx * 256 + 256],
                                in_=kvv_ag[rr:rr + 1, lo:lo + 1, :,
                                           gg * 128:gg * 128 + 256]), f"K{lpar}", 16)
                            load_q[li] = nqv
                            li_slot[li] = ss
                            cur_li = li
                            lp = lpar
                            li += 1
                        if not diag:
                            w.wait("t", f"K{lp}", load_q[cur_li])
                        if ss >= 2:
                            w.wait("t", "A", nA_exp[(ss - 2, 1)])
                        sc_ms = {}
                        for i in range(2):
                            g = g0 + i
                            if diag:
                                kl = diag_k[:, g * 1024 + j * 128:
                                            g * 1024 + j * 128 + 128]
                            else:
                                kl = kb[:, lp * 256 + i * 128: lp * 256 + i * 128 + 128]
                            sc_ms[i] = w.op("t", lambda e, kk=kl, gg=g, pp=par, ii=i, jj=j: e.matmul(
                                out=sc2[pp][:, ii * 512:ii * 512 + 512],
                                lhsT=kk,
                                rhs=q_v[:, 4 * gg:4 * gg + 4, jj * 128:jj * 128 + 128],
                                start=True, stop=True), "PE")
                        if pend is not None:
                            emit_av(pend)
                        bidx = BOFF[j] + s
                        deps = []
                        for i in range(2):
                            w.wait("a", "PE", sc_ms[i])
                            if ss >= 2:
                                w.wait("a", "PE", nPE_av.get(ss - 2, 0))
                            nA_exp[(ss, i)] = w.op("a", lambda e, pp=par, bi=bidx, ii=i: e.activation(
                                out=expst[:, pp * 1024 + ii * 512:
                                          pp * 1024 + ii * 512 + 512],
                                in_=sc2[pp][:, ii * 512:ii * 512 + 512],
                                func=AF.Exp, scale=epsb[:, 2:3],
                                bias=btab_b[:, bi:bi + 1]), "A")
                            if diag:
                                w.wait("v", "A", nA_exp[(ss, i)])
                                nvm = w.op("v", lambda e, pp=par, ii=i: e.tensor_tensor(
                                    out=expst[:, pp * 1024 + ii * 512:
                                              pp * 1024 + ii * 512 + 512],
                                    in0=expst[:, pp * 1024 + ii * 512:
                                              pp * 1024 + ii * 512 + 512],
                                    in1=caus_s[:, 0:512], op=ALU.mult), "V")
                                deps.append(("V", nvm))
                            else:
                                deps.append(("A", nA_exp[(ss, i)]))
                        pend = (diag, j, g0, par, lp, deps,
                                s == 0, s == L[j] - 1, ss)
                        ss += 1
                    emit_av(pend)
                    pend = None
                    w.wait("v", "PE", nPE_av[ss - 1])
                    w.op("v", lambda e: e.reciprocal(out=rcp[:], in_=sums2[:]))
                    vdrain()
                    nV_norm = w.op("v", lambda e, pp=p, jj=j: e.tensor_tensor(
                        out=attn_v[:, 8 * pp:8 * pp + 8, jj * 128:jj * 128 + 128],
                        in0=acc2[:], in1=rcp[:], op=ALU.mult), "V")

            w.fence()

            if SKIP456:
                for j in range(JT):
                    nv0 = w.op("v", lambda e: e.memset(pool[0][:], 0.0), "V")
                    w.wait("q", "V", nv0)
                    w.op("q", lambda e, jj=j: e.dma_start(
                        out=out[jj * 128:jj * 128 + 128, :], in_=pool[0][:]), "Q", 16)
                w.wait("q", "Q", w.n["Q"])
                w.wait("q", "K", w.n["K"])
                w.wait("g", "G", w.n["G"])
                w.wait("g", "S", w.n["S"])
                w.wait("g", "O0", w.n["O0"])
            w.wait("g", "O1", w.n["O1"])
            _skip456 = SKIP456
            nA_stg = {0: None, 1: None}
            if _skip456:
                JT_RANGE = []
            else:
                JT_RANGE = list(range(JT))
            nQ_stg = {0: 0, 1: 0}
            npe_ob = None
            for ob in (range(4) if not _skip456 else []):
                if npe_ob is not None:
                    w.wait("g", "PE", npe_ob)
                ncol = w.op("g", lambda e, o=ob: e.dma_start(
                    out=wot_col[:],
                    in_=wot.rearrange("(ic p) o -> p ic o", p=128)
                    [:, :, o * 512:o * 512 + 512]), "C", 16)
                for j in range(JT):
                    par = j % 2
                    w.wait("t", "C", ncol)
                    if nA_stg[par] is not None:
                        w.wait("t", "A", nA_stg[par])
                    npe_ob = proj16(
                        acc2[:, par * 512:par * 512 + 512],
                        lambda i, jj=j: attn_v[:, i:i + 1, jj * 128:jj * 128 + 128],
                        lambda i: wot_col[:, i * 512:i * 512 + 512])
                    w.wait("a", "PE", npe_ob)
                    w.wait("a", f"T{par}", nQ_stg[par])
                    na = w.op("a", lambda e, pp=par: e.activation(
                        out=aout_stg[:, pp * 512:pp * 512 + 512],
                        in_=acc2[:, pp * 512:pp * 512 + 512], func=AF.Identity), "A")
                    nA_stg[par] = na
                    w.wait("q", "A", na)
                    nQ_stg[par] = w.op("q", lambda e, jj=j, o=ob, pp=par: e.dma_start(
                        out=ao_d[jj * 128:jj * 128 + 128, o * 512:o * 512 + 512],
                        in_=aout_stg[:, pp * 512:pp * 512 + 512]), f"T{par}", 16)
            prev_v = None
            for j in JT_RANGE:
                if prev_v is not None:
                    w.wait("q", "V", prev_v)
                w.wait("q", "T0", nQ_stg[0])
                w.wait("q", "T1", nQ_stg[1])
                nq1 = w.op("q", lambda e, jj=j: e.dma_start(
                    out=pool[0][:], in_=ao_d[jj * 128:jj * 128 + 128, :]), "Q", 16)
                nq2 = w.op("q", lambda e, jj=j: e.dma_start(
                    out=pool[1][:], in_=xq[jj * 128:jj * 128 + 128, :]), "Q", 16)
                ln_stats(pool[0][:], epsb[:, 0:1], nq2)
                v_ts(sq_s[:], pool[0][:], stx(I_NM), stx(I_RI), ALU.add, ALU.mult)
                w.wait("v", "Q", nq2)
                nv = v_tt(pool[0][:], sq_s[:], pool[1][:], ALU.add, "V")
                w.wait("q", "V", nv)
                w.op("q", lambda e, jj=j: e.dma_start(
                    out=h1_d[jj * 128:jj * 128 + 128, :], in_=pool[0][:]), "Q", 16)
                ln_stats(pool[0][:], epsb[:, 0:1], nv, rsem="V")
                nv_f = v_ts(pool[1][:], pool[0][:], stx(I_NM), stx(I_RI),
                            ALU.add, ALU.mult, sem="V")
                w.wait("a", "V", nv_f)
                na = w.op("a", lambda e: e.activation(
                    out=sq_s[:], in_=pool[1][:], func=AF.Square,
                    accum_out=stx(I_T1)), "A")
                w.wait("v", "A", na)
                v_ts(stx(I_T2), stx(I_T1), 1.0 / 2048.0)
                vdrain()
                nv2 = w.op("v", lambda e: e.tensor_copy(out=stx(I_T3), in_=stx(I_T2)), "V")
                vdrain()
                w.wait("a", "V", nv2)
                na2 = w.op("a", lambda e: e.activation(
                    out=stx(I_SD), in_=stx(I_T3), func=AF.Sqrt,
                    bias=epsb[:, 1:2]), "A")
                w.wait("v", "A", na2)
                w.op("v", lambda e: e.reciprocal(out=stx(I_RI2), in_=stx(I_SD)))
                vdrain()
                v_stt(pool[1][:], pool[1][:], stx(I_RI2), gam_b[:], ALU.mult, ALU.mult)
                nv3 = w.op("v", lambda e: e.tensor_reduce(
                    out=stx(I_T1), in_=pool[1][:], axis=AX.X, op=ALU.max,
                    apply_absolute_value=True), "V")
                vdrain()
                if j == 0:
                    nv4 = w.op("v", lambda e: e.tensor_copy(out=stx(I_RMX), in_=stx(I_T1)), "V")
                else:
                    nv4 = v_tt(stx(I_RMX), stx(I_RMX), stx(I_T1), ALU.max, "V")
                vdrain()
                prev_v = nv4
                w.wait("q", "V", nv4)
                w.op("q", lambda e, jj=j: e.dma_start(
                    out=xn_d[jj * 128:jj * 128 + 128, :], in_=pool[1][:]), "Q", 16)

            if not _skip456:
                w.wait("q", "V", prev_v)
                nqm = w.op("q", lambda e: e.dma_start(
                    out=wrow[:, :], in_=st8[:, I_RMX:I_RMX + 1]), "Q", 16)
                w.wait("v", "Q", nqm)
                nvm = w.op("v", lambda e: e.tensor_reduce(
                    out=one1[0:1, 4:5], in_=wrow[:, :], axis=AX.X, op=ALU.max), "V")
                vdrain()
                w.wait("q", "V", nvm)
                nqm2 = w.op("q", lambda e: e.dma_start(
                    out=mx_in[0:1, 0:1], in_=one1[0:1, 4:5]), "Q", 16)
                w.wait("g", "Q", nqm2)
                nP_ar = w.op("g", lambda e: e.collective_compute(
                    "AllReduce", ALU.max, replica_groups=[list(range(NC))],
                    ins=[mx_in], outs=[mx_out]), "P")
                w.wait("q", "P", nP_ar)
                nqm3 = w.op("q", lambda e: e.dma_start(
                    out=scal[:, 5:6],
                    in_=mx_out[0:1, 0:1].partition_broadcast(128)), "Q", 16)
                w.wait("v", "Q", nqm3)
                w.op("v", lambda e: e.tensor_scalar(
                    out=scal[:, 6:7], in0=scal[:, 5:6], scalar1=1e-5,
                    scalar2=None, op0=ALU.max))
                vdrain()
                w.op("v", lambda e: e.reciprocal(out=scal[:, 7:8], in_=scal[:, 6:7]))
                vdrain()
                w.op("v", lambda e: e.tensor_scalar(
                    out=scal[:, 2:3], in0=scal[:, 7:8], scalar1=127.0,
                    scalar2=None, op0=ALU.mult))
                w.op("v", lambda e: e.tensor_scalar(
                    out=scal[:, 8:9], in0=scal[:, 6:7], scalar1=INV127,
                    scalar2=None, op0=ALU.mult))
                vdrain()
                nq_sc = w.op("v", lambda e: e.tensor_tensor(
                    out=scal[:, 3:4], in0=scal[:, 8:9], in1=scal[:, 0:1],
                    op=ALU.mult), "V")
                vdrain()

                prev_v = None
                prev_evac = None
                for j in range(JT):
                    if prev_v is not None:
                        w.wait("q", "V", prev_v)
                    nq1 = w.op("q", lambda e, jj=j: e.dma_start(
                        out=pool[0][:], in_=xn_d[jj * 128:jj * 128 + 128, :]), "Q", 16)
                    w.wait("v", "Q", nq1)
                    w.wait("v", "V", nq_sc) if False else None
                    v_ts(pool[1][:], pool[0][:], scal[:, 2:3])
                    v_ts(pool[1][:], pool[1][:], -128.0, 127.0, ALU.max, ALU.min)
                    w.op("v", lambda e: e.tensor_copy(out=ki_b[:], in_=pool[1][:]))
                    w.op("v", lambda e: e.tensor_copy(out=sq_s[:], in_=ki_b[:]))
                    v_tt(pool[1][:], pool[1][:], sq_s[:], ALU.subtract)
                    v_ts(pool[0][:], pool[1][:], 0.5, op0=ALU.is_ge)
                    v_ts(pool[1][:], pool[1][:], -0.5, op0=ALU.is_lt)
                    v_tt(pool[0][:], pool[0][:], pool[1][:], ALU.subtract)
                    nv = v_tt(pool[1][:], sq_s[:], pool[0][:], ALU.add, "V")
                    w.wait("t", "V", nv)
                    if prev_evac is not None:
                        w.wait("t", "V", prev_evac)
                    npe_h = {}
                    for half in range(2):
                        for ic in range(8):
                            icg = half * 8 + ic
                            npe_h[half] = w.op("t", lambda e, i=icg, hh=half, ii=ic: e.transpose(
                                out=sc2[hh][:, ii * 128:ii * 128 + 128],
                                in_=pool[1][:, i * 128:i * 128 + 128],
                                identity=id_fs[:]), "PE" if ic == 7 else None)
                    nevs = []
                    w.wait("v", "G", ng_wfq)
                    for half in range(2):
                        w.wait("v", "PE", npe_h[half])
                        nevs.append(w.op("v", lambda e, jj=j, hh=half: e.tensor_copy(
                            out=xqt[:, jj * 2048 + hh * 1024: jj * 2048 + hh * 1024 + 1024],
                            in_=sc2[hh][:]), "V"))
                    prev_evac = nevs[-1]
                    prev_v = nevs[-1]

                nA_stg = {0: None, 1: None}
                nQ_stg = {0: w.n["T0"], 1: w.n["T1"]}
                npe_ob = None
                for ob in range(4):
                    if npe_ob is not None:
                        w.wait("q", "PE", npe_ob)
                    w.wait("q", "G", ng_wfq)
                    ncol = w.op("q", lambda e, o=ob: e.dma_start(
                        out=wfq_col[:],
                        in_=wfq_d.rearrange("ic p o -> p ic o")
                        [:, :, o * 512:o * 512 + 512]), "C", 16)
                    for j in range(JT):
                        par = j % 2
                        w.wait("t", "C", ncol)
                        w.wait("t", "V", prev_evac)
                        if nA_stg[par] is not None:
                            w.wait("t", "A", nA_stg[par])
                        npe_ob = proj16(
                            acc2[:, par * 512:par * 512 + 512],
                            lambda i, jj=j: xqt[:, jj * 2048 + i * 128:
                                               jj * 2048 + i * 128 + 128],
                            lambda i: wfq_col[:, i * 512:i * 512 + 512])
                        w.wait("a", "PE", npe_ob)
                        w.wait("a", "Q", nQ_stg[par])
                        w.wait("a", "V", nq_sc)
                        na = w.op("a", lambda e, pp=par: e.activation(
                            out=aout_stg[:, pp * 512:pp * 512 + 512],
                            in_=acc2[:, pp * 512:pp * 512 + 512], func=AF.Identity,
                            scale=scal[:, 3:4]), "A")
                        nA_stg[par] = na
                        w.wait("q", "A", na)
                        nQ_stg[par] = w.op("q", lambda e, jj=j, o=ob, pp=par: e.dma_start(
                            out=fo_d[jj * 128:jj * 128 + 128, o * 512:o * 512 + 512],
                            in_=aout_stg[:, pp * 512:pp * 512 + 512]), "Q", 16)
                prev_v = None
                for j in range(JT):
                    if prev_v is not None:
                        w.wait("q", "V", prev_v)
                    w.wait("q", "Q", nQ_stg[0])
                    w.wait("q", "Q", nQ_stg[1])
                    nq1 = w.op("q", lambda e, jj=j: e.dma_start(
                        out=pool[0][:], in_=fo_d[jj * 128:jj * 128 + 128, :]), "Q", 16)
                    nq2 = w.op("q", lambda e, jj=j: e.dma_start(
                        out=pool[1][:], in_=h1_d[jj * 128:jj * 128 + 128, :]), "Q", 16)
                    ln_stats(pool[0][:], epsb[:, 0:1], nq2)
                    v_ts(sq_s[:], pool[0][:], stx(I_NM), stx(I_RI), ALU.add, ALU.mult)
                    w.wait("v", "Q", nq2)
                    nv = v_tt(pool[0][:], sq_s[:], pool[1][:], ALU.add, "V")
                    prev_v = nv
                    w.wait("q", "V", nv)
                    w.op("q", lambda e, jj=j: e.dma_start(
                        out=out[jj * 128:jj * 128 + 128, :], in_=pool[0][:]), "Q", 16)

                if debug:
                    w.wait("g", "V", w.n["V"])
                    w.wait("g", "A", w.n["A"])
                    w.wait("g", "Q", w.n["Q"])
                    w.op("g", lambda e: e.dma_start(out=dbg_attn[:, :], in_=attn[:]), "G", 16)
                    w.op("g", lambda e: e.dma_start(out=dbg_scal[:, :], in_=scal[:]), "G", 16)
                    w.op("g", lambda e: e.dma_start(out=dbg_one[:, :], in_=one1[:]), "G", 16)
                    w.op("g", lambda e: e.dma_start(out=dbg_xqt[:, :], in_=xqt[:]), "G", 16)
                    w.op("g", lambda e: e.dma_start(out=dbg_wfq[:, :, :], in_=wfq_d[:, :, :]), "G", 16)
            w.wait("q", "Q", w.n["Q"])
            for s_ in ("K0", "K1", "K2", "K3", "T0", "T1"):
                w.wait("q", s_, w.n[s_])
            w.wait("q", "C", w.n["C"])
            w.wait("g", "G", w.n["G"])
            for s_ in ("C0", "C1", "S0", "S1"):
                w.wait("g", s_, w.n[s_])
            w.wait("g", "O0", w.n["O0"])
            w.wait("g", "O1", w.n["O1"])

            with nc.Block() as BB:
                @BB.sync
                def _(e):
                    for f in w.cmds[1]["q"]:
                        f(e)

                @BB.gpsimd
                def _(e):
                    for f in w.cmds[1]["g"]:
                        f(e)

                @BB.vector
                def _(e):
                    for f in w.cmds[1]["v"]:
                        f(e)

                @BB.scalar
                def _(e):
                    for f in w.cmds[1]["a"]:
                        f(e)

                @BB.tensor
                def _(e):
                    for f in w.cmds[1]["t"]:
                        f(e)
    return nc


def kernel(L0_attn_x, L0_attn_Wq, L0_attn_Wk, L0_attn_Wv, L0_attn_Wo,
           L0_ffn_weight, L0_ffn_norm_weight):
    x = np.ascontiguousarray(np.asarray(L0_attn_x, dtype=np.float32).reshape(S, D))
    Wq = np.asarray(L0_attn_Wq, dtype=np.float32)
    Wk = np.asarray(L0_attn_Wk, dtype=np.float32)
    Wv = np.asarray(L0_attn_Wv, dtype=np.float32)
    Wo = np.asarray(L0_attn_Wo, dtype=np.float32)
    Wf = np.asarray(L0_ffn_weight, dtype=np.float32)
    gam = np.ascontiguousarray(np.asarray(L0_ffn_norm_weight, dtype=np.float32))

    perm = np.concatenate([np.arange(0, 128, 2), np.arange(1, 128, 2)])
    Wq_p = Wq.reshape(16, 128, D)[:, perm, :].reshape(D, D)
    Wk_p = Wk.reshape(4, 128, D)[:, perm, :].reshape(512, D)
    wqt = np.ascontiguousarray(Wq_p.T)
    wkt = np.ascontiguousarray(Wk_p.T)
    wvt = np.ascontiguousarray(Wv.T)
    wot = np.ascontiguousarray(Wo.T)
    wft = np.ascontiguousarray(Wf.T)

    nc = build(debug=False)
    in_maps = []
    for c in range(NC):
        tiles = [8 * j + c for j in range(JT)]
        xq_c = np.ascontiguousarray(x.reshape(64, 128, D)[tiles].reshape(TPC, D))
        xs_c = np.ascontiguousarray(x[c * TPC:(c + 1) * TPC])
        tvals = np.zeros((1, 2048), dtype=np.float32)
        tvals[0, :1024] = np.arange(c * TPC, (c + 1) * TPC, dtype=np.float32)
        own_tok = np.concatenate([np.arange(t * 128, t * 128 + 128) for t in tiles])
        tvals[0, 1024:] = own_tok.astype(np.float32)
        bt = np.zeros((1, NSLOT), dtype=np.float32)
        for j in range(JT):
            t_own = 8 * j + c
            for s_ in range(1, L[j]):
                if s_ - 1 >= t_own:
                    bt[0, BOFF[j] + s_] = BKILL
        in_maps.append({
            "xq": xq_c, "xs": xs_c, "wqt": wqt, "wkt": wkt, "wvt": wvt,
            "wot": wot, "wft": wft, "gam": gam, "tidx": tvals, "btab": bt,
        })
    res = run_bass_kernel_spmd(nc, in_maps, core_ids=list(range(NC)))
    full = np.empty((S, D), dtype=np.float32)
    for c in range(NC):
        o = res.results[c]["out"]
        for j in range(JT):
            t = 8 * j + c
            full[t * 128:(t + 1) * 128] = o[j * 128:(j + 1) * 128]
    return full.reshape(1, S, D)


# version 08
# speedup vs baseline: 1.1190x; 1.0281x over previous
import contextlib
import math
import numpy as np
import ml_dtypes
import concourse.bass as bass
import concourse.mybir as mybir
from concourse.bass_utils import run_bass_kernel_spmd

F32 = mybir.dt.float32
BF16 = mybir.dt.bfloat16
I32 = mybir.dt.int32
AF = mybir.ActivationFunctionType
ALU = mybir.AluOpType
AX = mybir.AxisListType

S, D = 8192, 2048
NC, JT = 8, 8
TPC = JT * 128
L = [8 * j + 8 for j in range(JT)]
BOFF = [sum(L[:j]) for j in range(JT)]
NSLOT = sum(L)

TWO_PI = 2 * math.pi
C1 = float(np.float32(int(TWO_PI * 2 ** 10) / 2 ** 10))
C2 = float(np.float32(TWO_PI - C1))
C3 = float(np.float32(TWO_PI - C1 - float(np.float32(TWO_PI - C1))))
INV2PI = float(np.float32(1.0 / TWO_PI))
BOUND = 3.141590
PI2_HI = float(np.float32(math.pi / 2))
PI2_LO = float(np.float32(math.pi / 2 - float(np.float32(math.pi / 2))))
SC64 = 1.0 / math.sqrt(128.0)
SC_HI = float(np.float32(SC64))
SC_LO = float(np.float32(SC64 / SC_HI))
BKILL = -30000.0
INV127 = float(np.float32(1.0 / 127.0))

_ID_BF = np.eye(128, dtype=ml_dtypes.bfloat16)
_ID_F32 = np.eye(128, dtype=np.float32)
_CAUSAL = np.tile((np.arange(128)[:, None] <= np.arange(128)[None, :])
                  .astype(ml_dtypes.bfloat16), (1, 8))


import os
_SEMALIAS = {"K": "Q"}
for _kv in os.environ.get("SEMALIAS", "").split(","):
    if ":" in _kv:
        _a, _b = _kv.split(":")
        _SEMALIAS[_a] = _b


class _W:
    ENG = "qgvat"

    def __init__(self):
        self.cmds = {0: {e: [] for e in self.ENG}, 1: {e: [] for e in self.ENG}}
        self.blk = 0
        self.n = {k: 0 for k in ("Q", "G", "PE", "V", "A", "P", "B",
                                 "K", "C", "S", "O", "K0", "K1", "K2", "K3",
                                 "C0", "C1", "S0", "S1", "T0", "T1", "O0", "O1")}
        self.m = dict(_SEMALIAS)
        self.last_wait = {}
        self.sems = None

    def emit(self, eng, fn):
        self.cmds[self.blk][eng].append(fn)

    def wait(self, eng, sem, val):
        sem = self.m.get(sem, sem)
        if val is None or val <= 0:
            return
        key = (eng, sem)
        if self.last_wait.get(key, -1) >= val:
            return
        self.last_wait[key] = val
        self.emit(eng, lambda e, s=sem, v=val: e.wait_ge(self.sems[s], v))

    def op(self, eng, fn, sem=None, k=1):
        if sem is None:
            self.emit(eng, fn)
            return None
        sem = self.m.get(sem, sem)
        self.n[sem] += k
        v = self.n[sem]
        self.emit(eng, lambda e, s=sem, kk=k: fn(e).then_inc(self.sems[s], kk))
        return v

    def dump(self, src_ap, dst_ap):
        self.wait("g", "V", self.n["V"])
        self.wait("g", "A", self.n["A"])
        self.wait("g", "PE", self.n["PE"])
        self.wait("g", "Q", self.n["Q"])
        ng = self.op("g", lambda e: e.dma_start(out=dst_ap, in_=src_ap), "G", 16)
        for e in "qvat":
            self.wait(e, "G", ng)
        return ng

    def fence(self):
        self.n["B"] += 5
        tgt = self.n["B"]
        for e in self.ENG:
            self.emit(e, lambda en: en.nop().then_inc(self.sems["B"], 1))
        for e in self.ENG:
            self.emit(e, lambda en, t=tgt: en.wait_ge(self.sems["B"], t))
        return tgt


def build(debug=False):
    w = _W()
    nc = bass.Bass("TRN2", target_bir_lowering=False, debug=False)
    dbg_kind = "ExternalOutput" if debug else "Internal"

    xq = nc.dram_tensor("xq", [TPC, D], F32, kind="ExternalInput").ap()
    xs = nc.dram_tensor("xs", [TPC, D], F32, kind="ExternalInput").ap()
    wqt = nc.dram_tensor("wqt", [D, D], F32, kind="ExternalInput").ap()
    wkt = nc.dram_tensor("wkt", [D, 512], F32, kind="ExternalInput").ap()
    wvt = nc.dram_tensor("wvt", [D, 512], F32, kind="ExternalInput").ap()
    wot = nc.dram_tensor("wot", [D, D], F32, kind="ExternalInput").ap()
    wft = nc.dram_tensor("wft", [D, D], F32, kind="ExternalInput").ap()
    gam = nc.dram_tensor("gam", [D], F32, kind="ExternalInput").ap()
    tidx = nc.dram_tensor("tidx", [1, 2048], F32, kind="ExternalInput").ap()
    btab = nc.dram_tensor("btab", [1, NSLOT], F32, kind="ExternalInput").ap()
    out = nc.dram_tensor("out", [TPC, D], F32, kind="ExternalOutput").ap()

    kvk_in = nc.dram_tensor("kvk_in", [4, 128, 1024], BF16).ap()
    kvv_in = nc.dram_tensor("kvv_in", [8, 128, 512], BF16).ap()
    kvk_ag = nc.dram_tensor("kvk_ag", [NC, 4, 128, 1024], BF16).ap()
    kvv_ag = nc.dram_tensor("kvv_ag", [NC, 8, 128, 512], BF16).ap()
    mx_in = nc.dram_tensor("mx_in", [1, 16], F32).ap()
    mx_out = nc.dram_tensor("mx_out", [1, 16], F32).ap()
    sc_d = nc.dram_tensor("sc_d", [1, 16], F32).ap()
    h1_d = nc.dram_tensor("h1_d", [TPC, D], F32, kind=dbg_kind).ap()
    xn_d = nc.dram_tensor("xn_d", [TPC, D], F32, kind=dbg_kind).ap()
    ao_d = nc.dram_tensor("ao_d", [TPC, D], F32, kind=dbg_kind).ap()
    fo_d = nc.dram_tensor("fo_d", [TPC, D], F32, kind=dbg_kind).ap()
    wfq_d = nc.dram_tensor("wfq_d", [16, 128, D], BF16).ap()

    if debug:
        dbg_q = nc.dram_tensor("dbg_q", [128, 16 * 1024], BF16, kind="ExternalOutput").ap()
        dbg_dk = nc.dram_tensor("dbg_dk", [128, 4096], BF16, kind="ExternalOutput").ap()
        dbg_dv = nc.dram_tensor("dbg_dv", [128, 4096], BF16, kind="ExternalOutput").ap()
        dbg_attn = nc.dram_tensor("dbg_attn", [128, 16 * 1024], BF16, kind="ExternalOutput").ap()
        dbg_tabc = nc.dram_tensor("dbg_tabc", [128, 2048], BF16, kind="ExternalOutput").ap()
        dbg_tabs = nc.dram_tensor("dbg_tabs", [128, 2048], BF16, kind="ExternalOutput").ap()
        dbg_kvk = nc.dram_tensor("dbg_kvk", [4, 128, 1024], BF16, kind="ExternalOutput").ap()
        dbg_kvv = nc.dram_tensor("dbg_kvv", [8, 128, 512], BF16, kind="ExternalOutput").ap()
        dbg_ln1t = nc.dram_tensor("dbg_ln1t", [128, 8192], BF16, kind="ExternalOutput").ap()
        dbg_scal = nc.dram_tensor("dbg_scal", [128, 16], F32, kind="ExternalOutput").ap()
        dbg_one = nc.dram_tensor("dbg_one", [1, 16], F32, kind="ExternalOutput").ap()
        dbg_wfq = nc.dram_tensor("dbg_wfq", [16, 128, D], BF16, kind="ExternalOutput").ap()
        dbg_xqt = nc.dram_tensor("dbg_xqt", [128, JT * 2048], BF16, kind="ExternalOutput").ap()
        dbg_fi = nc.dram_tensor("dbg_fi", [128, 1], F32, kind="ExternalOutput").ap()
        dbg_ang = nc.dram_tensor("dbg_ang", [128, 1024], F32, kind="ExternalOutput").ap()
        dbg_sin = nc.dram_tensor("dbg_sin", [128, 1024], F32, kind="ExternalOutput").ap()
        dbg_lnr = nc.dram_tensor("dbg_lnr", [128, D], BF16, kind="ExternalOutput").ap()
        dbg_st8 = nc.dram_tensor("dbg_st8", [128, 16], F32, kind="ExternalOutput").ap()
        dbg_wrow = nc.dram_tensor("dbg_wrow", [1, 128], F32, kind="ExternalOutput").ap()
        dbg_tokb = nc.dram_tensor("dbg_tokb", [128, 1024], F32, kind="ExternalOutput").ap()
        dbg_red = nc.dram_tensor("dbg_red", [128, 1024], F32, kind="ExternalOutput").ap()
        dbg_ang2 = nc.dram_tensor("dbg_ang2", [128, 1024], F32, kind="ExternalOutput").ap()
        dbg_fe = nc.dram_tensor("dbg_fe", [128, 1], F32, kind="ExternalOutput").ap()
    id_bf_d = nc.inline_tensor(_ID_BF, name="id_bf").ap()
    id_f_d = nc.inline_tensor(_ID_F32, name="id_f").ap()
    caus_d = nc.inline_tensor(np.asarray(_CAUSAL), name="caus").ap()

    with contextlib.ExitStack() as top:
        sems = {}
        for s in ("Q", "G", "PE", "V", "A", "P", "B", "K", "C", "S", "O",
                  "K0", "K1", "K2", "K3", "C0", "C1", "S0", "S1", "T0", "T1", "O0", "O1"):
            sems[s] = top.enter_context(nc.semaphore("s" + s))
        w.sems = sems

        q_all = top.enter_context(nc.sbuf_tensor("q_all", [128, 16 * 1024], BF16))
        diag_k = top.enter_context(nc.sbuf_tensor("diag_k", [128, 4 * 1024], BF16))
        diag_v = top.enter_context(nc.sbuf_tensor("diag_v", [128, JT * 512], BF16))
        gam_b = top.enter_context(nc.sbuf_tensor("gam_b", [128, D], F32))
        btab_b = top.enter_context(nc.sbuf_tensor("btab_b", [128, NSLOT], F32))
        scal = top.enter_context(nc.sbuf_tensor("scal", [128, 16], F32))
        id_bs = top.enter_context(nc.sbuf_tensor("id_bs", [128, 128], BF16))
        id_fs = top.enter_context(nc.sbuf_tensor("id_fs", [128, 128], F32))
        ones_bf = top.enter_context(nc.sbuf_tensor("ones_bf", [128, 128], BF16))
        pool = [top.enter_context(nc.sbuf_tensor(f"pool{i}", [128, D], F32))
                for i in range(2)]
        st8 = top.enter_context(nc.sbuf_tensor("st8", [128, 16], F32))
        one1 = top.enter_context(nc.sbuf_tensor("one1", [1, 16], F32))
        wsum = top.enter_context(nc.sbuf_tensor("wsum", [128, 1], F32))
        wrow = top.enter_context(nc.sbuf_tensor("wrow", [1, 128], F32))
        epsb = top.enter_context(nc.sbuf_tensor("epsb", [128, 4], F32))
        bnt = top.enter_context(nc.sbuf_tensor("bnt", [128, 32], F32))

        (I_SUM, I_NM, I_VAR, I_SD, I_RI, I_SQ, I_M2, I_RMX, I_T1, I_RI2,
         I_T2, I_T3) = range(12)

        def stx(i):
            return st8[:, i:i + 1]

        def vdrain():
            w.emit("v", lambda e: e.drain())

        def adrain():
            w.emit("a", lambda e: e.drain())

        def v_ts(dst, src, s1, s2=None, op0=ALU.mult, op1=None, sem=None):
            if op1 is None:
                return w.op("v", lambda e: e.tensor_scalar(
                    out=dst, in0=src, scalar1=s1, scalar2=None, op0=op0), sem)
            return w.op("v", lambda e: e.tensor_scalar(
                out=dst, in0=src, scalar1=s1, scalar2=s2, op0=op0, op1=op1), sem)

        def v_tt(dst, a, bb, op, sem=None):
            return w.op("v", lambda e: e.tensor_tensor(out=dst, in0=a, in1=bb, op=op), sem)

        def v_stt(dst, a, s, bb, op0, op1, sem=None):
            return w.op("v", lambda e: e.scalar_tensor_tensor(
                out=dst, in0=a, scalar=s, in1=bb, op0=op0, op1=op1), sem)

        def ln_stats(px, eps, nq_ready, rsem="Q"):
            if rsem != "V":
                w.wait("v", rsem, nq_ready)
            for c in range(4):
                w.op("v", lambda e, cc=c: e.bn_stats(
                    out=bnt[:, cc * 6:cc * 6 + 6],
                    in_=px[:, cc * 512:cc * 512 + 512]))
            vdrain()
            w.op("v", lambda e: e.bn_aggr(out=bnt[:, 24:26], in_=bnt[:, 0:24]))
            vdrain()
            nv2 = v_ts(stx(I_NM), bnt[:, 24:25], -1.0, sem="V")
            w.wait("a", "V", nv2)
            na2 = w.op("a", lambda e, ee=eps: e.activation(
                out=stx(I_SD), in_=bnt[:, 25:26], func=AF.Sqrt, bias=ee), "A")
            w.wait("v", "A", na2)
            nvr = w.op("v", lambda e: e.reciprocal(out=stx(I_RI), in_=stx(I_SD)), "V")
            vdrain()
            return nvr

        with contextlib.ExitStack() as sa:
            tab_c = sa.enter_context(nc.sbuf_tensor("tab_c", [128, 2048], BF16))
            tab_s = sa.enter_context(nc.sbuf_tensor("tab_s", [128, 2048], BF16))
            tab_cq = sa.enter_context(nc.sbuf_tensor("tab_cq", [128, 2048], BF16))
            tab_sq = sa.enter_context(nc.sbuf_tensor("tab_sq", [128, 2048], BF16))
            tokb_t = sa.enter_context(nc.sbuf_tensor("tokb_t", [128, 1024], F32))
            ang_t = sa.enter_context(nc.sbuf_tensor("ang_t", [128, 1024], F32))
            tr1_t = sa.enter_context(nc.sbuf_tensor("tr1_t", [128, 1024], F32))
            tr2_t = sa.enter_context(nc.sbuf_tensor("tr2_t", [128, 1024], F32))
            tokb = tokb_t[:, :]
            ang = ang_t[:, :]
            tr1 = tr1_t[:, :]
            tr2 = tr2_t[:, :]
            tr3 = tokb_t[:, :]
            sneg = ang_t[:, :]
            trk = sa.enter_context(nc.sbuf_tensor("trk", [128, 1024], I32))
            sin_f = sa.enter_context(nc.sbuf_tensor("sin_f", [128, 1024], F32))
            cos_f = sa.enter_context(nc.sbuf_tensor("cos_f", [128, 1024], F32))
            fi = sa.enter_context(nc.sbuf_tensor("fi", [128, 1], F32))
            fe = sa.enter_context(nc.sbuf_tensor("fe", [128, 1], F32))
            lnv = sa.enter_context(nc.sbuf_tensor("lnv", [128, 1], F32))
            tenk = sa.enter_context(nc.sbuf_tensor("tenk", [128, 1], F32))
            ln1t = sa.enter_context(nc.sbuf_tensor("ln1t", [128, 2 * 16 * 512], BF16))
            lnraw = sa.enter_context(nc.sbuf_tensor("lnraw", [128, 2 * D], BF16))
            wkt_s = sa.enter_context(nc.sbuf_tensor("wkt_s", [128, 16 * 512], BF16))
            wvt_s = sa.enter_context(nc.sbuf_tensor("wvt_s", [128, 16 * 512], BF16))
            wq_col = sa.enter_context(nc.sbuf_tensor("wq_col", [128, 2 * 2048], BF16))
            praw = sa.enter_context(nc.sbuf_tensor("praw", [128, 2 * 512], BF16))
            pswp = sa.enter_context(nc.sbuf_tensor("pswp", [128, 2 * 512], BF16))
            pr1 = sa.enter_context(nc.sbuf_tensor("pr1", [128, 512], BF16))
            pr2 = sa.enter_context(nc.sbuf_tensor("pr2", [128, 512], BF16))
            kvstg = sa.enter_context(nc.sbuf_tensor("kvstg", [128, 2 * 512], BF16))
            ps_tr = sa.enter_context(nc.psum_tensor("ps_tr", [128, 2048], BF16))
            ps_p = sa.enter_context(nc.psum_tensor("ps_p", [128, 2 * 512], F32))

            w.blk = 0
            w.op("q", lambda e: e.dma_start(out=id_fs[:], in_=id_f_d[:, :]), "Q", 16)
            w.op("q", lambda e: e.dma_start(
                out=tokb[0:64, :], in_=tidx[0:1, 0:1024].partition_broadcast(64)), "Q", 16)
            nQ_tok = w.op("q", lambda e: e.dma_start(
                out=tokb[64:128, :],
                in_=tidx[0:1, 1024:2048].partition_broadcast(64)), "Q", 16)
            w.op("q", lambda e: e.dma_start(
                out=gam_b[:], in_=gam[None, :].partition_broadcast(128)[:, 0, :]), "Q", 16)
            w.op("q", lambda e: e.dma_start(
                out=btab_b[:], in_=btab[0:1, :].partition_broadcast(128)), "Q", 16)
            nV_z = w.op("v", lambda e: e.memset(one1[0:1, 8:16], 0.0), "V")
            w.wait("q", "V", nV_z)
            w.op("q", lambda e: e.dma_start(out=mx_in[0:1, 0:8],
                                            in_=one1[0:1, 8:16]), "Q", 16)
            w.op("q", lambda e: e.dma_start(out=mx_in[0:1, 8:16],
                                            in_=one1[0:1, 8:16]), "Q", 16)
            nG_id = w.op("g", lambda e: e.dma_start(out=id_bs[:], in_=id_bf_d[:, :]), "G", 16)
            w.op("g", lambda e: e.dma_start(
                out=wkt_s[:], in_=wkt.rearrange("(ic p) o -> p ic o", p=128)), "G", 16)
            nG_w = w.op("g", lambda e: e.dma_start(
                out=wvt_s[:], in_=wvt.rearrange("(ic p) o -> p ic o", p=128)), "G", 16)
            w.op("v", lambda e: e.memset(ones_bf[:], 1.0), "V")
            w.op("v", lambda e: e.memset(epsb[:, 0:1], 1e-5))
            w.op("v", lambda e: e.memset(epsb[:, 1:2], 1e-6))
            nV_eps = w.op("v", lambda e: e.memset(epsb[:, 2:3], SC_HI), "V")
            w.wait("a", "V", nV_eps)

            w.op("g", lambda e: e.iota(
                out=fi[0:64, :], pattern=[[0, 1]], base=0, channel_multiplier=1,
                allow_small_or_imprecise_dtypes=True), "P")
            nP_iota = w.op("g", lambda e: e.iota(
                out=fi[64:128, :], pattern=[[0, 1]], base=0, channel_multiplier=1,
                allow_small_or_imprecise_dtypes=True), "P")
            nV_tenk = w.op("v", lambda e: e.memset(tenk[:], 10000.0), "V")
            w.wait("a", "V", nV_tenk)
            w.op("a", lambda e: e.activation(out=lnv[:], in_=tenk[:], func=AF.Ln), "A")
            adrain()
            w.wait("v", "P", nP_iota)
            nV_fe = v_ts(fe[:], fi[:], -1.0 / 64.0, sem="V")
            w.wait("a", "V", nV_fe)
            nA_fr = w.op("a", lambda e: e.activation(
                out=fi[:], in_=fe[:], func=AF.Exp, scale=lnv[:]), "A")
            w.wait("v", "Q", w.n["Q"])
            w.wait("v", "A", nA_fr)
            w.op("v", lambda e: e.tensor_copy(out=fe[:], in_=fi[:]))
            vdrain()
            nV_ang = v_ts(ang[:], tokb[:], fe[:], sem="V")
            if debug:
                w.dump(fe[:], dbg_fe[:, :])
                w.dump(fi[:], dbg_fi[:, :])
                w.dump(tokb, dbg_tokb[:, :])
                w.dump(ang, dbg_ang[:, :])
                w.dump(ang, dbg_ang2[:, :])

            def reduce_sin(src, dst):
                v_ts(tr1[:], src, INV2PI)
                vdrain()
                w.op("v", lambda e: e.tensor_copy(out=trk[:], in_=tr1[:]))
                vdrain()
                w.op("v", lambda e: e.tensor_copy(out=tr2[:], in_=trk[:]))
                vdrain()
                v_stt(tr1[:], tr2[:], -C1, src, ALU.mult, ALU.add)
                vdrain()
                v_stt(tr1[:], tr2[:], -C2, tr1[:], ALU.mult, ALU.add)
                vdrain()
                v_stt(tr1[:], tr2[:], -C3, tr1[:], ALU.mult, ALU.add)
                vdrain()
                v_ts(tr2[:], tr1[:], BOUND, op0=ALU.is_gt)
                v_ts(tr3[:], tr1[:], -BOUND, op0=ALU.is_lt)
                vdrain()
                v_tt(tr2[:], tr2[:], tr3[:], ALU.subtract)
                vdrain()
                v_stt(tr1[:], tr2[:], -C1, tr1[:], ALU.mult, ALU.add)
                vdrain()
                v_stt(tr1[:], tr2[:], -C2, tr1[:], ALU.mult, ALU.add)
                vdrain()
                nv = v_stt(tr1[:], tr2[:], -C3, tr1[:], ALU.mult, ALU.add, "V")
                vdrain()
                if debug:
                    w.dump(tr1, dbg_red[:, :])
                w.wait("a", "V", nv)
                return w.op("a", lambda e, d=dst: e.activation(
                    out=d[:], in_=tr1[:], func=AF.Sin), "A")

            vdrain()
            nA_sin = reduce_sin(ang[:], sin_f)
            if debug:
                w.wait("g", "A", nA_sin)
                w.dump(sin_f[:], dbg_sin[:, :])
            w.wait("v", "A", nA_sin)
            v_ts(ang[:], ang[:], PI2_HI, PI2_LO, ALU.add, ALU.add)
            vdrain()
            nA_cos = reduce_sin(ang[:], cos_f)
            w.wait("v", "A", nA_cos)
            v_ts(sneg[:], sin_f[:], -1.0)
            v_ts(tr1[:], cos_f[:], SC_LO)
            v_ts(tr2[:], sin_f[:], SC_LO)
            vdrain()
            nV_tab = v_ts(tr3[:], sneg[:], SC_LO, sem="V")
            vdrain()
            w.wait("g", "V", nV_tab)
            nG_tab = None
            for dst, half, srcp in ((tab_c, 0, cos_f), (tab_c, 1, cos_f),
                                    (tab_s, 0, sneg), (tab_s, 1, sin_f),
                                    (tab_cq, 0, tr1), (tab_cq, 1, tr1),
                                    (tab_sq, 0, tr3), (tab_sq, 1, tr2)):
                for th in range(2):
                    nG_tab = w.op("g", lambda e, d=dst, h=half, s=srcp, t=th:
                                  e.dma_start(
                        out=d[64 * h:64 * h + 64, t * 1024:t * 1024 + 1024],
                        in_=s[64 * t:64 * t + 64, :]), "G", 16)

            ln1t_v = ln1t.rearrange("p (b ic f) -> p b ic f", b=2, ic=16)

            def lnsl(gp, i, t0=None):
                base = gp * 8192 + i * 512
                if t0 is None:
                    return ln1t[:, base:base + 512]
                return ln1t[:, base + t0:base + t0 + 128]

            busy = {"pv": {0: None, 1: None}, "pa": {0: None, 1: None},
                    "tr": None}
            projdone = {0: None, 1: None}

            dbg_first = {"done": False}

            def ln_transpose(src_dram, row0, gslot, gp):
                npe_free = projdone.get(gp)
                par = gslot % 2
                px = pool[par][:]
                lw = lnraw[:, par * D:par * D + D]
                w.wait("q", "V", busy["pv"][par])
                w.wait("q", "A", busy["pa"][par])
                nq = w.op("q", lambda e, r=row0, sd=src_dram: e.dma_start(
                    out=px, in_=sd[r:r + 128, :]), f"T{par}", 16)
                ln_stats(px, epsb[:, 0:1], nq, rsem=f"T{par}")
                busy["pa"][par] = w.n["A"]
                nv = w.op("v", lambda e, p=px, ll=lw: e.tensor_scalar(
                    out=ll, in0=p, scalar1=stx(I_NM), scalar2=stx(I_RI),
                    op0=ALU.add, op1=ALU.mult), "V")
                busy["pv"][par] = nv
                w.wait("t", "V", nv)
                w.wait("t", "G", nG_id)
                w.wait("t", "V", busy["tr"])
                npe = None
                for ic in range(16):
                    npe = w.op("t", lambda e, i=ic, ll=lw: e.transpose(
                        out=ps_tr[:, i * 128:i * 128 + 128],
                        in_=ll[:, i * 128:i * 128 + 128], identity=id_bs[:]),
                        "PE" if ic == 15 else None)
                w.wait("v", "PE", npe)
                if npe_free is not None:
                    w.wait("v", "PE", npe_free)
                nev = w.op("v", lambda e, g=gslot, b=gp: e.tensor_copy(
                    out=ln1t_v[:, b:b + 1, :, g * 128:g * 128 + 128],
                    in_=ps_tr[:]), "V")
                busy["tr"] = nev
                if debug and not dbg_first["done"]:
                    dbg_first["done"] = True
                    w.dump(lw, dbg_lnr[:, :])
                    w.dump(st8[:], dbg_st8[:, :])
                return nev

            rope_busy = {0: None, 1: None}

            def rope(par, tabc, tabs, tcol, dst_ap_fn, na_raw):
                raw = praw[:, par * 512: par * 512 + 512]
                swp = pswp[:, par * 512: par * 512 + 512]
                w.wait("g", "A", na_raw)
                if rope_busy[par] is not None:
                    w.wait("g", "V", rope_busy[par])
                w.op("g", lambda e: e.dma_start(out=swp[0:64, :], in_=raw[64:128, :]), f"S{par}", 16)
                ng = w.op("g", lambda e: e.dma_start(out=swp[64:128, :], in_=raw[0:64, :]), f"S{par}", 16)
                w.wait("v", f"S{par}", ng)
                w.wait("v", "A", na_raw)
                v_tt(pr1[:], swp, tabs[:, tcol:tcol + 512], ALU.mult)
                v_tt(pr2[:], raw, tabc[:, tcol:tcol + 512], ALU.mult)
                nv = w.op("v", lambda e: e.tensor_tensor(
                    out=dst_ap_fn(), in0=pr1[:], in1=pr2[:], op=ALU.add), "V")
                rope_busy[par] = nv
                return nv

            def proj16(psum_ap, lhs_fn, rhs_fn, sem="PE"):
                npe = None
                for ic in range(16):
                    npe = w.op("t", lambda e, i=ic: e.matmul(
                        out=psum_ap, lhsT=lhs_fn(i), rhs=rhs_fn(i),
                        start=(i == 0), stop=(i == 15)),
                        sem if ic == 15 else None)
                return npe

            w.wait("t", "G", nG_w)
            last_o = {}
            nA_evac = {0: None, 1: None}
            for grp in range(2):
                gp = grp % 2
                nev = None
                for tl in range(4):
                    nev = ln_transpose(xs, (grp * 4 + tl) * 128, tl, gp)
                w.wait("t", "V", nev)
                if debug and grp == 0:
                    w.dump(ln1t[:, 0:8192], dbg_ln1t[:, :])
                for g in range(4):
                    par = g % 2
                    if nA_evac[par] is not None:
                        w.wait("t", "A", nA_evac[par])
                    npe = proj16(
                        ps_p[:, par * 512:par * 512 + 512],
                        lambda i, gg=g: wkt_s[:, i * 512 + gg * 128:
                                              i * 512 + gg * 128 + 128],
                        lambda i, b=gp: lnsl(b, i))
                    w.wait("a", "PE", npe)
                    na = w.op("a", lambda e, p=par: e.activation(
                        out=praw[:, p * 512:p * 512 + 512],
                        in_=ps_p[:, p * 512:p * 512 + 512], func=AF.Identity), "A")
                    nA_evac[par] = na
                    w.wait("v", "G", nG_tab)
                    w.wait("v", f"O{par}", last_o.get(par, 0))
                    nv = rope(par, tab_c, tab_s, grp * 512,
                              lambda p=par: kvstg[:, p * 512:p * 512 + 512], na)
                    w.wait("g", "V", nv)
                    last_o[par] = w.op("g", lambda e, gg=g, gr=grp, p=par: e.dma_start(
                        out=kvk_in[gg, :, gr * 512:gr * 512 + 512],
                        in_=kvstg[:, p * 512:p * 512 + 512]), f"O{par}", 16)
                for tl in range(4):
                    par = tl % 2
                    if nA_evac[par] is not None:
                        w.wait("t", "A", nA_evac[par])
                    npe = proj16(
                        ps_p[:, par * 512:par * 512 + 512],
                        lambda i, t_=tl, b=gp: lnsl(b, i, t_ * 128),
                        lambda i: wvt_s[:, i * 512:i * 512 + 512])
                    w.wait("a", "PE", npe)
                    w.wait("a", f"O{par}", last_o.get(par, 0))
                    na = w.op("a", lambda e, p=par: e.activation(
                        out=kvstg[:, p * 512:p * 512 + 512],
                        in_=ps_p[:, p * 512:p * 512 + 512], func=AF.Identity), "A")
                    nA_evac[par] = na
                    w.wait("g", "A", na)
                    last_o[par] = w.op("g", lambda e, t_=tl, gr=grp, p=par: e.dma_start(
                        out=kvv_in[gr * 4 + t_, :, :],
                        in_=kvstg[:, p * 512:p * 512 + 512]), f"O{par}", 16)
                projdone[gp] = npe
            w.wait("g", "O0", w.n["O0"])
            w.wait("g", "O1", w.n["O1"])
            w.op("g", lambda e: e.collective_compute(
                "AllGather", ALU.bypass, replica_groups=[list(range(NC))],
                ins=[kvk_in], outs=[kvk_ag]), "P")
            nP_ag = w.op("g", lambda e: e.collective_compute(
                "AllGather", ALU.bypass, replica_groups=[list(range(NC))],
                ins=[kvv_in], outs=[kvv_ag]), "P")

            for grp in range(2):
                gp = grp % 2
                nev = None
                for tl in range(4):
                    nev = ln_transpose(xq, (grp * 4 + tl) * 128, tl, gp)
                w.wait("t", "V", nev)
                ncol = {0: 0, 1: 0}
                npe_q = {0: None, 1: None}
                for h in range(16):
                    parw = h % 2
                    w.wait("g", "PE", npe_q[parw])
                    ncol[parw] = w.op("g", lambda e, hh=h, p=parw: e.dma_start(
                        out=wq_col[:, p * 2048:p * 2048 + 2048],
                        in_=wqt.rearrange("(ic p) o -> p ic o", p=128)
                        [:, :, hh * 128:hh * 128 + 128]), f"C{parw}", 16)
                    par = h % 2
                    if nA_evac[par] is not None:
                        w.wait("t", "A", nA_evac[par])
                    w.wait("t", f"C{parw}", ncol[parw])
                    npe = proj16(
                        ps_p[:, par * 512:par * 512 + 512],
                        lambda i, pw=parw: wq_col[:, pw * 2048 + i * 128:
                                                  pw * 2048 + i * 128 + 128],
                        lambda i, b=gp: lnsl(b, i))
                    npe_q[parw] = npe
                    w.wait("a", "PE", npe)
                    na = w.op("a", lambda e, p=par: e.activation(
                        out=praw[:, p * 512:p * 512 + 512],
                        in_=ps_p[:, p * 512:p * 512 + 512], func=AF.Identity), "A")
                    nA_evac[par] = na
                    rope(par, tab_cq, tab_sq, 1024 + grp * 512,
                         lambda hh=h, g=grp: q_all[:, hh * 1024 + g * 512:
                                                   hh * 1024 + g * 512 + 512], na)
                for g in range(4):
                    par = g % 2
                    if nA_evac[par] is not None:
                        w.wait("t", "A", nA_evac[par])
                    npe = proj16(
                        ps_p[:, par * 512:par * 512 + 512],
                        lambda i, gg=g: wkt_s[:, i * 512 + gg * 128:
                                              i * 512 + gg * 128 + 128],
                        lambda i, b=gp: lnsl(b, i))
                    w.wait("a", "PE", npe)
                    na = w.op("a", lambda e, p=par: e.activation(
                        out=praw[:, p * 512:p * 512 + 512],
                        in_=ps_p[:, p * 512:p * 512 + 512], func=AF.Identity), "A")
                    nA_evac[par] = na
                    rope(par, tab_c, tab_s, 1024 + grp * 512,
                         lambda gg=g, gr=grp: diag_k[:, gg * 1024 + gr * 512:
                                                     gg * 1024 + gr * 512 + 512], na)
                for tl in range(4):
                    par = tl % 2
                    if nA_evac[par] is not None:
                        w.wait("t", "A", nA_evac[par])
                    npe = proj16(
                        ps_p[:, par * 512:par * 512 + 512],
                        lambda i, t_=tl, b=gp: lnsl(b, i, t_ * 128),
                        lambda i: wvt_s[:, i * 512:i * 512 + 512])
                    w.wait("a", "PE", npe)
                    na = w.op("a", lambda e, t_=tl, gr=grp, p=par: e.activation(
                        out=diag_v[:, (gr * 4 + t_) * 512:(gr * 4 + t_) * 512 + 512],
                        in_=ps_p[:, p * 512:p * 512 + 512], func=AF.Identity), "A")
                    nA_evac[par] = na
                projdone[gp] = npe
            if debug:
                w.wait("g", "V", w.n["V"])
                w.wait("g", "A", w.n["A"])
                for dst_, src_ in ((dbg_tabc, tab_c), (dbg_tabs, tab_s)):
                    w.op("g", lambda e, d=dst_, s=src_: e.dma_start(
                        out=d[:, :], in_=s[:]), "G", 16)
                w.op("g", lambda e: e.dma_start(out=dbg_q[:, :], in_=q_all[:]), "G", 16)
                w.op("g", lambda e: e.dma_start(out=dbg_dk[:, :], in_=diag_k[:]), "G", 16)
                w.op("g", lambda e: e.dma_start(out=dbg_dv[:, :], in_=diag_v[:]), "G", 16)
                w.op("g", lambda e: e.dma_start(out=dbg_kvk[:, :, :], in_=kvk_in[:, :, :]), "G", 16)
                w.op("g", lambda e: e.dma_start(out=dbg_kvv[:, :, :], in_=kvv_in[:, :, :]), "G", 16)
            w.fence()

            with nc.Block() as BA:
                @BA.sync
                def _(e):
                    for f in w.cmds[0]["q"]:
                        f(e)

                @BA.gpsimd
                def _(e):
                    for f in w.cmds[0]["g"]:
                        f(e)

                @BA.vector
                def _(e):
                    for f in w.cmds[0]["v"]:
                        f(e)

                @BA.scalar
                def _(e):
                    for f in w.cmds[0]["a"]:
                        f(e)

                @BA.tensor
                def _(e):
                    for f in w.cmds[0]["t"]:
                        f(e)

        with contextlib.ExitStack() as sb:
            kb = sb.enter_context(nc.sbuf_tensor("kb", [128, 1024], BF16))
            vb = sb.enter_context(nc.sbuf_tensor("vb", [128, 1024], BF16))
            expst = sb.enter_context(nc.sbuf_tensor("expst", [128, 2048], BF16))
            attn = sb.enter_context(nc.sbuf_tensor("attn", [128, 16 * 1024], BF16))
            rcp = sb.enter_context(nc.sbuf_tensor("rcp", [128, 1024], F32))
            wot_col = sb.enter_context(nc.sbuf_tensor("wot_col", [128, 16 * 512], BF16))
            wfq_col = wot_col
            caus_s = sb.enter_context(nc.sbuf_tensor("caus_s", [128, 1024], BF16))
            sq_s = sb.enter_context(nc.sbuf_tensor("sq_s", [128, D], F32))
            xqt = sb.enter_context(nc.sbuf_tensor("xqt", [128, JT * 2048], BF16))
            wq_stage = xqt[:, 0:D]
            aout_stg = None
            ki_b = sb.enter_context(nc.sbuf_tensor("ki_b", [128, D], I32))
            sc2 = [sb.enter_context(nc.psum_tensor(f"sc2_{i}", [128, 1024], F32))
                   for i in range(2)]
            acc2 = sb.enter_context(nc.psum_tensor("acc2", [128, 1024], F32))
            sums2 = sb.enter_context(nc.psum_tensor("sums2", [128, 1024], F32))
            aout_stg = rcp

            w.blk = 1
            nG_caus = w.op("g", lambda e: e.dma_start(out=caus_s[:], in_=caus_d[:, :]), "G", 16)
            w.wait("v", "G", nG_caus)
            kvk_v = kvk_ag.rearrange("r g p t -> r p g t")
            kb_v = kb.rearrange("p (c g t) -> p c g t", c=4, g=2)
            q_v = q_all.rearrange("p (h f) -> p h f", h=16)
            attn_v = attn.rearrange("p (h f) -> p h f", h=16)

            wst = {"ic1": 0, "ic2": 0, "v_done": None, "a_done": None}

            def wprep1():
                ic = wst["ic1"]; wst["ic1"] += 1
                if wst["v_done"] is not None:
                    w.wait("g", "V", wst["v_done"])
                nq = w.op("g", lambda e, i=ic: e.dma_start(
                    out=pool[0][:], in_=wft[i * 128:i * 128 + 128, :]), "G", 16)
                w.wait("v", "G", nq)
                nv = w.op("v", lambda e: e.tensor_reduce(
                    out=stx(I_T1), in_=pool[0][:], axis=AX.X, op=ALU.add,
                    apply_absolute_value=True), "V")
                vdrain()
                if ic == 0:
                    nv = w.op("v", lambda e: e.tensor_copy(out=wsum[:], in_=stx(I_T1)), "V")
                else:
                    nv = v_tt(wsum[:], wsum[:], stx(I_T1), ALU.add, "V")
                vdrain()
                wst["v_done"] = nv

            def wprep_scale():
                w.wait("g", "V", wst["v_done"])
                nq = w.op("g", lambda e: e.dma_start(out=wrow[:, :], in_=wsum[:, :]), "G", 16)
                w.wait("v", "G", nq)
                nvr = w.op("v", lambda e: e.tensor_reduce(
                    out=one1[0:1, 0:1], in_=wrow[:, :], axis=AX.X, op=ALU.add), "V")
                vdrain()
                w.wait("g", "V", nvr)
                nq2 = w.op("g", lambda e: e.dma_start(
                    out=sc_d[0:1, 4:5], in_=one1[0:1, 0:1]), "G", 16)
                w.wait("g", "G", nq2)
                nq3 = w.op("g", lambda e: e.dma_start(
                    out=scal[:, 4:5],
                    in_=sc_d[0:1, 4:5].partition_broadcast(128)), "G", 16)
                w.wait("v", "G", nq3)
                w.op("v", lambda e: e.tensor_scalar(
                    out=scal[:, 0:1], in0=scal[:, 4:5],
                    scalar1=1.0 / 4194304.0, scalar2=1e-5,
                    op0=ALU.mult, op1=ALU.max))
                vdrain()
                nv3 = w.op("v", lambda e: e.tensor_scalar(
                    out=scal[:, 1:2], in0=scal[:, 0:1], scalar1=0.5,
                    scalar2=None, op0=ALU.mult), "V")
                vdrain()
                wst["v_done"] = nv3
                return nv3

            def wprep2(nq_tau):
                ic = wst["ic2"]; wst["ic2"] += 1
                if wst["v_done"] is not None:
                    w.wait("q", "V", wst["v_done"])
                if wst["a_done"] is not None:
                    w.wait("q", "A", wst["a_done"])
                nq = w.op("q", lambda e, i=ic: e.dma_start(
                    out=pool[0][:], in_=wft[i * 128:i * 128 + 128, :]), "Q", 16)
                w.wait("a", "Q", nq)
                na0 = w.op("a", lambda e: e.activation(
                    out=sq_s[:], in_=pool[0][:], func=AF.Abs), "A")
                na = w.op("a", lambda e: e.activation(
                    out=pool[1][:], in_=pool[0][:], func=AF.Sign), "A")
                wst["a_done"] = na
                w.wait("v", "A", na)
                v_ts(sq_s[:], sq_s[:], scal[:, 1:2], op0=ALU.is_ge)
                nv2 = v_tt(wq_stage, sq_s[:], pool[1][:], ALU.mult, "V")
                wst["v_done"] = nv2
                w.wait("g", "V", nv2)
                return w.op("g", lambda e, i=ic: e.dma_start(
                    out=wfq_d[i:i + 1, :, :], in_=wq_stage), "G", 16)

            SKIPW = bool(os.environ.get("SKIPW"))
            JMAX = int(os.environ.get("JMAX", "8"))
            SKIP456 = bool(os.environ.get("SKIP456"))
            ss = 0
            li = 0
            nA_exp = {}
            nPE_sc = {}
            nPE_av = {}
            li_slot = {}
            load_q = {}
            nV_norm = None
            nq_tau = None
            ng_wfq = 0
            for j in range(min(JT, JMAX)):
                if SKIPW:
                    pass
                elif j <= 3:
                    for _ in range(4):
                        wprep1()
                elif j == 4:
                    nq_tau = wprep_scale()
                else:
                    for _ in range(5 if j < 7 else 6):
                        ng_wfq = wprep2(nq_tau)
                for p in range(2):
                    g0 = 2 * p
                    pend = None

                    def emit_av(info):
                        (pdiag, pj, pg0, ppar, plp, pdeps, pfirst, plast,
                         pss) = info
                        if pfirst and nV_norm is not None:
                            w.wait("t", "V", nV_norm)
                        for i in range(2):
                            w.wait("t", pdeps[i][0], pdeps[i][1])
                            g = pg0 + i
                            if pdiag:
                                vl = diag_v[:, pj * 512 + g * 128:
                                            pj * 512 + g * 128 + 128]
                            else:
                                vl = vb[:, plp * 256 + i * 128:
                                        plp * 256 + i * 128 + 128]
                            w.op("t", lambda e, vv=vl, pp=ppar, ii=i, f=pfirst,
                                 la=plast: e.matmul(
                                out=acc2[:, ii * 512:ii * 512 + 512], lhsT=vv,
                                rhs=expst[:, pp * 1024 + ii * 512:
                                          pp * 1024 + ii * 512 + 512],
                                start=f, stop=la), None)
                            nPE_av[pss] = w.op("t", lambda e, pp=ppar, ii=i,
                                               f=pfirst, la=plast: e.matmul(
                                out=sums2[:, ii * 512:ii * 512 + 512],
                                lhsT=ones_bf[:],
                                rhs=expst[:, pp * 1024 + ii * 512:
                                          pp * 1024 + ii * 512 + 512],
                                start=f, stop=la),
                                "PE" if i == 1 else None)

                    for s in range(L[j]):
                        diag = (s == 0)
                        par = ss % 2
                        if diag:
                            cur_li = None
                            lp = None
                        else:
                            kt = s - 1
                            r, loc = kt // 8, kt % 8
                            lpar = li % 4
                            if li - 4 >= 0:
                                w.wait("q", "PE", nPE_av.get(li_slot[li - 4], 0))
                            w.wait("q", "P", nP_ag)
                            w.op("q", lambda e, rr=r, lo=loc, gg=g0, lpx=lpar: e.dma_start(
                                out=kb_v[:, lpx:lpx + 1, :, :],
                                in_=kvk_v[rr:rr + 1, :, gg:gg + 2,
                                          lo * 128:lo * 128 + 128]), f"K{lpar}", 16)
                            nqv = w.op("q", lambda e, rr=r, lo=loc, gg=g0, lpx=lpar: e.dma_start(
                                out=vb[:, lpx * 256:lpx * 256 + 256],
                                in_=kvv_ag[rr:rr + 1, lo:lo + 1, :,
                                           gg * 128:gg * 128 + 256]), f"K{lpar}", 16)
                            load_q[li] = nqv
                            li_slot[li] = ss
                            cur_li = li
                            lp = lpar
                            li += 1
                        if not diag:
                            w.wait("t", f"K{lp}", load_q[cur_li])
                        if ss >= 2:
                            w.wait("t", "A", nA_exp[ss - 2])
                        for i in range(2):
                            g = g0 + i
                            if diag:
                                kl = diag_k[:, g * 1024 + j * 128:
                                            g * 1024 + j * 128 + 128]
                            else:
                                kl = kb[:, lp * 256 + i * 128: lp * 256 + i * 128 + 128]
                            nsc = w.op("t", lambda e, kk=kl, gg=g, pp=par, ii=i, jj=j: e.matmul(
                                out=sc2[pp][:, ii * 512:ii * 512 + 512],
                                lhsT=kk,
                                rhs=q_v[:, 4 * gg:4 * gg + 4, jj * 128:jj * 128 + 128],
                                start=True, stop=True),
                                "PE" if i == 1 else None)
                        if pend is not None:
                            emit_av(pend)
                        bidx = BOFF[j] + s
                        w.wait("a", "PE", nsc)
                        if ss >= 2:
                            w.wait("a", "PE", nPE_av.get(ss - 2, 0))
                        nA_exp[ss] = w.op("a", lambda e, pp=par, bi=bidx: e.activation(
                            out=expst[:, pp * 1024:pp * 1024 + 1024],
                            in_=sc2[pp][:], func=AF.Exp, scale=epsb[:, 2:3],
                            bias=btab_b[:, bi:bi + 1]), "A")
                        if diag:
                            w.wait("v", "A", nA_exp[ss])
                            nvm = w.op("v", lambda e, pp=par: e.tensor_tensor(
                                out=expst[:, pp * 1024:pp * 1024 + 1024],
                                in0=expst[:, pp * 1024:pp * 1024 + 1024],
                                in1=caus_s[:], op=ALU.mult), "V")
                            deps = [("V", nvm), ("V", nvm)]
                        else:
                            deps = [("A", nA_exp[ss]), ("A", nA_exp[ss])]
                        pend = (diag, j, g0, par, lp, deps,
                                s == 0, s == L[j] - 1, ss)
                        ss += 1
                    emit_av(pend)
                    pend = None
                    w.wait("v", "PE", nPE_av[ss - 1])
                    w.op("v", lambda e: e.reciprocal(out=rcp[:], in_=sums2[:]))
                    vdrain()
                    nV_norm = w.op("v", lambda e, pp=p, jj=j: e.tensor_tensor(
                        out=attn_v[:, 8 * pp:8 * pp + 8, jj * 128:jj * 128 + 128],
                        in0=acc2[:], in1=rcp[:], op=ALU.mult), "V")

            w.fence()

            if SKIP456:
                for j in range(JT):
                    nv0 = w.op("v", lambda e: e.memset(pool[0][:], 0.0), "V")
                    w.wait("q", "V", nv0)
                    w.op("q", lambda e, jj=j: e.dma_start(
                        out=out[jj * 128:jj * 128 + 128, :], in_=pool[0][:]), "Q", 16)
                w.wait("q", "Q", w.n["Q"])
                w.wait("q", "K", w.n["K"])
                w.wait("g", "G", w.n["G"])
                w.wait("g", "S", w.n["S"])
                w.wait("g", "O0", w.n["O0"])
            w.wait("g", "O1", w.n["O1"])
            _skip456 = SKIP456
            nA_stg = {0: None, 1: None}
            if _skip456:
                JT_RANGE = []
            else:
                JT_RANGE = list(range(JT))
            nQ_stg = {0: 0, 1: 0}
            npe_ob = None
            for ob in (range(4) if not _skip456 else []):
                if npe_ob is not None:
                    w.wait("g", "PE", npe_ob)
                ncol = w.op("g", lambda e, o=ob: e.dma_start(
                    out=wot_col[:],
                    in_=wot.rearrange("(ic p) o -> p ic o", p=128)
                    [:, :, o * 512:o * 512 + 512]), "C", 16)
                for j in range(JT):
                    par = j % 2
                    w.wait("t", "C", ncol)
                    if nA_stg[par] is not None:
                        w.wait("t", "A", nA_stg[par])
                    npe_ob = proj16(
                        acc2[:, par * 512:par * 512 + 512],
                        lambda i, jj=j: attn_v[:, i:i + 1, jj * 128:jj * 128 + 128],
                        lambda i: wot_col[:, i * 512:i * 512 + 512])
                    w.wait("a", "PE", npe_ob)
                    w.wait("a", f"T{par}", nQ_stg[par])
                    na = w.op("a", lambda e, pp=par: e.activation(
                        out=aout_stg[:, pp * 512:pp * 512 + 512],
                        in_=acc2[:, pp * 512:pp * 512 + 512], func=AF.Identity), "A")
                    nA_stg[par] = na
                    w.wait("q", "A", na)
                    nQ_stg[par] = w.op("q", lambda e, jj=j, o=ob, pp=par: e.dma_start(
                        out=ao_d[jj * 128:jj * 128 + 128, o * 512:o * 512 + 512],
                        in_=aout_stg[:, pp * 512:pp * 512 + 512]), f"T{par}", 16)
            prev_v = None
            for j in JT_RANGE:
                if prev_v is not None:
                    w.wait("q", "V", prev_v)
                w.wait("q", "T0", nQ_stg[0])
                w.wait("q", "T1", nQ_stg[1])
                nq1 = w.op("q", lambda e, jj=j: e.dma_start(
                    out=pool[0][:], in_=ao_d[jj * 128:jj * 128 + 128, :]), "Q", 16)
                nq2 = w.op("q", lambda e, jj=j: e.dma_start(
                    out=pool[1][:], in_=xq[jj * 128:jj * 128 + 128, :]), "Q", 16)
                ln_stats(pool[0][:], epsb[:, 0:1], nq2)
                v_ts(sq_s[:], pool[0][:], stx(I_NM), stx(I_RI), ALU.add, ALU.mult)
                w.wait("v", "Q", nq2)
                nv = v_tt(pool[0][:], sq_s[:], pool[1][:], ALU.add, "V")
                w.wait("q", "V", nv)
                w.op("q", lambda e, jj=j: e.dma_start(
                    out=h1_d[jj * 128:jj * 128 + 128, :], in_=pool[0][:]), "Q", 16)
                ln_stats(pool[0][:], epsb[:, 0:1], nv, rsem="V")
                nv_f = v_ts(pool[1][:], pool[0][:], stx(I_NM), stx(I_RI),
                            ALU.add, ALU.mult, sem="V")
                w.wait("a", "V", nv_f)
                na = w.op("a", lambda e: e.activation(
                    out=sq_s[:], in_=pool[1][:], func=AF.Square,
                    accum_out=stx(I_T1)), "A")
                w.wait("v", "A", na)
                v_ts(stx(I_T2), stx(I_T1), 1.0 / 2048.0)
                vdrain()
                nv2 = w.op("v", lambda e: e.tensor_copy(out=stx(I_T3), in_=stx(I_T2)), "V")
                vdrain()
                w.wait("a", "V", nv2)
                na2 = w.op("a", lambda e: e.activation(
                    out=stx(I_SD), in_=stx(I_T3), func=AF.Sqrt,
                    bias=epsb[:, 1:2]), "A")
                w.wait("v", "A", na2)
                w.op("v", lambda e: e.reciprocal(out=stx(I_RI2), in_=stx(I_SD)))
                vdrain()
                v_stt(pool[1][:], pool[1][:], stx(I_RI2), gam_b[:], ALU.mult, ALU.mult)
                nv3 = w.op("v", lambda e: e.tensor_reduce(
                    out=stx(I_T1), in_=pool[1][:], axis=AX.X, op=ALU.max,
                    apply_absolute_value=True), "V")
                vdrain()
                if j == 0:
                    nv4 = w.op("v", lambda e: e.tensor_copy(out=stx(I_RMX), in_=stx(I_T1)), "V")
                else:
                    nv4 = v_tt(stx(I_RMX), stx(I_RMX), stx(I_T1), ALU.max, "V")
                vdrain()
                prev_v = nv4
                w.wait("q", "V", nv4)
                w.op("q", lambda e, jj=j: e.dma_start(
                    out=xn_d[jj * 128:jj * 128 + 128, :], in_=pool[1][:]), "Q", 16)

            if not _skip456:
                w.wait("q", "V", prev_v)
                nqm = w.op("q", lambda e: e.dma_start(
                    out=wrow[:, :], in_=st8[:, I_RMX:I_RMX + 1]), "Q", 16)
                w.wait("v", "Q", nqm)
                nvm = w.op("v", lambda e: e.tensor_reduce(
                    out=one1[0:1, 4:5], in_=wrow[:, :], axis=AX.X, op=ALU.max), "V")
                vdrain()
                w.wait("q", "V", nvm)
                nqm2 = w.op("q", lambda e: e.dma_start(
                    out=mx_in[0:1, 0:1], in_=one1[0:1, 4:5]), "Q", 16)
                w.wait("g", "Q", nqm2)
                nP_ar = w.op("g", lambda e: e.collective_compute(
                    "AllReduce", ALU.max, replica_groups=[list(range(NC))],
                    ins=[mx_in], outs=[mx_out]), "P")
                w.wait("q", "P", nP_ar)
                nqm3 = w.op("q", lambda e: e.dma_start(
                    out=scal[:, 5:6],
                    in_=mx_out[0:1, 0:1].partition_broadcast(128)), "Q", 16)
                w.wait("v", "Q", nqm3)
                w.op("v", lambda e: e.tensor_scalar(
                    out=scal[:, 6:7], in0=scal[:, 5:6], scalar1=1e-5,
                    scalar2=None, op0=ALU.max))
                vdrain()
                w.op("v", lambda e: e.reciprocal(out=scal[:, 7:8], in_=scal[:, 6:7]))
                vdrain()
                w.op("v", lambda e: e.tensor_scalar(
                    out=scal[:, 2:3], in0=scal[:, 7:8], scalar1=127.0,
                    scalar2=None, op0=ALU.mult))
                w.op("v", lambda e: e.tensor_scalar(
                    out=scal[:, 8:9], in0=scal[:, 6:7], scalar1=INV127,
                    scalar2=None, op0=ALU.mult))
                vdrain()
                nq_sc = w.op("v", lambda e: e.tensor_tensor(
                    out=scal[:, 3:4], in0=scal[:, 8:9], in1=scal[:, 0:1],
                    op=ALU.mult), "V")
                vdrain()

                prev_v = None
                prev_evac = None
                for j in range(JT):
                    if prev_v is not None:
                        w.wait("q", "V", prev_v)
                    nq1 = w.op("q", lambda e, jj=j: e.dma_start(
                        out=pool[0][:], in_=xn_d[jj * 128:jj * 128 + 128, :]), "Q", 16)
                    w.wait("v", "Q", nq1)
                    w.wait("v", "V", nq_sc) if False else None
                    v_ts(pool[1][:], pool[0][:], scal[:, 2:3])
                    v_ts(pool[1][:], pool[1][:], -128.0, 127.0, ALU.max, ALU.min)
                    w.op("v", lambda e: e.tensor_copy(out=ki_b[:], in_=pool[1][:]))
                    w.op("v", lambda e: e.tensor_copy(out=sq_s[:], in_=ki_b[:]))
                    v_tt(pool[1][:], pool[1][:], sq_s[:], ALU.subtract)
                    v_ts(pool[0][:], pool[1][:], 0.5, op0=ALU.is_ge)
                    v_ts(pool[1][:], pool[1][:], -0.5, op0=ALU.is_lt)
                    v_tt(pool[0][:], pool[0][:], pool[1][:], ALU.subtract)
                    nv = v_tt(pool[1][:], sq_s[:], pool[0][:], ALU.add, "V")
                    w.wait("t", "V", nv)
                    if prev_evac is not None:
                        w.wait("t", "V", prev_evac)
                    npe_h = {}
                    for half in range(2):
                        for ic in range(8):
                            icg = half * 8 + ic
                            npe_h[half] = w.op("t", lambda e, i=icg, hh=half, ii=ic: e.transpose(
                                out=sc2[hh][:, ii * 128:ii * 128 + 128],
                                in_=pool[1][:, i * 128:i * 128 + 128],
                                identity=id_fs[:]), "PE" if ic == 7 else None)
                    nevs = []
                    w.wait("v", "G", ng_wfq)
                    for half in range(2):
                        w.wait("v", "PE", npe_h[half])
                        nevs.append(w.op("v", lambda e, jj=j, hh=half: e.tensor_copy(
                            out=xqt[:, jj * 2048 + hh * 1024: jj * 2048 + hh * 1024 + 1024],
                            in_=sc2[hh][:]), "V"))
                    prev_evac = nevs[-1]
                    prev_v = nevs[-1]

                nA_stg = {0: None, 1: None}
                nQ_stg = {0: w.n["T0"], 1: w.n["T1"]}
                npe_ob = None
                for ob in range(4):
                    if npe_ob is not None:
                        w.wait("q", "PE", npe_ob)
                    w.wait("q", "G", ng_wfq)
                    ncol = w.op("q", lambda e, o=ob: e.dma_start(
                        out=wfq_col[:],
                        in_=wfq_d.rearrange("ic p o -> p ic o")
                        [:, :, o * 512:o * 512 + 512]), "C", 16)
                    for j in range(JT):
                        par = j % 2
                        w.wait("t", "C", ncol)
                        w.wait("t", "V", prev_evac)
                        if nA_stg[par] is not None:
                            w.wait("t", "A", nA_stg[par])
                        npe_ob = proj16(
                            acc2[:, par * 512:par * 512 + 512],
                            lambda i, jj=j: xqt[:, jj * 2048 + i * 128:
                                               jj * 2048 + i * 128 + 128],
                            lambda i: wfq_col[:, i * 512:i * 512 + 512])
                        w.wait("a", "PE", npe_ob)
                        w.wait("a", "Q", nQ_stg[par])
                        w.wait("a", "V", nq_sc)
                        na = w.op("a", lambda e, pp=par: e.activation(
                            out=aout_stg[:, pp * 512:pp * 512 + 512],
                            in_=acc2[:, pp * 512:pp * 512 + 512], func=AF.Identity,
                            scale=scal[:, 3:4]), "A")
                        nA_stg[par] = na
                        w.wait("q", "A", na)
                        nQ_stg[par] = w.op("q", lambda e, jj=j, o=ob, pp=par: e.dma_start(
                            out=fo_d[jj * 128:jj * 128 + 128, o * 512:o * 512 + 512],
                            in_=aout_stg[:, pp * 512:pp * 512 + 512]), "Q", 16)
                prev_v = None
                for j in range(JT):
                    if prev_v is not None:
                        w.wait("q", "V", prev_v)
                    w.wait("q", "Q", nQ_stg[0])
                    w.wait("q", "Q", nQ_stg[1])
                    nq1 = w.op("q", lambda e, jj=j: e.dma_start(
                        out=pool[0][:], in_=fo_d[jj * 128:jj * 128 + 128, :]), "Q", 16)
                    nq2 = w.op("q", lambda e, jj=j: e.dma_start(
                        out=pool[1][:], in_=h1_d[jj * 128:jj * 128 + 128, :]), "Q", 16)
                    ln_stats(pool[0][:], epsb[:, 0:1], nq2)
                    v_ts(sq_s[:], pool[0][:], stx(I_NM), stx(I_RI), ALU.add, ALU.mult)
                    w.wait("v", "Q", nq2)
                    nv = v_tt(pool[0][:], sq_s[:], pool[1][:], ALU.add, "V")
                    prev_v = nv
                    w.wait("q", "V", nv)
                    w.op("q", lambda e, jj=j: e.dma_start(
                        out=out[jj * 128:jj * 128 + 128, :], in_=pool[0][:]), "Q", 16)

                if debug:
                    w.wait("g", "V", w.n["V"])
                    w.wait("g", "A", w.n["A"])
                    w.wait("g", "Q", w.n["Q"])
                    w.op("g", lambda e: e.dma_start(out=dbg_attn[:, :], in_=attn[:]), "G", 16)
                    w.op("g", lambda e: e.dma_start(out=dbg_scal[:, :], in_=scal[:]), "G", 16)
                    w.op("g", lambda e: e.dma_start(out=dbg_one[:, :], in_=one1[:]), "G", 16)
                    w.op("g", lambda e: e.dma_start(out=dbg_xqt[:, :], in_=xqt[:]), "G", 16)
                    w.op("g", lambda e: e.dma_start(out=dbg_wfq[:, :, :], in_=wfq_d[:, :, :]), "G", 16)
            w.wait("q", "Q", w.n["Q"])
            for s_ in ("K0", "K1", "K2", "K3", "T0", "T1"):
                w.wait("q", s_, w.n[s_])
            w.wait("q", "C", w.n["C"])
            w.wait("g", "G", w.n["G"])
            for s_ in ("C0", "C1", "S0", "S1"):
                w.wait("g", s_, w.n[s_])
            w.wait("g", "O0", w.n["O0"])
            w.wait("g", "O1", w.n["O1"])

            with nc.Block() as BB:
                @BB.sync
                def _(e):
                    for f in w.cmds[1]["q"]:
                        f(e)

                @BB.gpsimd
                def _(e):
                    for f in w.cmds[1]["g"]:
                        f(e)

                @BB.vector
                def _(e):
                    for f in w.cmds[1]["v"]:
                        f(e)

                @BB.scalar
                def _(e):
                    for f in w.cmds[1]["a"]:
                        f(e)

                @BB.tensor
                def _(e):
                    for f in w.cmds[1]["t"]:
                        f(e)
    return nc


def kernel(L0_attn_x, L0_attn_Wq, L0_attn_Wk, L0_attn_Wv, L0_attn_Wo,
           L0_ffn_weight, L0_ffn_norm_weight):
    x = np.ascontiguousarray(np.asarray(L0_attn_x, dtype=np.float32).reshape(S, D))
    Wq = np.asarray(L0_attn_Wq, dtype=np.float32)
    Wk = np.asarray(L0_attn_Wk, dtype=np.float32)
    Wv = np.asarray(L0_attn_Wv, dtype=np.float32)
    Wo = np.asarray(L0_attn_Wo, dtype=np.float32)
    Wf = np.asarray(L0_ffn_weight, dtype=np.float32)
    gam = np.ascontiguousarray(np.asarray(L0_ffn_norm_weight, dtype=np.float32))

    perm = np.concatenate([np.arange(0, 128, 2), np.arange(1, 128, 2)])
    Wq_p = Wq.reshape(16, 128, D)[:, perm, :].reshape(D, D)
    Wk_p = Wk.reshape(4, 128, D)[:, perm, :].reshape(512, D)
    wqt = np.ascontiguousarray(Wq_p.T)
    wkt = np.ascontiguousarray(Wk_p.T)
    wvt = np.ascontiguousarray(Wv.T)
    wot = np.ascontiguousarray(Wo.T)
    wft = np.ascontiguousarray(Wf.T)

    nc = build(debug=False)
    in_maps = []
    for c in range(NC):
        tiles = [8 * j + c for j in range(JT)]
        xq_c = np.ascontiguousarray(x.reshape(64, 128, D)[tiles].reshape(TPC, D))
        xs_c = np.ascontiguousarray(x[c * TPC:(c + 1) * TPC])
        tvals = np.zeros((1, 2048), dtype=np.float32)
        tvals[0, :1024] = np.arange(c * TPC, (c + 1) * TPC, dtype=np.float32)
        own_tok = np.concatenate([np.arange(t * 128, t * 128 + 128) for t in tiles])
        tvals[0, 1024:] = own_tok.astype(np.float32)
        bt = np.zeros((1, NSLOT), dtype=np.float32)
        for j in range(JT):
            t_own = 8 * j + c
            for s_ in range(1, L[j]):
                if s_ - 1 >= t_own:
                    bt[0, BOFF[j] + s_] = BKILL
        in_maps.append({
            "xq": xq_c, "xs": xs_c, "wqt": wqt, "wkt": wkt, "wvt": wvt,
            "wot": wot, "wft": wft, "gam": gam, "tidx": tvals, "btab": bt,
        })
    res = run_bass_kernel_spmd(nc, in_maps, core_ids=list(range(NC)))
    full = np.empty((S, D), dtype=np.float32)
    for c in range(NC):
        o = res.results[c]["out"]
        for j in range(JT):
            t = 8 * j + c
            full[t * 128:(t + 1) * 128] = o[j * 128:(j + 1) * 128]
    return full.reshape(1, S, D)


# version 09 (final)
# speedup vs baseline: 1.1399x; 1.0187x over previous
import contextlib
import math
import numpy as np
import ml_dtypes
import concourse.bass as bass
import concourse.mybir as mybir
from concourse.bass_utils import run_bass_kernel_spmd

F32 = mybir.dt.float32
BF16 = mybir.dt.bfloat16
I32 = mybir.dt.int32
AF = mybir.ActivationFunctionType
ALU = mybir.AluOpType
AX = mybir.AxisListType

S, D = 8192, 2048
NC, JT = 8, 8
TPC = JT * 128
L = [8 * j + 8 for j in range(JT)]
BOFF = [sum(L[:j]) for j in range(JT)]
NSLOT = sum(L)

TWO_PI = 2 * math.pi
C1 = float(np.float32(int(TWO_PI * 2 ** 10) / 2 ** 10))
C2 = float(np.float32(TWO_PI - C1))
C3 = float(np.float32(TWO_PI - C1 - float(np.float32(TWO_PI - C1))))
INV2PI = float(np.float32(1.0 / TWO_PI))
BOUND = 3.141590
PI2_HI = float(np.float32(math.pi / 2))
PI2_LO = float(np.float32(math.pi / 2 - float(np.float32(math.pi / 2))))
SC64 = 1.0 / math.sqrt(128.0)
SC_HI = float(np.float32(SC64))
SC_LO = float(np.float32(SC64 / SC_HI))
BKILL = -30000.0
INV127 = float(np.float32(1.0 / 127.0))

_ID_BF = np.eye(128, dtype=ml_dtypes.bfloat16)
_ID_F32 = np.eye(128, dtype=np.float32)
_CAUSAL = np.tile((np.arange(128)[:, None] <= np.arange(128)[None, :])
                  .astype(ml_dtypes.bfloat16), (1, 8))


import os
_SEMALIAS = {"K": "Q"}
for _kv in os.environ.get("SEMALIAS", "").split(","):
    if ":" in _kv:
        _a, _b = _kv.split(":")
        _SEMALIAS[_a] = _b


class _W:
    ENG = "qgvat"

    def __init__(self):
        self.cmds = {0: {e: [] for e in self.ENG}, 1: {e: [] for e in self.ENG}}
        self.blk = 0
        self.n = {k: 0 for k in ("Q", "G", "PE", "V", "A", "P", "B",
                                 "K", "C", "S", "O", "K0", "K1", "K2", "K3",
                                 "K4", "K5", "K6", "K7", "C0", "C1", "S0", "S1", "T0", "T1", "O0", "O1")}
        self.m = dict(_SEMALIAS)
        self.last_wait = {}
        self.sems = None

    def emit(self, eng, fn):
        self.cmds[self.blk][eng].append(fn)

    def wait(self, eng, sem, val):
        sem = self.m.get(sem, sem)
        if val is None or val <= 0:
            return
        key = (eng, sem)
        if self.last_wait.get(key, -1) >= val:
            return
        self.last_wait[key] = val
        self.emit(eng, lambda e, s=sem, v=val: e.wait_ge(self.sems[s], v))

    def op(self, eng, fn, sem=None, k=1):
        if sem is None:
            self.emit(eng, fn)
            return None
        sem = self.m.get(sem, sem)
        self.n[sem] += k
        v = self.n[sem]
        self.emit(eng, lambda e, s=sem, kk=k: fn(e).then_inc(self.sems[s], kk))
        return v

    def dump(self, src_ap, dst_ap):
        self.wait("g", "V", self.n["V"])
        self.wait("g", "A", self.n["A"])
        self.wait("g", "PE", self.n["PE"])
        self.wait("g", "Q", self.n["Q"])
        ng = self.op("g", lambda e: e.dma_start(out=dst_ap, in_=src_ap), "G", 16)
        for e in "qvat":
            self.wait(e, "G", ng)
        return ng

    def fence(self):
        self.n["B"] += 5
        tgt = self.n["B"]
        for e in self.ENG:
            self.emit(e, lambda en: en.nop().then_inc(self.sems["B"], 1))
        for e in self.ENG:
            self.emit(e, lambda en, t=tgt: en.wait_ge(self.sems["B"], t))
        return tgt


def build(debug=False):
    w = _W()
    nc = bass.Bass("TRN2", target_bir_lowering=False, debug=False)
    dbg_kind = "ExternalOutput" if debug else "Internal"

    xq = nc.dram_tensor("xq", [TPC, D], F32, kind="ExternalInput").ap()
    xs = nc.dram_tensor("xs", [TPC, D], F32, kind="ExternalInput").ap()
    wqt = nc.dram_tensor("wqt", [D, D], F32, kind="ExternalInput").ap()
    wkt = nc.dram_tensor("wkt", [D, 512], F32, kind="ExternalInput").ap()
    wvt = nc.dram_tensor("wvt", [D, 512], F32, kind="ExternalInput").ap()
    wot = nc.dram_tensor("wot", [D, D], F32, kind="ExternalInput").ap()
    wft = nc.dram_tensor("wft", [D, D], F32, kind="ExternalInput").ap()
    gam = nc.dram_tensor("gam", [D], F32, kind="ExternalInput").ap()
    tidx = nc.dram_tensor("tidx", [1, 2048], F32, kind="ExternalInput").ap()
    btab = nc.dram_tensor("btab", [1, NSLOT], F32, kind="ExternalInput").ap()
    out = nc.dram_tensor("out", [TPC, D], F32, kind="ExternalOutput").ap()

    kvk_in = nc.dram_tensor("kvk_in", [4, 128, 1024], BF16).ap()
    kvv_in = nc.dram_tensor("kvv_in", [8, 128, 512], BF16).ap()
    kvk_ag = nc.dram_tensor("kvk_ag", [NC, 4, 128, 1024], BF16).ap()
    kvv_ag = nc.dram_tensor("kvv_ag", [NC, 8, 128, 512], BF16).ap()
    mx_in = nc.dram_tensor("mx_in", [1, 16], F32).ap()
    mx_out = nc.dram_tensor("mx_out", [1, 16], F32).ap()
    sc_d = nc.dram_tensor("sc_d", [1, 16], F32).ap()
    h1_d = nc.dram_tensor("h1_d", [TPC, D], F32, kind=dbg_kind).ap()
    xn_d = nc.dram_tensor("xn_d", [TPC, D], F32, kind=dbg_kind).ap()
    ao_d = nc.dram_tensor("ao_d", [TPC, D], F32, kind=dbg_kind).ap()
    fo_d = nc.dram_tensor("fo_d", [TPC, D], F32, kind=dbg_kind).ap()
    wfq_d = nc.dram_tensor("wfq_d", [16, 128, D], BF16).ap()

    if debug:
        dbg_q = nc.dram_tensor("dbg_q", [128, 16 * 1024], BF16, kind="ExternalOutput").ap()
        dbg_dk = nc.dram_tensor("dbg_dk", [128, 4096], BF16, kind="ExternalOutput").ap()
        dbg_dv = nc.dram_tensor("dbg_dv", [128, 4096], BF16, kind="ExternalOutput").ap()
        dbg_attn = nc.dram_tensor("dbg_attn", [128, 16 * 1024], BF16, kind="ExternalOutput").ap()
        dbg_tabc = nc.dram_tensor("dbg_tabc", [128, 2048], BF16, kind="ExternalOutput").ap()
        dbg_tabs = nc.dram_tensor("dbg_tabs", [128, 2048], BF16, kind="ExternalOutput").ap()
        dbg_kvk = nc.dram_tensor("dbg_kvk", [4, 128, 1024], BF16, kind="ExternalOutput").ap()
        dbg_kvv = nc.dram_tensor("dbg_kvv", [8, 128, 512], BF16, kind="ExternalOutput").ap()
        dbg_ln1t = nc.dram_tensor("dbg_ln1t", [128, 8192], BF16, kind="ExternalOutput").ap()
        dbg_scal = nc.dram_tensor("dbg_scal", [128, 16], F32, kind="ExternalOutput").ap()
        dbg_one = nc.dram_tensor("dbg_one", [1, 16], F32, kind="ExternalOutput").ap()
        dbg_wfq = nc.dram_tensor("dbg_wfq", [16, 128, D], BF16, kind="ExternalOutput").ap()
        dbg_xqt = nc.dram_tensor("dbg_xqt", [128, JT * 2048], BF16, kind="ExternalOutput").ap()
        dbg_fi = nc.dram_tensor("dbg_fi", [128, 1], F32, kind="ExternalOutput").ap()
        dbg_ang = nc.dram_tensor("dbg_ang", [128, 1024], F32, kind="ExternalOutput").ap()
        dbg_sin = nc.dram_tensor("dbg_sin", [128, 1024], F32, kind="ExternalOutput").ap()
        dbg_lnr = nc.dram_tensor("dbg_lnr", [128, D], BF16, kind="ExternalOutput").ap()
        dbg_st8 = nc.dram_tensor("dbg_st8", [128, 16], F32, kind="ExternalOutput").ap()
        dbg_wrow = nc.dram_tensor("dbg_wrow", [1, 128], F32, kind="ExternalOutput").ap()
        dbg_tokb = nc.dram_tensor("dbg_tokb", [128, 1024], F32, kind="ExternalOutput").ap()
        dbg_red = nc.dram_tensor("dbg_red", [128, 1024], F32, kind="ExternalOutput").ap()
        dbg_ang2 = nc.dram_tensor("dbg_ang2", [128, 1024], F32, kind="ExternalOutput").ap()
        dbg_fe = nc.dram_tensor("dbg_fe", [128, 1], F32, kind="ExternalOutput").ap()
    id_bf_d = nc.inline_tensor(_ID_BF, name="id_bf").ap()
    id_f_d = nc.inline_tensor(_ID_F32, name="id_f").ap()
    caus_d = nc.inline_tensor(np.asarray(_CAUSAL), name="caus").ap()

    with contextlib.ExitStack() as top:
        sems = {}
        for s in ("Q", "G", "PE", "V", "A", "P", "B", "K", "C", "S", "O",
                  "K0", "K1", "K2", "K3", "K4", "K5", "K6", "K7",
                  "C0", "C1", "S0", "S1", "T0", "T1", "O0", "O1"):
            sems[s] = top.enter_context(nc.semaphore("s" + s))
        w.sems = sems

        q_all = top.enter_context(nc.sbuf_tensor("q_all", [128, 16 * 1024], BF16))
        diag_k = top.enter_context(nc.sbuf_tensor("diag_k", [128, 4 * 1024], BF16))
        diag_v = top.enter_context(nc.sbuf_tensor("diag_v", [128, JT * 512], BF16))
        gam_b = top.enter_context(nc.sbuf_tensor("gam_b", [128, D], F32))
        btab_b = top.enter_context(nc.sbuf_tensor("btab_b", [128, NSLOT], F32))
        scal = top.enter_context(nc.sbuf_tensor("scal", [128, 16], F32))
        id_bs = top.enter_context(nc.sbuf_tensor("id_bs", [128, 128], BF16))
        id_fs = top.enter_context(nc.sbuf_tensor("id_fs", [128, 128], F32))
        ones_bf = top.enter_context(nc.sbuf_tensor("ones_bf", [128, 128], BF16))
        pool = [top.enter_context(nc.sbuf_tensor(f"pool{i}", [128, D], F32))
                for i in range(2)]
        st8 = top.enter_context(nc.sbuf_tensor("st8", [128, 16], F32))
        one1 = top.enter_context(nc.sbuf_tensor("one1", [1, 16], F32))
        wsum = top.enter_context(nc.sbuf_tensor("wsum", [128, 1], F32))
        wrow = top.enter_context(nc.sbuf_tensor("wrow", [1, 128], F32))
        epsb = top.enter_context(nc.sbuf_tensor("epsb", [128, 4], F32))
        bnt = top.enter_context(nc.sbuf_tensor("bnt", [128, 32], F32))

        (I_SUM, I_NM, I_VAR, I_SD, I_RI, I_SQ, I_M2, I_RMX, I_T1, I_RI2,
         I_T2, I_T3) = range(12)

        def stx(i):
            return st8[:, i:i + 1]

        def vdrain():
            w.emit("v", lambda e: e.drain())

        def adrain():
            w.emit("a", lambda e: e.drain())

        def v_ts(dst, src, s1, s2=None, op0=ALU.mult, op1=None, sem=None):
            if op1 is None:
                return w.op("v", lambda e: e.tensor_scalar(
                    out=dst, in0=src, scalar1=s1, scalar2=None, op0=op0), sem)
            return w.op("v", lambda e: e.tensor_scalar(
                out=dst, in0=src, scalar1=s1, scalar2=s2, op0=op0, op1=op1), sem)

        def v_tt(dst, a, bb, op, sem=None):
            return w.op("v", lambda e: e.tensor_tensor(out=dst, in0=a, in1=bb, op=op), sem)

        def v_stt(dst, a, s, bb, op0, op1, sem=None):
            return w.op("v", lambda e: e.scalar_tensor_tensor(
                out=dst, in0=a, scalar=s, in1=bb, op0=op0, op1=op1), sem)

        def ln_stats(px, eps, nq_ready, rsem="Q"):
            if rsem != "V":
                w.wait("v", rsem, nq_ready)
            for c in range(4):
                w.op("v", lambda e, cc=c: e.bn_stats(
                    out=bnt[:, cc * 6:cc * 6 + 6],
                    in_=px[:, cc * 512:cc * 512 + 512]))
            vdrain()
            w.op("v", lambda e: e.bn_aggr(out=bnt[:, 24:26], in_=bnt[:, 0:24]))
            vdrain()
            nv2 = v_ts(stx(I_NM), bnt[:, 24:25], -1.0, sem="V")
            w.wait("a", "V", nv2)
            na2 = w.op("a", lambda e, ee=eps: e.activation(
                out=stx(I_SD), in_=bnt[:, 25:26], func=AF.Sqrt, bias=ee), "A")
            w.wait("v", "A", na2)
            nvr = w.op("v", lambda e: e.reciprocal(out=stx(I_RI), in_=stx(I_SD)), "V")
            vdrain()
            return nvr

        with contextlib.ExitStack() as sa:
            tab_c = sa.enter_context(nc.sbuf_tensor("tab_c", [128, 2048], BF16))
            tab_s = sa.enter_context(nc.sbuf_tensor("tab_s", [128, 2048], BF16))
            tab_cq = sa.enter_context(nc.sbuf_tensor("tab_cq", [128, 2048], BF16))
            tab_sq = sa.enter_context(nc.sbuf_tensor("tab_sq", [128, 2048], BF16))
            tokb_t = sa.enter_context(nc.sbuf_tensor("tokb_t", [128, 1024], F32))
            ang_t = sa.enter_context(nc.sbuf_tensor("ang_t", [128, 1024], F32))
            tr1_t = sa.enter_context(nc.sbuf_tensor("tr1_t", [128, 1024], F32))
            tr2_t = sa.enter_context(nc.sbuf_tensor("tr2_t", [128, 1024], F32))
            tokb = tokb_t[:, :]
            ang = ang_t[:, :]
            tr1 = tr1_t[:, :]
            tr2 = tr2_t[:, :]
            tr3 = tokb_t[:, :]
            sneg = ang_t[:, :]
            trk = sa.enter_context(nc.sbuf_tensor("trk", [128, 1024], I32))
            sin_f = sa.enter_context(nc.sbuf_tensor("sin_f", [128, 1024], F32))
            cos_f = sa.enter_context(nc.sbuf_tensor("cos_f", [128, 1024], F32))
            fi = sa.enter_context(nc.sbuf_tensor("fi", [128, 1], F32))
            fe = sa.enter_context(nc.sbuf_tensor("fe", [128, 1], F32))
            lnv = sa.enter_context(nc.sbuf_tensor("lnv", [128, 1], F32))
            tenk = sa.enter_context(nc.sbuf_tensor("tenk", [128, 1], F32))
            ln1t = sa.enter_context(nc.sbuf_tensor("ln1t", [128, 2 * 16 * 512], BF16))
            lnraw = sa.enter_context(nc.sbuf_tensor("lnraw", [128, 2 * D], BF16))
            wkt_s = sa.enter_context(nc.sbuf_tensor("wkt_s", [128, 16 * 512], BF16))
            wvt_s = sa.enter_context(nc.sbuf_tensor("wvt_s", [128, 16 * 512], BF16))
            wq_col = sa.enter_context(nc.sbuf_tensor("wq_col", [128, 2 * 2048], BF16))
            praw = sa.enter_context(nc.sbuf_tensor("praw", [128, 2 * 512], BF16))
            pswp = sa.enter_context(nc.sbuf_tensor("pswp", [128, 2 * 512], BF16))
            pr1 = sa.enter_context(nc.sbuf_tensor("pr1", [128, 512], BF16))
            pr2 = sa.enter_context(nc.sbuf_tensor("pr2", [128, 512], BF16))
            kvstg = sa.enter_context(nc.sbuf_tensor("kvstg", [128, 2 * 512], BF16))
            ps_tr = sa.enter_context(nc.psum_tensor("ps_tr", [128, 2048], BF16))
            ps_p = sa.enter_context(nc.psum_tensor("ps_p", [128, 2 * 512], F32))

            w.blk = 0
            w.op("q", lambda e: e.dma_start(out=id_fs[:], in_=id_f_d[:, :]), "Q", 16)
            w.op("q", lambda e: e.dma_start(
                out=tokb[0:64, :], in_=tidx[0:1, 0:1024].partition_broadcast(64)), "Q", 16)
            nQ_tok = w.op("q", lambda e: e.dma_start(
                out=tokb[64:128, :],
                in_=tidx[0:1, 1024:2048].partition_broadcast(64)), "Q", 16)
            w.op("q", lambda e: e.dma_start(
                out=gam_b[:], in_=gam[None, :].partition_broadcast(128)[:, 0, :]), "Q", 16)
            w.op("q", lambda e: e.dma_start(
                out=btab_b[:], in_=btab[0:1, :].partition_broadcast(128)), "Q", 16)
            nV_z = w.op("v", lambda e: e.memset(one1[0:1, 8:16], 0.0), "V")
            w.wait("q", "V", nV_z)
            w.op("q", lambda e: e.dma_start(out=mx_in[0:1, 0:8],
                                            in_=one1[0:1, 8:16]), "Q", 16)
            w.op("q", lambda e: e.dma_start(out=mx_in[0:1, 8:16],
                                            in_=one1[0:1, 8:16]), "Q", 16)
            nG_id = w.op("g", lambda e: e.dma_start(out=id_bs[:], in_=id_bf_d[:, :]), "G", 16)
            w.op("g", lambda e: e.dma_start(
                out=wkt_s[:], in_=wkt.rearrange("(ic p) o -> p ic o", p=128)), "G", 16)
            nG_w = w.op("g", lambda e: e.dma_start(
                out=wvt_s[:], in_=wvt.rearrange("(ic p) o -> p ic o", p=128)), "G", 16)
            w.op("v", lambda e: e.memset(ones_bf[:], 1.0), "V")
            w.op("v", lambda e: e.memset(epsb[:, 0:1], 1e-5))
            w.op("v", lambda e: e.memset(epsb[:, 1:2], 1e-6))
            nV_eps = w.op("v", lambda e: e.memset(epsb[:, 2:3], SC_HI), "V")
            w.wait("a", "V", nV_eps)

            w.op("g", lambda e: e.iota(
                out=fi[0:64, :], pattern=[[0, 1]], base=0, channel_multiplier=1,
                allow_small_or_imprecise_dtypes=True), "P")
            nP_iota = w.op("g", lambda e: e.iota(
                out=fi[64:128, :], pattern=[[0, 1]], base=0, channel_multiplier=1,
                allow_small_or_imprecise_dtypes=True), "P")
            nV_tenk = w.op("v", lambda e: e.memset(tenk[:], 10000.0), "V")
            w.wait("a", "V", nV_tenk)
            w.op("a", lambda e: e.activation(out=lnv[:], in_=tenk[:], func=AF.Ln), "A")
            adrain()
            w.wait("v", "P", nP_iota)
            nV_fe = v_ts(fe[:], fi[:], -1.0 / 64.0, sem="V")
            w.wait("a", "V", nV_fe)
            nA_fr = w.op("a", lambda e: e.activation(
                out=fi[:], in_=fe[:], func=AF.Exp, scale=lnv[:]), "A")
            w.wait("v", "Q", w.n["Q"])
            w.wait("v", "A", nA_fr)
            w.op("v", lambda e: e.tensor_copy(out=fe[:], in_=fi[:]))
            vdrain()
            nV_ang = v_ts(ang[:], tokb[:], fe[:], sem="V")
            if debug:
                w.dump(fe[:], dbg_fe[:, :])
                w.dump(fi[:], dbg_fi[:, :])
                w.dump(tokb, dbg_tokb[:, :])
                w.dump(ang, dbg_ang[:, :])
                w.dump(ang, dbg_ang2[:, :])

            def reduce_sin(src, dst):
                v_ts(tr1[:], src, INV2PI)
                vdrain()
                w.op("v", lambda e: e.tensor_copy(out=trk[:], in_=tr1[:]))
                vdrain()
                w.op("v", lambda e: e.tensor_copy(out=tr2[:], in_=trk[:]))
                vdrain()
                v_stt(tr1[:], tr2[:], -C1, src, ALU.mult, ALU.add)
                vdrain()
                v_stt(tr1[:], tr2[:], -C2, tr1[:], ALU.mult, ALU.add)
                vdrain()
                v_stt(tr1[:], tr2[:], -C3, tr1[:], ALU.mult, ALU.add)
                vdrain()
                v_ts(tr2[:], tr1[:], BOUND, op0=ALU.is_gt)
                v_ts(tr3[:], tr1[:], -BOUND, op0=ALU.is_lt)
                vdrain()
                v_tt(tr2[:], tr2[:], tr3[:], ALU.subtract)
                vdrain()
                v_stt(tr1[:], tr2[:], -C1, tr1[:], ALU.mult, ALU.add)
                vdrain()
                v_stt(tr1[:], tr2[:], -C2, tr1[:], ALU.mult, ALU.add)
                vdrain()
                nv = v_stt(tr1[:], tr2[:], -C3, tr1[:], ALU.mult, ALU.add, "V")
                vdrain()
                if debug:
                    w.dump(tr1, dbg_red[:, :])
                w.wait("a", "V", nv)
                return w.op("a", lambda e, d=dst: e.activation(
                    out=d[:], in_=tr1[:], func=AF.Sin), "A")

            vdrain()
            nA_sin = reduce_sin(ang[:], sin_f)
            if debug:
                w.wait("g", "A", nA_sin)
                w.dump(sin_f[:], dbg_sin[:, :])
            w.wait("v", "A", nA_sin)
            v_ts(ang[:], ang[:], PI2_HI, PI2_LO, ALU.add, ALU.add)
            vdrain()
            nA_cos = reduce_sin(ang[:], cos_f)
            w.wait("v", "A", nA_cos)
            v_ts(sneg[:], sin_f[:], -1.0)
            v_ts(tr1[:], cos_f[:], SC_LO)
            v_ts(tr2[:], sin_f[:], SC_LO)
            vdrain()
            nV_tab = v_ts(tr3[:], sneg[:], SC_LO, sem="V")
            vdrain()
            w.wait("g", "V", nV_tab)
            nG_tab = None
            for dst, half, srcp in ((tab_c, 0, cos_f), (tab_c, 1, cos_f),
                                    (tab_s, 0, sneg), (tab_s, 1, sin_f),
                                    (tab_cq, 0, tr1), (tab_cq, 1, tr1),
                                    (tab_sq, 0, tr3), (tab_sq, 1, tr2)):
                for th in range(2):
                    nG_tab = w.op("g", lambda e, d=dst, h=half, s=srcp, t=th:
                                  e.dma_start(
                        out=d[64 * h:64 * h + 64, t * 1024:t * 1024 + 1024],
                        in_=s[64 * t:64 * t + 64, :]), "G", 16)

            ln1t_v = ln1t.rearrange("p (b ic f) -> p b ic f", b=2, ic=16)

            def lnsl(gp, i, t0=None):
                base = gp * 8192 + i * 512
                if t0 is None:
                    return ln1t[:, base:base + 512]
                return ln1t[:, base + t0:base + t0 + 128]

            busy = {"pv": {0: None, 1: None}, "pa": {0: None, 1: None},
                    "tr": None}
            projdone = {0: None, 1: None}

            dbg_first = {"done": False}

            def ln_transpose(src_dram, row0, gslot, gp):
                npe_free = projdone.get(gp)
                par = gslot % 2
                px = pool[par][:]
                lw = lnraw[:, par * D:par * D + D]
                w.wait("q", "V", busy["pv"][par])
                w.wait("q", "A", busy["pa"][par])
                nq = w.op("q", lambda e, r=row0, sd=src_dram: e.dma_start(
                    out=px, in_=sd[r:r + 128, :]), f"T{par}", 16)
                ln_stats(px, epsb[:, 0:1], nq, rsem=f"T{par}")
                busy["pa"][par] = w.n["A"]
                nv = w.op("v", lambda e, p=px, ll=lw: e.tensor_scalar(
                    out=ll, in0=p, scalar1=stx(I_NM), scalar2=stx(I_RI),
                    op0=ALU.add, op1=ALU.mult), "V")
                busy["pv"][par] = nv
                w.wait("t", "V", nv)
                w.wait("t", "G", nG_id)
                w.wait("t", "V", busy["tr"])
                npe = None
                for ic in range(16):
                    npe = w.op("t", lambda e, i=ic, ll=lw: e.transpose(
                        out=ps_tr[:, i * 128:i * 128 + 128],
                        in_=ll[:, i * 128:i * 128 + 128], identity=id_bs[:]),
                        "PE" if ic == 15 else None)
                w.wait("v", "PE", npe)
                if npe_free is not None:
                    w.wait("v", "PE", npe_free)
                nev = w.op("v", lambda e, g=gslot, b=gp: e.tensor_copy(
                    out=ln1t_v[:, b:b + 1, :, g * 128:g * 128 + 128],
                    in_=ps_tr[:]), "V")
                busy["tr"] = nev
                if debug and not dbg_first["done"]:
                    dbg_first["done"] = True
                    w.dump(lw, dbg_lnr[:, :])
                    w.dump(st8[:], dbg_st8[:, :])
                return nev

            rope_busy = {0: None, 1: None}

            def rope(par, tabc, tabs, tcol, dst_ap_fn, na_raw):
                raw = praw[:, par * 512: par * 512 + 512]
                swp = pswp[:, par * 512: par * 512 + 512]
                w.wait("g", "A", na_raw)
                if rope_busy[par] is not None:
                    w.wait("g", "V", rope_busy[par])
                w.op("g", lambda e: e.dma_start(out=swp[0:64, :], in_=raw[64:128, :]), f"S{par}", 16)
                ng = w.op("g", lambda e: e.dma_start(out=swp[64:128, :], in_=raw[0:64, :]), f"S{par}", 16)
                w.wait("v", f"S{par}", ng)
                w.wait("v", "A", na_raw)
                v_tt(pr1[:], swp, tabs[:, tcol:tcol + 512], ALU.mult)
                v_tt(pr2[:], raw, tabc[:, tcol:tcol + 512], ALU.mult)
                nv = w.op("v", lambda e: e.tensor_tensor(
                    out=dst_ap_fn(), in0=pr1[:], in1=pr2[:], op=ALU.add), "V")
                rope_busy[par] = nv
                return nv

            def proj16(psum_ap, lhs_fn, rhs_fn, sem="PE"):
                npe = None
                for ic in range(16):
                    npe = w.op("t", lambda e, i=ic: e.matmul(
                        out=psum_ap, lhsT=lhs_fn(i), rhs=rhs_fn(i),
                        start=(i == 0), stop=(i == 15)),
                        sem if ic == 15 else None)
                return npe

            w.wait("t", "G", nG_w)
            last_o = {}
            nA_evac = {0: None, 1: None}
            for grp in range(2):
                gp = grp % 2
                nev = None
                for tl in range(4):
                    nev = ln_transpose(xs, (grp * 4 + tl) * 128, tl, gp)
                w.wait("t", "V", nev)
                if debug and grp == 0:
                    w.dump(ln1t[:, 0:8192], dbg_ln1t[:, :])
                for g in range(4):
                    par = g % 2
                    if nA_evac[par] is not None:
                        w.wait("t", "A", nA_evac[par])
                    npe = proj16(
                        ps_p[:, par * 512:par * 512 + 512],
                        lambda i, gg=g: wkt_s[:, i * 512 + gg * 128:
                                              i * 512 + gg * 128 + 128],
                        lambda i, b=gp: lnsl(b, i))
                    w.wait("a", "PE", npe)
                    na = w.op("a", lambda e, p=par: e.activation(
                        out=praw[:, p * 512:p * 512 + 512],
                        in_=ps_p[:, p * 512:p * 512 + 512], func=AF.Identity), "A")
                    nA_evac[par] = na
                    w.wait("v", "G", nG_tab)
                    w.wait("v", f"O{par}", last_o.get(par, 0))
                    nv = rope(par, tab_c, tab_s, grp * 512,
                              lambda p=par: kvstg[:, p * 512:p * 512 + 512], na)
                    w.wait("g", "V", nv)
                    last_o[par] = w.op("g", lambda e, gg=g, gr=grp, p=par: e.dma_start(
                        out=kvk_in[gg, :, gr * 512:gr * 512 + 512],
                        in_=kvstg[:, p * 512:p * 512 + 512]), f"O{par}", 16)
                for tl in range(4):
                    par = tl % 2
                    if nA_evac[par] is not None:
                        w.wait("t", "A", nA_evac[par])
                    npe = proj16(
                        ps_p[:, par * 512:par * 512 + 512],
                        lambda i, t_=tl, b=gp: lnsl(b, i, t_ * 128),
                        lambda i: wvt_s[:, i * 512:i * 512 + 512])
                    w.wait("a", "PE", npe)
                    w.wait("a", f"O{par}", last_o.get(par, 0))
                    na = w.op("a", lambda e, p=par: e.activation(
                        out=kvstg[:, p * 512:p * 512 + 512],
                        in_=ps_p[:, p * 512:p * 512 + 512], func=AF.Identity), "A")
                    nA_evac[par] = na
                    w.wait("g", "A", na)
                    last_o[par] = w.op("g", lambda e, t_=tl, gr=grp, p=par: e.dma_start(
                        out=kvv_in[gr * 4 + t_, :, :],
                        in_=kvstg[:, p * 512:p * 512 + 512]), f"O{par}", 16)
                projdone[gp] = npe
            w.wait("g", "O0", w.n["O0"])
            w.wait("g", "O1", w.n["O1"])
            w.op("g", lambda e: e.collective_compute(
                "AllGather", ALU.bypass, replica_groups=[list(range(NC))],
                ins=[kvk_in], outs=[kvk_ag]), "P")
            nP_ag = w.op("g", lambda e: e.collective_compute(
                "AllGather", ALU.bypass, replica_groups=[list(range(NC))],
                ins=[kvv_in], outs=[kvv_ag]), "P")

            for grp in range(2):
                gp = grp % 2
                nev = None
                for tl in range(4):
                    nev = ln_transpose(xq, (grp * 4 + tl) * 128, tl, gp)
                w.wait("t", "V", nev)
                ncol = {0: 0, 1: 0}
                npe_q = {0: None, 1: None}
                for h in range(16):
                    parw = h % 2
                    w.wait("g", "PE", npe_q[parw])
                    ncol[parw] = w.op("g", lambda e, hh=h, p=parw: e.dma_start(
                        out=wq_col[:, p * 2048:p * 2048 + 2048],
                        in_=wqt.rearrange("(ic p) o -> p ic o", p=128)
                        [:, :, hh * 128:hh * 128 + 128]), f"C{parw}", 16)
                    par = h % 2
                    if nA_evac[par] is not None:
                        w.wait("t", "A", nA_evac[par])
                    w.wait("t", f"C{parw}", ncol[parw])
                    npe = proj16(
                        ps_p[:, par * 512:par * 512 + 512],
                        lambda i, pw=parw: wq_col[:, pw * 2048 + i * 128:
                                                  pw * 2048 + i * 128 + 128],
                        lambda i, b=gp: lnsl(b, i))
                    npe_q[parw] = npe
                    w.wait("a", "PE", npe)
                    na = w.op("a", lambda e, p=par: e.activation(
                        out=praw[:, p * 512:p * 512 + 512],
                        in_=ps_p[:, p * 512:p * 512 + 512], func=AF.Identity), "A")
                    nA_evac[par] = na
                    rope(par, tab_cq, tab_sq, 1024 + grp * 512,
                         lambda hh=h, g=grp: q_all[:, hh * 1024 + g * 512:
                                                   hh * 1024 + g * 512 + 512], na)
                for g in range(4):
                    par = g % 2
                    if nA_evac[par] is not None:
                        w.wait("t", "A", nA_evac[par])
                    npe = proj16(
                        ps_p[:, par * 512:par * 512 + 512],
                        lambda i, gg=g: wkt_s[:, i * 512 + gg * 128:
                                              i * 512 + gg * 128 + 128],
                        lambda i, b=gp: lnsl(b, i))
                    w.wait("a", "PE", npe)
                    na = w.op("a", lambda e, p=par: e.activation(
                        out=praw[:, p * 512:p * 512 + 512],
                        in_=ps_p[:, p * 512:p * 512 + 512], func=AF.Identity), "A")
                    nA_evac[par] = na
                    rope(par, tab_c, tab_s, 1024 + grp * 512,
                         lambda gg=g, gr=grp: diag_k[:, gg * 1024 + gr * 512:
                                                     gg * 1024 + gr * 512 + 512], na)
                for tl in range(4):
                    par = tl % 2
                    if nA_evac[par] is not None:
                        w.wait("t", "A", nA_evac[par])
                    npe = proj16(
                        ps_p[:, par * 512:par * 512 + 512],
                        lambda i, t_=tl, b=gp: lnsl(b, i, t_ * 128),
                        lambda i: wvt_s[:, i * 512:i * 512 + 512])
                    w.wait("a", "PE", npe)
                    na = w.op("a", lambda e, t_=tl, gr=grp, p=par: e.activation(
                        out=diag_v[:, (gr * 4 + t_) * 512:(gr * 4 + t_) * 512 + 512],
                        in_=ps_p[:, p * 512:p * 512 + 512], func=AF.Identity), "A")
                    nA_evac[par] = na
                projdone[gp] = npe
            if debug:
                w.wait("g", "V", w.n["V"])
                w.wait("g", "A", w.n["A"])
                for dst_, src_ in ((dbg_tabc, tab_c), (dbg_tabs, tab_s)):
                    w.op("g", lambda e, d=dst_, s=src_: e.dma_start(
                        out=d[:, :], in_=s[:]), "G", 16)
                w.op("g", lambda e: e.dma_start(out=dbg_q[:, :], in_=q_all[:]), "G", 16)
                w.op("g", lambda e: e.dma_start(out=dbg_dk[:, :], in_=diag_k[:]), "G", 16)
                w.op("g", lambda e: e.dma_start(out=dbg_dv[:, :], in_=diag_v[:]), "G", 16)
                w.op("g", lambda e: e.dma_start(out=dbg_kvk[:, :, :], in_=kvk_in[:, :, :]), "G", 16)
                w.op("g", lambda e: e.dma_start(out=dbg_kvv[:, :, :], in_=kvv_in[:, :, :]), "G", 16)
            w.fence()

            with nc.Block() as BA:
                @BA.sync
                def _(e):
                    for f in w.cmds[0]["q"]:
                        f(e)

                @BA.gpsimd
                def _(e):
                    for f in w.cmds[0]["g"]:
                        f(e)

                @BA.vector
                def _(e):
                    for f in w.cmds[0]["v"]:
                        f(e)

                @BA.scalar
                def _(e):
                    for f in w.cmds[0]["a"]:
                        f(e)

                @BA.tensor
                def _(e):
                    for f in w.cmds[0]["t"]:
                        f(e)

        with contextlib.ExitStack() as sb:
            kb = sb.enter_context(nc.sbuf_tensor("kb", [128, 2048], BF16))
            vb = sb.enter_context(nc.sbuf_tensor("vb", [128, 2048], BF16))
            expst = sb.enter_context(nc.sbuf_tensor("expst", [128, 2048], BF16))
            attn = sb.enter_context(nc.sbuf_tensor("attn", [128, 16 * 1024], BF16))
            rcp = sb.enter_context(nc.sbuf_tensor("rcp", [128, 1024], F32))
            wot_col = sb.enter_context(nc.sbuf_tensor("wot_col", [128, 16 * 512], BF16))
            wfq_col = wot_col
            caus_s = sb.enter_context(nc.sbuf_tensor("caus_s", [128, 1024], BF16))
            sq_s = sb.enter_context(nc.sbuf_tensor("sq_s", [128, D], F32))
            xqt = sb.enter_context(nc.sbuf_tensor("xqt", [128, JT * 2048], BF16))
            wq_stage = xqt[:, 0:D]
            aout_stg = None
            ki_b = sb.enter_context(nc.sbuf_tensor("ki_b", [128, D], I32))
            sc2 = [sb.enter_context(nc.psum_tensor(f"sc2_{i}", [128, 1024], F32))
                   for i in range(2)]
            acc2 = sb.enter_context(nc.psum_tensor("acc2", [128, 1024], F32))
            sums2 = sb.enter_context(nc.psum_tensor("sums2", [128, 1024], F32))
            aout_stg = rcp

            w.blk = 1
            nG_caus = w.op("g", lambda e: e.dma_start(out=caus_s[:], in_=caus_d[:, :]), "G", 16)
            w.wait("v", "G", nG_caus)
            kvk_v = kvk_ag.rearrange("r g p t -> r p g t")
            kb_v = kb.rearrange("p (c g t) -> p c g t", c=8, g=2)
            q_v = q_all.rearrange("p (h f) -> p h f", h=16)
            attn_v = attn.rearrange("p (h f) -> p h f", h=16)

            wst = {"ic1": 0, "ic2": 0, "v_done": None, "a_done": None}

            def wprep1():
                ic = wst["ic1"]; wst["ic1"] += 1
                if wst["v_done"] is not None:
                    w.wait("g", "V", wst["v_done"])
                nq = w.op("g", lambda e, i=ic: e.dma_start(
                    out=pool[0][:], in_=wft[i * 128:i * 128 + 128, :]), "G", 16)
                w.wait("v", "G", nq)
                nv = w.op("v", lambda e: e.tensor_reduce(
                    out=stx(I_T1), in_=pool[0][:], axis=AX.X, op=ALU.add,
                    apply_absolute_value=True), "V")
                vdrain()
                if ic == 0:
                    nv = w.op("v", lambda e: e.tensor_copy(out=wsum[:], in_=stx(I_T1)), "V")
                else:
                    nv = v_tt(wsum[:], wsum[:], stx(I_T1), ALU.add, "V")
                vdrain()
                wst["v_done"] = nv

            def wprep_scale():
                w.wait("g", "V", wst["v_done"])
                nq = w.op("g", lambda e: e.dma_start(out=wrow[:, :], in_=wsum[:, :]), "G", 16)
                w.wait("v", "G", nq)
                nvr = w.op("v", lambda e: e.tensor_reduce(
                    out=one1[0:1, 0:1], in_=wrow[:, :], axis=AX.X, op=ALU.add), "V")
                vdrain()
                w.wait("g", "V", nvr)
                nq2 = w.op("g", lambda e: e.dma_start(
                    out=sc_d[0:1, 4:5], in_=one1[0:1, 0:1]), "G", 16)
                w.wait("g", "G", nq2)
                nq3 = w.op("g", lambda e: e.dma_start(
                    out=scal[:, 4:5],
                    in_=sc_d[0:1, 4:5].partition_broadcast(128)), "G", 16)
                w.wait("v", "G", nq3)
                w.op("v", lambda e: e.tensor_scalar(
                    out=scal[:, 0:1], in0=scal[:, 4:5],
                    scalar1=1.0 / 4194304.0, scalar2=1e-5,
                    op0=ALU.mult, op1=ALU.max))
                vdrain()
                nv3 = w.op("v", lambda e: e.tensor_scalar(
                    out=scal[:, 1:2], in0=scal[:, 0:1], scalar1=0.5,
                    scalar2=None, op0=ALU.mult), "V")
                vdrain()
                wst["v_done"] = nv3
                return nv3

            def wprep2(nq_tau):
                ic = wst["ic2"]; wst["ic2"] += 1
                if wst["v_done"] is not None:
                    w.wait("q", "V", wst["v_done"])
                if wst["a_done"] is not None:
                    w.wait("q", "A", wst["a_done"])
                nq = w.op("q", lambda e, i=ic: e.dma_start(
                    out=pool[0][:], in_=wft[i * 128:i * 128 + 128, :]), "Q", 16)
                w.wait("a", "Q", nq)
                na0 = w.op("a", lambda e: e.activation(
                    out=sq_s[:], in_=pool[0][:], func=AF.Abs), "A")
                na = w.op("a", lambda e: e.activation(
                    out=pool[1][:], in_=pool[0][:], func=AF.Sign), "A")
                wst["a_done"] = na
                w.wait("v", "A", na)
                v_ts(sq_s[:], sq_s[:], scal[:, 1:2], op0=ALU.is_ge)
                nv2 = v_tt(wq_stage, sq_s[:], pool[1][:], ALU.mult, "V")
                wst["v_done"] = nv2
                w.wait("g", "V", nv2)
                return w.op("g", lambda e, i=ic: e.dma_start(
                    out=wfq_d[i:i + 1, :, :], in_=wq_stage), "G", 16)

            SKIPW = bool(os.environ.get("SKIPW"))
            JMAX = int(os.environ.get("JMAX", "8"))
            SKIP456 = bool(os.environ.get("SKIP456"))
            ss = 0
            li = 0
            nA_exp = {}
            nPE_sc = {}
            nPE_av = {}
            li_slot = {}
            load_q = {}
            nV_norm = None
            nq_tau = None
            ng_wfq = 0
            for j in range(min(JT, JMAX)):
                if SKIPW:
                    pass
                elif j <= 3:
                    for _ in range(4):
                        wprep1()
                elif j == 4:
                    nq_tau = wprep_scale()
                else:
                    for _ in range(5 if j < 7 else 6):
                        ng_wfq = wprep2(nq_tau)
                for p in range(2):
                    g0 = 2 * p
                    pend = None

                    def emit_av(info):
                        (pdiag, pj, pg0, ppar, plp, pdeps, pfirst, plast,
                         pss) = info
                        if pfirst and nV_norm is not None:
                            w.wait("t", "V", nV_norm)
                        for i in range(2):
                            w.wait("t", pdeps[i][0], pdeps[i][1])
                            g = pg0 + i
                            if pdiag:
                                vl = diag_v[:, pj * 512 + g * 128:
                                            pj * 512 + g * 128 + 128]
                            else:
                                vl = vb[:, plp * 256 + i * 128:
                                        plp * 256 + i * 128 + 128]
                            w.op("t", lambda e, vv=vl, pp=ppar, ii=i, f=pfirst,
                                 la=plast: e.matmul(
                                out=acc2[:, ii * 512:ii * 512 + 512], lhsT=vv,
                                rhs=expst[:, pp * 1024 + ii * 512:
                                          pp * 1024 + ii * 512 + 512],
                                start=f, stop=la), None)
                            nPE_av[pss] = w.op("t", lambda e, pp=ppar, ii=i,
                                               f=pfirst, la=plast: e.matmul(
                                out=sums2[:, ii * 512:ii * 512 + 512],
                                lhsT=ones_bf[:],
                                rhs=expst[:, pp * 1024 + ii * 512:
                                          pp * 1024 + ii * 512 + 512],
                                start=f, stop=la),
                                "PE" if i == 1 else None)

                    for s in range(L[j]):
                        diag = (s == 0)
                        par = ss % 2
                        if diag:
                            cur_li = None
                            lp = None
                        else:
                            kt = s - 1
                            r, loc = kt // 8, kt % 8
                            lpar = li % 8
                            if li - 8 >= 0:
                                w.wait("q", "PE", nPE_av.get(li_slot[li - 8], 0))
                            w.wait("q", "P", nP_ag)
                            w.op("q", lambda e, rr=r, lo=loc, gg=g0, lpx=lpar: e.dma_start(
                                out=kb_v[:, lpx:lpx + 1, :, :],
                                in_=kvk_v[rr:rr + 1, :, gg:gg + 2,
                                          lo * 128:lo * 128 + 128]), f"K{lpar}", 16)
                            nqv = w.op("q", lambda e, rr=r, lo=loc, gg=g0, lpx=lpar: e.dma_start(
                                out=vb[:, lpx * 256:lpx * 256 + 256],
                                in_=kvv_ag[rr:rr + 1, lo:lo + 1, :,
                                           gg * 128:gg * 128 + 256]), f"K{lpar}", 16)
                            load_q[li] = nqv
                            li_slot[li] = ss
                            cur_li = li
                            lp = lpar
                            li += 1
                        if not diag:
                            w.wait("t", f"K{lp}", load_q[cur_li])
                        if ss >= 2:
                            w.wait("t", "A", nA_exp[ss - 2])
                        for i in range(2):
                            g = g0 + i
                            if diag:
                                kl = diag_k[:, g * 1024 + j * 128:
                                            g * 1024 + j * 128 + 128]
                            else:
                                kl = kb[:, lp * 256 + i * 128: lp * 256 + i * 128 + 128]
                            nsc = w.op("t", lambda e, kk=kl, gg=g, pp=par, ii=i, jj=j: e.matmul(
                                out=sc2[pp][:, ii * 512:ii * 512 + 512],
                                lhsT=kk,
                                rhs=q_v[:, 4 * gg:4 * gg + 4, jj * 128:jj * 128 + 128],
                                start=True, stop=True),
                                "PE" if i == 1 else None)
                        if pend is not None:
                            emit_av(pend)
                        bidx = BOFF[j] + s
                        w.wait("a", "PE", nsc)
                        if ss >= 2:
                            w.wait("a", "PE", nPE_av.get(ss - 2, 0))
                        nA_exp[ss] = w.op("a", lambda e, pp=par, bi=bidx: e.activation(
                            out=expst[:, pp * 1024:pp * 1024 + 1024],
                            in_=sc2[pp][:], func=AF.Exp, scale=epsb[:, 2:3],
                            bias=btab_b[:, bi:bi + 1]), "A")
                        if diag:
                            w.wait("v", "A", nA_exp[ss])
                            nvm = w.op("v", lambda e, pp=par: e.tensor_tensor(
                                out=expst[:, pp * 1024:pp * 1024 + 1024],
                                in0=expst[:, pp * 1024:pp * 1024 + 1024],
                                in1=caus_s[:], op=ALU.mult), "V")
                            deps = [("V", nvm), ("V", nvm)]
                        else:
                            deps = [("A", nA_exp[ss]), ("A", nA_exp[ss])]
                        pend = (diag, j, g0, par, lp, deps,
                                s == 0, s == L[j] - 1, ss)
                        ss += 1
                    emit_av(pend)
                    pend = None
                    w.wait("v", "PE", nPE_av[ss - 1])
                    w.op("v", lambda e: e.reciprocal(out=rcp[:], in_=sums2[:]))
                    vdrain()
                    nV_norm = w.op("v", lambda e, pp=p, jj=j: e.tensor_tensor(
                        out=attn_v[:, 8 * pp:8 * pp + 8, jj * 128:jj * 128 + 128],
                        in0=acc2[:], in1=rcp[:], op=ALU.mult), "V")

            w.fence()

            if SKIP456:
                for j in range(JT):
                    nv0 = w.op("v", lambda e: e.memset(pool[0][:], 0.0), "V")
                    w.wait("q", "V", nv0)
                    w.op("q", lambda e, jj=j: e.dma_start(
                        out=out[jj * 128:jj * 128 + 128, :], in_=pool[0][:]), "Q", 16)
                w.wait("q", "Q", w.n["Q"])
                w.wait("q", "K", w.n["K"])
                w.wait("g", "G", w.n["G"])
                w.wait("g", "S", w.n["S"])
                w.wait("g", "O0", w.n["O0"])
            w.wait("g", "O1", w.n["O1"])
            _skip456 = SKIP456
            nA_stg = {0: None, 1: None}
            if _skip456:
                JT_RANGE = []
            else:
                JT_RANGE = list(range(JT))
            nQ_stg = {0: 0, 1: 0}
            npe_ob = None
            for ob in (range(4) if not _skip456 else []):
                if npe_ob is not None:
                    w.wait("g", "PE", npe_ob)
                ncol = w.op("g", lambda e, o=ob: e.dma_start(
                    out=wot_col[:],
                    in_=wot.rearrange("(ic p) o -> p ic o", p=128)
                    [:, :, o * 512:o * 512 + 512]), "C", 16)
                for j in range(JT):
                    par = j % 2
                    w.wait("t", "C", ncol)
                    if nA_stg[par] is not None:
                        w.wait("t", "A", nA_stg[par])
                    npe_ob = proj16(
                        acc2[:, par * 512:par * 512 + 512],
                        lambda i, jj=j: attn_v[:, i:i + 1, jj * 128:jj * 128 + 128],
                        lambda i: wot_col[:, i * 512:i * 512 + 512])
                    w.wait("a", "PE", npe_ob)
                    w.wait("a", f"T{par}", nQ_stg[par])
                    na = w.op("a", lambda e, pp=par: e.activation(
                        out=aout_stg[:, pp * 512:pp * 512 + 512],
                        in_=acc2[:, pp * 512:pp * 512 + 512], func=AF.Identity), "A")
                    nA_stg[par] = na
                    w.wait("q", "A", na)
                    nQ_stg[par] = w.op("q", lambda e, jj=j, o=ob, pp=par: e.dma_start(
                        out=ao_d[jj * 128:jj * 128 + 128, o * 512:o * 512 + 512],
                        in_=aout_stg[:, pp * 512:pp * 512 + 512]), f"T{par}", 16)
            prev_v = None
            for j in JT_RANGE:
                if prev_v is not None:
                    w.wait("q", "V", prev_v)
                w.wait("q", "T0", nQ_stg[0])
                w.wait("q", "T1", nQ_stg[1])
                nq1 = w.op("q", lambda e, jj=j: e.dma_start(
                    out=pool[0][:], in_=ao_d[jj * 128:jj * 128 + 128, :]), "Q", 16)
                nq2 = w.op("q", lambda e, jj=j: e.dma_start(
                    out=pool[1][:], in_=xq[jj * 128:jj * 128 + 128, :]), "Q", 16)
                ln_stats(pool[0][:], epsb[:, 0:1], nq2)
                v_ts(sq_s[:], pool[0][:], stx(I_NM), stx(I_RI), ALU.add, ALU.mult)
                w.wait("v", "Q", nq2)
                nv = v_tt(pool[0][:], sq_s[:], pool[1][:], ALU.add, "V")
                w.wait("q", "V", nv)
                w.op("q", lambda e, jj=j: e.dma_start(
                    out=h1_d[jj * 128:jj * 128 + 128, :], in_=pool[0][:]), "Q", 16)
                ln_stats(pool[0][:], epsb[:, 0:1], nv, rsem="V")
                nv_f = v_ts(pool[1][:], pool[0][:], stx(I_NM), stx(I_RI),
                            ALU.add, ALU.mult, sem="V")
                w.wait("a", "V", nv_f)
                na = w.op("a", lambda e: e.activation(
                    out=sq_s[:], in_=pool[1][:], func=AF.Square,
                    accum_out=stx(I_T1)), "A")
                w.wait("v", "A", na)
                v_ts(stx(I_T2), stx(I_T1), 1.0 / 2048.0)
                vdrain()
                nv2 = w.op("v", lambda e: e.tensor_copy(out=stx(I_T3), in_=stx(I_T2)), "V")
                vdrain()
                w.wait("a", "V", nv2)
                na2 = w.op("a", lambda e: e.activation(
                    out=stx(I_SD), in_=stx(I_T3), func=AF.Sqrt,
                    bias=epsb[:, 1:2]), "A")
                w.wait("v", "A", na2)
                w.op("v", lambda e: e.reciprocal(out=stx(I_RI2), in_=stx(I_SD)))
                vdrain()
                v_stt(pool[1][:], pool[1][:], stx(I_RI2), gam_b[:], ALU.mult, ALU.mult)
                nv3 = w.op("v", lambda e: e.tensor_reduce(
                    out=stx(I_T1), in_=pool[1][:], axis=AX.X, op=ALU.max,
                    apply_absolute_value=True), "V")
                vdrain()
                if j == 0:
                    nv4 = w.op("v", lambda e: e.tensor_copy(out=stx(I_RMX), in_=stx(I_T1)), "V")
                else:
                    nv4 = v_tt(stx(I_RMX), stx(I_RMX), stx(I_T1), ALU.max, "V")
                vdrain()
                prev_v = nv4
                w.wait("q", "V", nv4)
                w.op("q", lambda e, jj=j: e.dma_start(
                    out=xn_d[jj * 128:jj * 128 + 128, :], in_=pool[1][:]), "Q", 16)

            if not _skip456:
                w.wait("q", "V", prev_v)
                nqm = w.op("q", lambda e: e.dma_start(
                    out=wrow[:, :], in_=st8[:, I_RMX:I_RMX + 1]), "Q", 16)
                w.wait("v", "Q", nqm)
                nvm = w.op("v", lambda e: e.tensor_reduce(
                    out=one1[0:1, 4:5], in_=wrow[:, :], axis=AX.X, op=ALU.max), "V")
                vdrain()
                w.wait("q", "V", nvm)
                nqm2 = w.op("q", lambda e: e.dma_start(
                    out=mx_in[0:1, 0:1], in_=one1[0:1, 4:5]), "Q", 16)
                w.wait("g", "Q", nqm2)
                nP_ar = w.op("g", lambda e: e.collective_compute(
                    "AllReduce", ALU.max, replica_groups=[list(range(NC))],
                    ins=[mx_in], outs=[mx_out]), "P")
                w.wait("q", "P", nP_ar)
                nqm3 = w.op("q", lambda e: e.dma_start(
                    out=scal[:, 5:6],
                    in_=mx_out[0:1, 0:1].partition_broadcast(128)), "Q", 16)
                w.wait("v", "Q", nqm3)
                w.op("v", lambda e: e.tensor_scalar(
                    out=scal[:, 6:7], in0=scal[:, 5:6], scalar1=1e-5,
                    scalar2=None, op0=ALU.max))
                vdrain()
                w.op("v", lambda e: e.reciprocal(out=scal[:, 7:8], in_=scal[:, 6:7]))
                vdrain()
                w.op("v", lambda e: e.tensor_scalar(
                    out=scal[:, 2:3], in0=scal[:, 7:8], scalar1=127.0,
                    scalar2=None, op0=ALU.mult))
                w.op("v", lambda e: e.tensor_scalar(
                    out=scal[:, 8:9], in0=scal[:, 6:7], scalar1=INV127,
                    scalar2=None, op0=ALU.mult))
                vdrain()
                nq_sc = w.op("v", lambda e: e.tensor_tensor(
                    out=scal[:, 3:4], in0=scal[:, 8:9], in1=scal[:, 0:1],
                    op=ALU.mult), "V")
                vdrain()

                prev_v = None
                prev_evac = None
                for j in range(JT):
                    if prev_v is not None:
                        w.wait("q", "V", prev_v)
                    nq1 = w.op("q", lambda e, jj=j: e.dma_start(
                        out=pool[0][:], in_=xn_d[jj * 128:jj * 128 + 128, :]), "Q", 16)
                    w.wait("v", "Q", nq1)
                    w.wait("v", "V", nq_sc) if False else None
                    v_ts(pool[1][:], pool[0][:], scal[:, 2:3])
                    v_ts(pool[1][:], pool[1][:], -128.0, 127.0, ALU.max, ALU.min)
                    w.op("v", lambda e: e.tensor_copy(out=ki_b[:], in_=pool[1][:]))
                    w.op("v", lambda e: e.tensor_copy(out=sq_s[:], in_=ki_b[:]))
                    v_tt(pool[1][:], pool[1][:], sq_s[:], ALU.subtract)
                    v_ts(pool[0][:], pool[1][:], 0.5, op0=ALU.is_ge)
                    v_ts(pool[1][:], pool[1][:], -0.5, op0=ALU.is_lt)
                    v_tt(pool[0][:], pool[0][:], pool[1][:], ALU.subtract)
                    nv = v_tt(pool[1][:], sq_s[:], pool[0][:], ALU.add, "V")
                    w.wait("t", "V", nv)
                    if prev_evac is not None:
                        w.wait("t", "V", prev_evac)
                    npe_h = {}
                    for half in range(2):
                        for ic in range(8):
                            icg = half * 8 + ic
                            npe_h[half] = w.op("t", lambda e, i=icg, hh=half, ii=ic: e.transpose(
                                out=sc2[hh][:, ii * 128:ii * 128 + 128],
                                in_=pool[1][:, i * 128:i * 128 + 128],
                                identity=id_fs[:]), "PE" if ic == 7 else None)
                    nevs = []
                    w.wait("v", "G", ng_wfq)
                    for half in range(2):
                        w.wait("v", "PE", npe_h[half])
                        nevs.append(w.op("v", lambda e, jj=j, hh=half: e.tensor_copy(
                            out=xqt[:, jj * 2048 + hh * 1024: jj * 2048 + hh * 1024 + 1024],
                            in_=sc2[hh][:]), "V"))
                    prev_evac = nevs[-1]
                    prev_v = nevs[-1]

                nA_stg = {0: None, 1: None}
                nQ_stg = {0: w.n["T0"], 1: w.n["T1"]}
                npe_ob = None
                for ob in range(4):
                    if npe_ob is not None:
                        w.wait("q", "PE", npe_ob)
                    w.wait("q", "G", ng_wfq)
                    ncol = w.op("q", lambda e, o=ob: e.dma_start(
                        out=wfq_col[:],
                        in_=wfq_d.rearrange("ic p o -> p ic o")
                        [:, :, o * 512:o * 512 + 512]), "C", 16)
                    for j in range(JT):
                        par = j % 2
                        w.wait("t", "C", ncol)
                        w.wait("t", "V", prev_evac)
                        if nA_stg[par] is not None:
                            w.wait("t", "A", nA_stg[par])
                        npe_ob = proj16(
                            acc2[:, par * 512:par * 512 + 512],
                            lambda i, jj=j: xqt[:, jj * 2048 + i * 128:
                                               jj * 2048 + i * 128 + 128],
                            lambda i: wfq_col[:, i * 512:i * 512 + 512])
                        w.wait("a", "PE", npe_ob)
                        w.wait("a", "Q", nQ_stg[par])
                        w.wait("a", "V", nq_sc)
                        na = w.op("a", lambda e, pp=par: e.activation(
                            out=aout_stg[:, pp * 512:pp * 512 + 512],
                            in_=acc2[:, pp * 512:pp * 512 + 512], func=AF.Identity,
                            scale=scal[:, 3:4]), "A")
                        nA_stg[par] = na
                        w.wait("q", "A", na)
                        nQ_stg[par] = w.op("q", lambda e, jj=j, o=ob, pp=par: e.dma_start(
                            out=fo_d[jj * 128:jj * 128 + 128, o * 512:o * 512 + 512],
                            in_=aout_stg[:, pp * 512:pp * 512 + 512]), "Q", 16)
                prev_v = None
                for j in range(JT):
                    if prev_v is not None:
                        w.wait("q", "V", prev_v)
                    w.wait("q", "Q", nQ_stg[0])
                    w.wait("q", "Q", nQ_stg[1])
                    nq1 = w.op("q", lambda e, jj=j: e.dma_start(
                        out=pool[0][:], in_=fo_d[jj * 128:jj * 128 + 128, :]), "Q", 16)
                    nq2 = w.op("q", lambda e, jj=j: e.dma_start(
                        out=pool[1][:], in_=h1_d[jj * 128:jj * 128 + 128, :]), "Q", 16)
                    ln_stats(pool[0][:], epsb[:, 0:1], nq2)
                    v_ts(sq_s[:], pool[0][:], stx(I_NM), stx(I_RI), ALU.add, ALU.mult)
                    w.wait("v", "Q", nq2)
                    nv = v_tt(pool[0][:], sq_s[:], pool[1][:], ALU.add, "V")
                    prev_v = nv
                    w.wait("q", "V", nv)
                    w.op("q", lambda e, jj=j: e.dma_start(
                        out=out[jj * 128:jj * 128 + 128, :], in_=pool[0][:]), "Q", 16)

                if debug:
                    w.wait("g", "V", w.n["V"])
                    w.wait("g", "A", w.n["A"])
                    w.wait("g", "Q", w.n["Q"])
                    w.op("g", lambda e: e.dma_start(out=dbg_attn[:, :], in_=attn[:]), "G", 16)
                    w.op("g", lambda e: e.dma_start(out=dbg_scal[:, :], in_=scal[:]), "G", 16)
                    w.op("g", lambda e: e.dma_start(out=dbg_one[:, :], in_=one1[:]), "G", 16)
                    w.op("g", lambda e: e.dma_start(out=dbg_xqt[:, :], in_=xqt[:]), "G", 16)
                    w.op("g", lambda e: e.dma_start(out=dbg_wfq[:, :, :], in_=wfq_d[:, :, :]), "G", 16)
            w.wait("q", "Q", w.n["Q"])
            for s_ in ("K0", "K1", "K2", "K3", "K4", "K5", "K6", "K7",
                       "T0", "T1"):
                w.wait("q", s_, w.n[s_])
            w.wait("q", "C", w.n["C"])
            w.wait("g", "G", w.n["G"])
            for s_ in ("C0", "C1", "S0", "S1"):
                w.wait("g", s_, w.n[s_])
            w.wait("g", "O0", w.n["O0"])
            w.wait("g", "O1", w.n["O1"])

            with nc.Block() as BB:
                @BB.sync
                def _(e):
                    for f in w.cmds[1]["q"]:
                        f(e)

                @BB.gpsimd
                def _(e):
                    for f in w.cmds[1]["g"]:
                        f(e)

                @BB.vector
                def _(e):
                    for f in w.cmds[1]["v"]:
                        f(e)

                @BB.scalar
                def _(e):
                    for f in w.cmds[1]["a"]:
                        f(e)

                @BB.tensor
                def _(e):
                    for f in w.cmds[1]["t"]:
                        f(e)
    return nc


def kernel(L0_attn_x, L0_attn_Wq, L0_attn_Wk, L0_attn_Wv, L0_attn_Wo,
           L0_ffn_weight, L0_ffn_norm_weight):
    x = np.ascontiguousarray(np.asarray(L0_attn_x, dtype=np.float32).reshape(S, D))
    Wq = np.asarray(L0_attn_Wq, dtype=np.float32)
    Wk = np.asarray(L0_attn_Wk, dtype=np.float32)
    Wv = np.asarray(L0_attn_Wv, dtype=np.float32)
    Wo = np.asarray(L0_attn_Wo, dtype=np.float32)
    Wf = np.asarray(L0_ffn_weight, dtype=np.float32)
    gam = np.ascontiguousarray(np.asarray(L0_ffn_norm_weight, dtype=np.float32))

    perm = np.concatenate([np.arange(0, 128, 2), np.arange(1, 128, 2)])
    Wq_p = Wq.reshape(16, 128, D)[:, perm, :].reshape(D, D)
    Wk_p = Wk.reshape(4, 128, D)[:, perm, :].reshape(512, D)
    wqt = np.ascontiguousarray(Wq_p.T)
    wkt = np.ascontiguousarray(Wk_p.T)
    wvt = np.ascontiguousarray(Wv.T)
    wot = np.ascontiguousarray(Wo.T)
    wft = np.ascontiguousarray(Wf.T)

    nc = build(debug=False)
    in_maps = []
    for c in range(NC):
        tiles = [8 * j + c for j in range(JT)]
        xq_c = np.ascontiguousarray(x.reshape(64, 128, D)[tiles].reshape(TPC, D))
        xs_c = np.ascontiguousarray(x[c * TPC:(c + 1) * TPC])
        tvals = np.zeros((1, 2048), dtype=np.float32)
        tvals[0, :1024] = np.arange(c * TPC, (c + 1) * TPC, dtype=np.float32)
        own_tok = np.concatenate([np.arange(t * 128, t * 128 + 128) for t in tiles])
        tvals[0, 1024:] = own_tok.astype(np.float32)
        bt = np.zeros((1, NSLOT), dtype=np.float32)
        for j in range(JT):
            t_own = 8 * j + c
            for s_ in range(1, L[j]):
                if s_ - 1 >= t_own:
                    bt[0, BOFF[j] + s_] = BKILL
        in_maps.append({
            "xq": xq_c, "xs": xs_c, "wqt": wqt, "wkt": wkt, "wvt": wvt,
            "wot": wot, "wft": wft, "gam": gam, "tidx": tvals, "btab": bt,
        })
    res = run_bass_kernel_spmd(nc, in_maps, core_ids=list(range(NC)))
    full = np.empty((S, D), dtype=np.float32)
    for c in range(NC):
        o = res.results[c]["out"]
        for j in range(JT):
            t = 8 * j + c
            full[t * 128:(t + 1) * 128] = o[j * 128:(j + 1) * 128]
    return full.reshape(1, S, D)
